# version 10
# baseline (speedup 1.0000x reference)
import numpy as np
from collections import defaultdict

import concourse.bass as bass
import concourse.mybir as mybir
from concourse.bass_utils import run_bass_kernel_spmd

F32 = mybir.dt.float32
BF16 = mybir.dt.bfloat16
AF = mybir.ActivationFunctionType
ALU = mybir.AluOpType
AX = mybir.AxisListType

L = 4096
DM = 1024
DI = 2048
NS = 16
Q = 128
NCHUNK = L // Q
NT = 16
TW = 512
NTW = L // TW
NHG = L // 256
PHASES = "all"
P0_LIMIT = NCHUNK
P0_SKIP = ""
SKIP_SCATTER = False
DBG_SCRATCH = False


class Builder:
    ENGS = ("sy", "pe", "act", "dve", "pool")

    def __init__(self):
        self.q = {e: [] for e in self.ENGS}
        self.cnt = defaultdict(int)
        self.sem = None

    def emit(self, eng, fn, waits=(), incs=()):
        waits = [(s, v) for (s, v) in waits if v > 0]
        for (s, v) in waits:
            assert v <= self.cnt[s], f"wait on future count {s} {v} > {self.cnt[s]}"
        incs = [(s, n) for (s, n) in incs]
        for (s, n) in incs:
            self.cnt[s] += n

        def cl(ec, _fn=fn, _w=waits, _i=incs):
            for (s, v) in _w:
                ec.wait_ge(self.sem[s], v)
            inst = _fn(ec)
            for (s, n) in _i:
                inst.then_inc(self.sem[s], n)
        self.q[eng].append(cl)
        return {s: self.cnt[s] for s, _ in incs}

    def c(self, s):
        return self.cnt[s]

    def wait(self, eng, waits):
        def cl(ec, _w=list(waits)):
            for (s, v) in _w:
                ec.wait_ge(self.sem[s], v)
        self.q[eng].append(cl)

    def barrier(self):
        targets = {}
        for e in self.ENGS:
            self.cnt["b" + e] += 1
            targets[e] = self.cnt["b" + e]
        for e in self.ENGS:
            def cl(ec, _e=e, _t=dict(targets)):
                ec.sem_inc(self.sem["b" + _e], 1)
                for oe, v in _t.items():
                    ec.wait_ge(self.sem["b" + oe], v)
            self.q[e].append(cl)


def build_program():
    nc = bass.Bass("TRN2", target_bir_lowering=False, debug=False)
    b = Builder()
    E, C = b.emit, b.c

    x_h = nc.dram_tensor("x", [L, DM], F32, kind="ExternalInput").ap()
    winT_h = nc.dram_tensor("winT", [DM, 2 * DI], F32, kind="ExternalInput").ap()
    wxdtT_h = nc.dram_tensor("wxdtT", [DI, 48], F32, kind="ExternalInput").ap()
    woutT_h = nc.dram_tensor("woutT", [DI, DM], F32, kind="ExternalInput").ap()
    convw_h = nc.dram_tensor("convw", [128, NT * 4], F32, kind="ExternalInput").ap()
    convb_h = nc.dram_tensor("convb", [128, NT], F32, kind="ExternalInput").ap()
    dfm_h = nc.dram_tensor("dfm", [128, NT], F32, kind="ExternalInput").ap()
    bdt_h = nc.dram_tensor("bdt", [1, NS], F32, kind="ExternalInput").ap()
    alog_h = nc.dram_tensor("alog", [1, NS], F32, kind="ExternalInput").ap()
    gwT_h = nc.dram_tensor("gwT", [DM, 8], F32, kind="ExternalInput").ap()
    wgT_h = nc.dram_tensor("wgT", [8, DM, 512], F32, kind="ExternalInput").ap()
    wuT_h = nc.dram_tensor("wuT", [8, DM, 512], F32, kind="ExternalInput").ap()
    wdT_h = nc.dram_tensor("wdT", [8, 512, DM], F32, kind="ExternalInput").ap()
    out_h = nc.dram_tensor("out", [L, DM], F32, kind="ExternalOutput").ap()

    KSC = "ExternalOutput" if DBG_SCRATCH else "Internal"
    ufmH = nc.dram_tensor("ufmH", [NCHUNK, 128, NT, Q], BF16, kind=KSC).ap()
    utokH = nc.dram_tensor("utokH", [NCHUNK, 128, DI], BF16, kind=KSC).ap()
    bcH = nc.dram_tensor("bcH", [48, L], F32, kind=KSC).ap()
    yzH = nc.dram_tensor("yzH", [NCHUNK, 128, DI], BF16, kind=KSC).ap()
    gH = nc.dram_tensor("gH", [NCHUNK, 128, DM], BF16, kind=KSC).ap()
    xnH = nc.dram_tensor("xnH", [NCHUNK, 128, DM], BF16, kind="Internal").ap()
    xsH = nc.dram_tensor("xsH", [8 * 1024, DM], BF16, kind="Internal").ap()
    gsH = nc.dram_tensor("gsH", [8 * 1024, DM], BF16, kind="Internal").ap()
    xsD = nc.dram_tensor("xsD", [8 * 1024, DM], BF16, kind=KSC).ap() if DBG_SCRATCH else None
    gsD = nc.dram_tensor("gsD", [8 * 1024, DM], BF16, kind=KSC).ap() if DBG_SCRATCH else None
    bceH = nc.dram_tensor("bceH", [NCHUNK, 8], F32, kind="Internal").ap()
    slotH = nc.dram_tensor("slotH", [128, NCHUNK], mybir.dt.int32, kind=KSC).ap()
    rtD = nc.dram_tensor("rtD", [128, 640], F32, kind=KSC).ap()
    partH = nc.dram_tensor("partH", [NCHUNK, 128, DM], F32, kind="Internal").ap()
    wzHd = nc.dram_tensor("wzHd", [DM, DI], BF16, kind="Internal").ap()
    fcumH = nc.dram_tensor("fcumH", [NS, Q], F32, kind="Internal").ap()
    fcH = nc.dram_tensor("fcH", [NS, Q], BF16, kind="Internal").ap()

    def sb(name, shape, dtype):
        return nc.alloc_sbuf_tensor(name, shape, dtype)

    xnT = sb("xnT", [128, 8 * L], BF16)
    hst = sb("hst", [NS, DI], F32)
    triu = sb("triu", [128, 128], F32)
    mneg = sb("mneg", [128, 128], F32)
    identb = sb("identb", [128, 128], BF16)
    identf = sb("identf", [128, 128], F32)
    onescol = sb("onescol", [128, 1], F32)
    ones1f = sb("ones1f", [1, 128], F32)
    ones1b = sb("ones1b", [1, 128], BF16)
    convw = sb("convw_s", [128, NT * 4], F32)
    convb = sb("convb_s", [128, NT], F32)
    dfm = sb("dfm_s", [128, NT], F32)
    bdt_r = sb("bdt_r", [1, NS], F32)
    alog_r = sb("alog_r", [1, NS], F32)
    arow = sb("arow", [1, NS], F32)
    bdtB = sb("bdtB", [128, NS], F32)
    aB = sb("aB", [128, NS], F32)
    wxdtT = sb("wxdtT_s", [128, NT * 48], BF16)
    gwT = sb("gwT_s", [128, 8 * 8], BF16)
    rwall = sb("rwall", [128, NCHUNK * 8], F32)
    slotI = sb("slotI", [128, NCHUNK], mybir.dt.int32)
    tris = sb("tris", [128, 128], F32)
    ecapB = sb("ecapB", [128, 8], F32)
    limB = sb("limB", [128, 8], F32)
    stage = sb("stage", [128, 2048], F32)
    WA = sb("WA", [128, 16 * 1024], BF16)
    WB = sb("WB", [128, 16 * 1024], BF16)

    SCR = 52 * 1024
    scr = sb("scr", [128, SCR], mybir.dt.uint8)
    scr_addr = nc.lookup_mloc(scr).addr

    def at(name, off, shape, dtype):
        assert off % 64 == 0
        nbytes = int(np.prod(shape[1:])) * mybir.dt.size(dtype)
        assert off + nbytes <= SCR, f"{name} overflows scratch: {off}+{nbytes}"
        return nc.alloc_sbuf_tensor_at(name, shape, dtype, offset=scr_addr + off)

    xt0 = at("xt0", 0, [128, 2 * DM], F32)
    xnb = at("xnb", 8192, [128, DM], BF16)
    st1 = at("st1", 10240, [128, 1], F32)
    st2 = at("st2", 10304, [128, 1], F32)
    st3 = at("st3", 10368, [128, 1], F32)
    tileb = at("tileb", 0, [128, 2 * (TW + 3)], F32)
    convacc = at("convacc", 4224, [128, 2 * TW], F32)
    tails = at("tails", 8320, [128, NT * 3], F32)
    bcev = at("bcev", 8576, [48, TW], F32)
    ufm16 = at("ufm16", 10624, [128, NT * TW], BF16)
    utokb = at("utokb", 27008, [128, 2 * DI], BF16)
    convacc2 = at("convacc2", 35200, [128, 2 * TW], F32)
    tbuf2 = at("tbuf2", 39296, [128, TW], F32)
    szf = at("szf", 0, [128, 2 * NT * 256], BF16)
    ut4 = at("ut4", 16384, [128, DI], BF16)
    uf4 = at("uf4", 20480, [128, DI], BF16)
    mb1 = at("mb1", 24576, [128, NS * Q], F32)
    mb2 = at("mb2", 32768, [128, NS * Q], BF16)
    yzb = at("yzb", 36864, [128, DI], BF16)
    flt = at("flt", 40960, [1, NS * Q], F32)
    fltc = at("fltc", 49152, [1, NS * Q], BF16)
    sm_cum = sb("sm_cum", [128, NS], F32)
    sm_mbias = sb("sm_mbias", [128, 2 * NS], F32)
    sm_dB = sb("sm_dB", [128, NS], F32)
    sm_lnb = sb("sm_lnb", [128, NS], F32)
    sm_sgn = sb("sm_sgn", [128, 2 * NS], F32)
    sm_dlt = sb("sm_dlt", [128, NS], F32)
    sm_a = sb("sm_a", [128, NS], F32)
    sm_b = sb("sm_b", [128, NS], F32)
    sm_cbf = sb("sm_cbf", [128, NS], BF16)
    sm_ctl = sb("sm_ctl", [128, NS], F32)
    cumT_e = sb("cumT_e", [NS, Q], F32)
    cb16 = sb("cb16", [NS, Q], BF16)
    ctT = sb("ctT", [NS, 2 * Q], F32)
    btok = sb("btok", [128, 2 * NS], BF16)
    dec_c = sb("dec_c", [NS, 2], F32)
    clF = sb("clF", [1, NS], F32)
    msb = sb("msb", [128, Q], F32)
    mbf = sb("mbf", [128, Q], BF16)
    bc4 = sb("bc4", [48, 2 * Q], F32)
    sgb = at("sgb", 0, [128, 512], F32)
    upb = at("upb", 2048, [128, 512], F32)
    pbf_ = at("pbf_", 4096, [128, 512], BF16)
    pTb = at("pTb", 5120, [128, 512], BF16)
    ffnacc = at("ffnacc", 6144, [128, DM], F32)
    rmaxb = at("rmaxb", 22528, [128, 1], F32)
    rwb = at("rwb", 22592, [128, 1], F32)
    oh8 = at("oh8", 22656, [128, 8], F32)
    cum8 = at("cum8", 22720, [128, 8], F32)
    z8 = at("z8", 22784, [128, 8], F32)
    yzin = at("yzin", 24576, [128, 2 * DI], BF16)
    outb = at("outb", 32768, [128, 2 * DM], F32)
    xt5 = at("xt5", 40960, [128, 2 * DM], F32)
    g5 = at("g5", 49152, [128, 2 * DM], BF16)

    psA = nc.alloc_psum_tensor("psA", [128, 1024], F32)
    psB = nc.alloc_psum_tensor("psB", [128, 2048], F32)
    psC = nc.alloc_psum_tensor("psC", [128, 512], F32)
    psT = nc.alloc_psum_tensor("psT", [128, 1024], BF16)

    for sn in (["qSY", "qSC", "qPL", "sPE", "sACT", "sDVE", "sPOOL"] +
               ["b" + e for e in Builder.ENGS]):
        pass
    b.sem = {sn: nc.alloc_semaphore(sn) for sn in
             ["qSY", "qSC", "qPL", "sPE", "sACT", "sDVE", "sPOOL"] +
             ["b" + e for e in Builder.ENGS]}

    def finalize():
        b.wait("sy", [("qPL", C("qPL")), ("qSY", C("qSY")), ("qSC", C("qSC"))])
        b.barrier()
        with nc.Block() as block:
            @block.sync
            def _(eng):
                for cl in b.q["sy"]:
                    cl(eng)

            @block.tensor
            def _(eng):
                for cl in b.q["pe"]:
                    cl(eng)

            @block.scalar
            def _(eng):
                for cl in b.q["act"]:
                    cl(eng)

            @block.vector
            def _(eng):
                for cl in b.q["dve"]:
                    cl(eng)

            @block.gpsimd
            def _(eng):
                for cl in b.q["pool"]:
                    cl(eng)
        return nc

    for ap_sb, ap_h in [(convw, convw_h), (convb, convb_h), (dfm, dfm_h),
                        (bdt_r, bdt_h), (alog_r, alog_h)]:
        E("sy", lambda e, a=ap_sb, h=ap_h: e.dma_start(out=a[:], in_=h),
          incs=[("qSY", 16)])
    small_in = C("qSY")

    E("pool", lambda e: e.memset(triu[:], 1.0))
    E("pool", lambda e: e.affine_select(out=triu[:], in_=triu[:], compare_op=ALU.is_ge,
                                        fill=0.0, base=0, pattern=[[1, 128]],
                                        channel_multiplier=-1))
    E("pool", lambda e: e.memset(mneg[:], 0.0))
    E("pool", lambda e: e.affine_select(out=mneg[:], in_=mneg[:], compare_op=ALU.is_ge,
                                        fill=-1e9, base=0, pattern=[[1, 128]],
                                        channel_multiplier=-1))
    E("pool", lambda e: e.memset(identb[:], 0.0))
    E("pool", lambda e: e.affine_select(out=identb[:], in_=identb[:],
                                        compare_op=ALU.not_equal, fill=1.0, base=0,
                                        pattern=[[-1, 128]], channel_multiplier=1))
    E("pool", lambda e: e.memset(identf[:], 0.0))
    E("pool", lambda e: e.affine_select(out=identf[:], in_=identf[:],
                                        compare_op=ALU.not_equal, fill=1.0, base=0,
                                        pattern=[[-1, 128]], channel_multiplier=1))
    E("pool", lambda e: e.memset(onescol[:], 1.0))
    E("pool", lambda e: e.memset(ones1f[:], 1.0))
    E("pool", lambda e: e.memset(ones1b[:], 1.0))
    E("pool", lambda e: e.memset(tris[:], 1.0))
    E("pool", lambda e: e.affine_select(out=tris[:], in_=tris[:], compare_op=ALU.is_gt,
                                        fill=0.0, base=0, pattern=[[1, 128]],
                                        channel_multiplier=-1))
    E("pool", lambda e: e.iota(out=ecapB[:], pattern=[[1024, 8]], base=0,
                               channel_multiplier=0,
                               allow_small_or_imprecise_dtypes=True))
    E("pool", lambda e: e.iota(out=limB[:], pattern=[[1024, 8]], base=1023,
                               channel_multiplier=0,
                               allow_small_or_imprecise_dtypes=True))
    E("pool", lambda e: e.memset(hst[:], 0.0), incs=[("sPOOL", 1)])
    consts_done = C("sPOOL")

    E("act", lambda e: e.activation(out=arow[:], in_=alog_r[:], func=AF.Exp),
      waits=[("qSY", small_in)], incs=[("sACT", 1)])
    E("dve", lambda e: e.tensor_scalar_mul(out=arow[:], in0=arow[:], scalar1=-1.0),
      waits=[("sACT", C("sACT"))], incs=[("sDVE", 1)])
    arow_done = C("sDVE")
    E("pe", lambda e: e.matmul(out=psC[:, 0:NS], lhsT=ones1f[:], rhs=bdt_r[:],
                               start=True, stop=True),
      waits=[("qSY", small_in), ("sPOOL", consts_done)], incs=[("sPE", 1)])
    E("pe", lambda e: e.matmul(out=psC[:, NS:2 * NS], lhsT=ones1f[:], rhs=arow[:],
                               start=True, stop=True),
      waits=[("sDVE", arow_done)], incs=[("sPE", 1)])
    bc_mm = C("sPE")
    E("dve", lambda e: e.tensor_copy(out=bdtB[:], in_=psC[:, 0:NS]),
      waits=[("sPE", bc_mm)], incs=[("sDVE", 1)])
    E("dve", lambda e: e.tensor_copy(out=aB[:], in_=psC[:, NS:2 * NS]),
      incs=[("sDVE", 1)])
    bcasts_done = C("sDVE")

    cast_mark = 0
    for kc in range(NT):
        sl = stage[:, (kc % 2) * 1024:(kc % 2) * 1024 + 48]
        wts = [("sPOOL", cast_mark)] if kc >= 2 else []
        E("sy", lambda e, kc=kc, sl=sl: e.dma_start(
            out=sl, in_=wxdtT_h[kc * 128:(kc + 1) * 128, :]),
          waits=wts, incs=[("qSY", 16)])
        E("pool", lambda e, kc=kc, sl=sl: e.tensor_copy(
            out=wxdtT[:, kc * 48:(kc + 1) * 48], in_=sl),
          waits=[("qSY", C("qSY"))], incs=[("sPOOL", 1)])
        cast_mark = C("sPOOL")
    for kc in range(8):
        sl = stage[:, (kc % 2) * 1024:(kc % 2) * 1024 + 8]
        E("sy", lambda e, kc=kc, sl=sl: e.dma_start(
            out=sl, in_=gwT_h[kc * 128:(kc + 1) * 128, :]),
          waits=[("sPOOL", cast_mark)], incs=[("qSY", 16)])
        E("pool", lambda e, kc=kc, sl=sl: e.tensor_copy(
            out=gwT[:, kc * 8:(kc + 1) * 8], in_=sl),
          waits=[("qSY", C("qSY"))], incs=[("sPOOL", 1)])
        cast_mark = C("sPOOL")

    if PHASES == "consts":
        return finalize()
    psT_evac = [0, 0, 0, 0]
    xn_cons = [0, 0]
    xnb_stored = 0
    for c in range(min(NCHUNK, P0_LIMIT)):
        par = c % 2
        xs = xt0[:, par * DM:(par + 1) * DM]
        wts = [("sDVE", xn_cons[par])] if c >= 2 else []
        E("sy", lambda e, c=c, xs=xs: e.dma_start(out=xs, in_=x_h[c * 128:(c + 1) * 128, :]),
          waits=wts, incs=[("qSY", 16)])
        ldc = C("qSY")
        if "l" in P0_SKIP:
            xnd = 0
        else:
         if True:
            E("dve", lambda e, xs=xs: e.tensor_reduce(out=st1[:], in_=xs, axis=AX.X,
                                                      op=ALU.add),
                  waits=[("qSY", ldc)], incs=[("sDVE", 1)])
            E("dve", lambda e: e.tensor_scalar_mul(out=st1[:], in0=st1[:], scalar1=1.0 / DM),
              waits=[("sDVE", C("sDVE"))], incs=[("sDVE", 1)])
            mdone = C("sDVE")
            E("act", lambda e, xs=xs: e.activation(out=psC[:, 0:512],
                                                   in_=xs[:, 0:512], func=AF.Square,
                                                   accum_out=st2[:]),
              waits=[("qSY", ldc)], incs=[("sACT", 1)])
            E("act", lambda e, xs=xs: e.activation(out=psC[:, 0:512], in_=xs[:, 512:1024],
                                                   func=AF.Square, accum_out=st3[:]),
              waits=[("sACT", C("sACT"))], incs=[("sACT", 1)])
            sqd = C("sACT")
            E("dve", lambda e: e.tensor_add(out=st2[:], in0=st2[:], in1=st3[:]),
              waits=[("sACT", sqd), ("sDVE", mdone)], incs=[("sDVE", 1)])
            E("dve", lambda e: e.tensor_mul(out=st3[:], in0=st1[:], in1=st1[:]),
              waits=[("sDVE", C("sDVE"))], incs=[("sDVE", 1)])
            E("dve", lambda e: e.tensor_scalar(out=st3[:], in0=st3[:], scalar1=-1.0,
                                               scalar2=1e-5, op0=ALU.mult, op1=ALU.add),
              waits=[("sDVE", C("sDVE"))], incs=[("sDVE", 1)])
            vb = C("sDVE")
            E("act", lambda e: e.activation(out=st3[:], in_=st2[:], func=AF.Ln,
                                            bias=st3[:], scale=1.0 / DM),
              waits=[("sDVE", vb)], incs=[("sACT", 1)])
            E("act", lambda e: e.activation(out=st2[:], in_=st3[:], func=AF.Exp, scale=-0.5),
              waits=[("sACT", C("sACT"))], incs=[("sACT", 1)])
            rstd = C("sACT")
            aff = C("sDVE")
            E("dve", lambda e, xs=xs: e.tensor_scalar(out=xnb[:], in0=xs, scalar1=st1[:],
                                                      scalar2=st2[:], op0=ALU.subtract,
                                                      op1=ALU.mult),
              waits=[("sDVE", aff), ("sACT", rstd)], incs=[("sDVE", 1)])
            xnd = C("sDVE")
        if "t" in P0_SKIP:
            xn_cons[par] = xnd if isinstance(xnd, int) else 0
            continue
        for g in range(2):
            for sl4 in range(4):
                kc = g * 4 + sl4
                tdst = psT[:, sl4 * 128:(sl4 + 1) * 128]
                E("pe", lambda e, kc=kc, tdst=tdst: e.transpose(
                    out=tdst, in_=xnb[:, kc * 128:(kc + 1) * 128], identity=identb[:]),
                  waits=[("sDVE", xnd), ("sDVE", psT_evac[0]),
                         ("sPOOL", consts_done)],
                  incs=[("sPE", 1)])
            td = C("sPE")
            for sl4 in range(4):
                kc = g * 4 + sl4
                tdst = psT[:, sl4 * 128:(sl4 + 1) * 128]
                E("dve", lambda e, kc=kc, c=c, tdst=tdst: e.tensor_copy(
                    out=xnT[:, kc * L + c * 128:kc * L + (c + 1) * 128], in_=tdst),
                  waits=[("sPE", td)], incs=[("sDVE", 1)])
            psT_evac[0] = C("sDVE")
        xn_cons[par] = xnd

    if PHASES == "p0":
        return finalize()
    b.barrier()

    cast_mark = 0
    for kc in range(8):
        par = kc % 2
        sl = stage[:, par * 1024:(par + 1) * 1024]
        for hh in range(2):
            wts = [("sPOOL", cast_mark)] if kc >= 1 or hh >= 1 else []
            wts = [("sPOOL", cast_mark)] if (kc, hh) > (0, 0) else []
            E("sy", lambda e, kc=kc, hh=hh, sl=sl: e.dma_start(
                out=sl, in_=winT_h[kc * 128:(kc + 1) * 128, hh * 1024:(hh + 1) * 1024]),
              waits=wts, incs=[("qSY", 16)])
            E("pool", lambda e, kc=kc, hh=hh, sl=sl: e.tensor_copy(
                out=WA[:, kc * DI + hh * 1024:kc * DI + (hh + 1) * 1024], in_=sl),
              waits=[("qSY", C("qSY"))], incs=[("sPOOL", 1)])
            cast_mark = C("sPOOL")
    winxi_done = cast_mark

    E("pool", lambda e: e.memset(tails[:], 0.0), incs=[("sPOOL", 1)])
    tails_zeroed = C("sPOOL")
    utok_stored = [0] * NCHUNK
    ufm_stored = [0] * NTW
    bc_stored = [0] * NTW
    psA_free = [0, 0]
    tile_free = [0, 0]
    ufm_cons_pe = 0
    ufm_cons_pl = 0
    utokb_free = [0, 0]
    for T in range(NTW):
        silu_marks = [0] * NT
        tapend_marks = [0] * NT
        for ft in range(NT):
            par = ft % 2
            tb = tileb[:, par * (TW + 3):(par + 1) * (TW + 3)]
            psl = psA[:, par * 512:(par + 1) * 512]
            for kc in range(8):
                wts = []
                if kc == 0:
                    wts = [("sPOOL", winxi_done), ("sDVE", psA_free[par]),
                           ("bsy", C("bsy"))]
                E("pe", lambda e, T=T, ft=ft, kc=kc, psl=psl: e.matmul(
                    out=psl,
                    lhsT=WA[:, kc * DI + ft * 128:kc * DI + (ft + 1) * 128],
                    rhs=xnT[:, kc * L + T * TW:kc * L + (T + 1) * TW],
                    start=(kc == 0), stop=(kc == 7)),
                  waits=wts, incs=([("sPE", 1)] if kc == 7 else []))
            mmd = C("sPE")
            E("dve", lambda e, ft=ft, tb=tb: e.tensor_copy(
                out=tb[:, 0:3], in_=tails[:, ft * 3:(ft + 1) * 3]),
              waits=[("sACT", tile_free[par]), ("sPOOL", tails_zeroed)],
              incs=[("sDVE", 1)])
            E("dve", lambda e, tb=tb, psl=psl: e.tensor_copy(out=tb[:, 3:3 + TW], in_=psl),
              waits=[("sPE", mmd)], incs=[("sDVE", 1)])
            psA_free[par] = C("sDVE")
            E("dve", lambda e, ft=ft, tb=tb: e.tensor_copy(
                out=tails[:, ft * 3:(ft + 1) * 3], in_=tb[:, TW:TW + 3]),
              waits=[("sDVE", C("sDVE"))], incs=[("sDVE", 1)])
            ca = convacc[:, par * TW:(par + 1) * TW]
            cb2 = convacc2[:, par * TW:(par + 1) * TW]
            E("dve", lambda e, ft=ft, tb=tb, ca=ca: e.tensor_scalar_mul(
                out=ca, in0=tb[:, 0:TW], scalar1=convw[:, ft * 4:ft * 4 + 1]),
              waits=[("sDVE", C("sDVE")), ("sACT", tile_free[par])], incs=[("sDVE", 1)])
            E("dve", lambda e, ft=ft, tb=tb, ca=ca: e.scalar_tensor_tensor(
                out=ca, in0=tb[:, 1:1 + TW], scalar=convw[:, ft * 4 + 1:ft * 4 + 2],
                in1=ca, op0=ALU.mult, op1=ALU.add),
              waits=[("sDVE", C("sDVE"))], incs=[("sDVE", 1)])
            tapa = C("sDVE")
            E("pool", lambda e, ft=ft, tb=tb, cb2=cb2: e.tensor_single_scalar(
                out=cb2, in_=tb[:, 2:2 + TW], scalar=convw[:, ft * 4 + 2:ft * 4 + 3],
                op=ALU.mult),
              waits=[("sDVE", C("sDVE") - 2), ("sACT", tile_free[par]),
                     ("sPOOL", C("sPOOL"))], incs=[("sPOOL", 1)])
            E("pool", lambda e, ft=ft, tb=tb, cb2=cb2: e.scalar_tensor_tensor(
                out=cb2, in0=tb[:, 3:3 + TW], scalar=convw[:, ft * 4 + 3:ft * 4 + 4],
                in1=cb2, op0=ALU.mult, op1=ALU.add)
              if False else e.tensor_single_scalar(
                out=tbuf2[:, 0:TW], in_=tb[:, 3:3 + TW],
                scalar=convw[:, ft * 4 + 3:ft * 4 + 4], op=ALU.mult),
              waits=[("sPOOL", C("sPOOL"))], incs=[("sPOOL", 1)])
            E("pool", lambda e, cb2=cb2: e.tensor_add(out=cb2, in0=cb2,
                                                      in1=tbuf2[:, 0:TW]),
              waits=[("sPOOL", C("sPOOL"))], incs=[("sPOOL", 1)])
            tapp = C("sPOOL")
            E("dve", lambda e, ca=ca, cb2=cb2: e.tensor_add(out=ca, in0=ca, in1=cb2),
              waits=[("sDVE", tapa), ("sPOOL", tapp)], incs=[("sDVE", 1)])
            tapd = C("sDVE")
            tapend_marks[ft] = tapd
            E("act", lambda e, ft=ft, ca=ca: e.activation(
                out=ufm16[:, ft * TW:(ft + 1) * TW], in_=ca, func=AF.Silu,
                bias=convb[:, ft:ft + 1]),
              waits=[("sDVE", tapd), ("sPE", ufm_cons_pe), ("qPL", ufm_cons_pl)],
              incs=[("sACT", 1)])
            silu_marks[ft] = C("sACT")
            tile_free[par] = C("sACT")
        for ft in range(NT):
            E("pe", lambda e, ft=ft: e.matmul(
                out=psC[0:48, 0:512],
                lhsT=wxdtT[:, ft * 48:(ft + 1) * 48],
                rhs=ufm16[:, ft * TW:(ft + 1) * TW],
                start=(ft == 0), stop=(ft == NT - 1)),
              waits=[("sACT", silu_marks[ft])] +
                    ([("sDVE", C("sDVE"))] if ft == 0 else []),
              incs=([("sPE", 1)] if ft == NT - 1 else []))
        bcmm = C("sPE")
        E("dve", lambda e: e.tensor_copy(out=bcev[:], in_=psC[0:48, 0:512]),
          waits=[("sPE", bcmm)], incs=[("sDVE", 1)])
        bev = C("sDVE")
        E("pool", lambda e, T=T: e.dma_start(out=bcH[:, T * TW:(T + 1) * TW], in_=bcev[:]),
          waits=[("sDVE", bev)], incs=[("qPL", 16)])
        bc_stored[T] = C("qPL")
        for ft in range(NT):
            E("pool", lambda e, T=T, ft=ft: e.dma_start(
                out=ufmH[T * 4:(T + 1) * 4, :, ft, :].rearrange("c p t -> p c t"),
                in_=ufm16[:, ft * TW:(ft + 1) * TW].rearrange("p (c t) -> p c t", c=4)),
              waits=[("sACT", silu_marks[ft])], incs=[("qPL", 16)])
        ufm_stored[T] = C("qPL")
        for tc in range(4):
            c = T * 4 + tc
            spar = tc % 2
            ub = utokb[:, spar * DI:(spar + 1) * DI]
            for g in range(4):
                for sl4 in range(4):
                    ft = g * 4 + sl4
                    tdst = psT[:, sl4 * 128:(sl4 + 1) * 128]
                    E("pe", lambda e, ft=ft, tc=tc, tdst=tdst: e.transpose(
                        out=tdst,
                        in_=ufm16[:, ft * TW + tc * 128:ft * TW + (tc + 1) * 128],
                        identity=identb[:]),
                      waits=[("sACT", silu_marks[ft]), ("sDVE", psT_evac[0])],
                      incs=[("sPE", 1)])
                td = C("sPE")
                for sl4 in range(4):
                    ft = g * 4 + sl4
                    tdst = psT[:, sl4 * 128:(sl4 + 1) * 128]
                    E("dve", lambda e, ft=ft, ub=ub, tdst=tdst: e.tensor_copy(
                        out=ub[:, ft * 128:(ft + 1) * 128], in_=tdst),
                      waits=[("sPE", td), ("qPL", utokb_free[spar])],
                      incs=[("sDVE", 1)])
                psT_evac[0] = C("sDVE")
            utd = C("sDVE")
            E("pool", lambda e, c=c, ub=ub: e.dma_start(out=utokH[c], in_=ub),
              waits=[("sDVE", utd)], incs=[("qPL", 16)])
            utok_stored[c] = C("qPL")
            utokb_free[spar] = C("qPL")
        ufm_cons_pe = C("sPE")
        ufm_cons_pl = C("qPL")

    if PHASES == "p2":
        return finalize()
    b.barrier()

    cast_mark = C("sPOOL")
    for kc in range(8):
        par = kc % 2
        sl = stage[:, par * 1024:(par + 1) * 1024]
        for hh in range(2):
            wts = [("sPOOL", cast_mark)] if (kc, hh) > (0, 0) else []
            E("sy", lambda e, kc=kc, hh=hh, sl=sl: e.dma_start(
                out=sl, in_=winT_h[kc * 128:(kc + 1) * 128,
                                   DI + hh * 1024:DI + (hh + 1) * 1024]),
              waits=wts, incs=[("qSY", 16)])
            E("pool", lambda e, kc=kc, hh=hh, sl=sl: e.tensor_copy(
                out=WB[:, kc * DI + hh * 1024:kc * DI + (hh + 1) * 1024], in_=sl),
              waits=[("qSY", C("qSY"))], incs=[("sPOOL", 1)])
            cast_mark = C("sPOOL")
    winz_done = cast_mark

    if PHASES == "pg":
        return finalize()
    b.barrier()

    def emit_zhalf(hg, szpar, after_psB_free_dve, szf_war_pool=0):
        prev_silu = [0, 0]
        for rnd in range(4):
            base = (rnd % 2) * 1024
            for fti in range(4):
                ft = rnd * 4 + fti
                for kc in range(8):
                    wts = []
                    if fti == 0 and kc == 0:
                        wts = [("sPOOL", winz_done), ("sDVE", after_psB_free_dve),
                               ("sACT", prev_silu[rnd % 2])]
                    E("pe", lambda e, hg=hg, ft=ft, fti=fti, kc=kc, base=base: e.matmul(
                        out=psB[:, base + fti * 256:base + (fti + 1) * 256],
                        lhsT=WB[:, kc * DI + ft * 128:kc * DI + (ft + 1) * 128],
                        rhs=xnT[:, kc * L + hg * 256:kc * L + (hg + 1) * 256],
                        start=(kc == 0), stop=(kc == 7)),
                      waits=wts, incs=([("sPE", 1)] if kc == 7 else []))
            zmm = C("sPE")
            for fti in range(4):
                ft = rnd * 4 + fti
                E("act", lambda e, ft=ft, fti=fti, szpar=szpar, base=base: e.activation(
                    out=szf[:, szpar * NT * 256 + ft * 256:
                            szpar * NT * 256 + (ft + 1) * 256],
                    in_=psB[:, base + fti * 256:base + (fti + 1) * 256], func=AF.Silu),
                  waits=[("sPE", zmm), ("sPOOL", szf_war_pool)], incs=[("sACT", 1)])
            prev_silu[rnd % 2] = C("sACT")
        return C("sACT")

    sz_ready = [0] * NHG
    sz_ready[0] = emit_zhalf(0, 0, 0)
    szf_cons = [0, 0]

    state = {
        "utok_cons": 0, "ufm_cons": 0, "bc_cons": [0, 0],
        "flt_cons": 0, "fltc_cons": 0, "yzb_stored": 0,
        "psB_evac_dve": 0, "psB_z_done": C("sACT"),
        "prev_h_done": 0, "prev_X1_done": 0, "psA_free": 0,
    }
    SM = {}

    def emit_stats(c):
        par = c % 2
        E("sy", lambda e, c=c, par=par: e.dma_start(
            out=bc4[:, par * Q:(par + 1) * Q], in_=bcH[:, c * Q:(c + 1) * Q]),
          waits=[("qPL", bc_stored[c // 4]), ("sPE", state["bc_cons"][par])],
          incs=[("qSY", 16)])
        bc_ld = C("qSY")
        prev_psc_free = SM[c - 1]["ctT"] if (c - 1) in SM else 0
        E("pe", lambda e, par=par: e.transpose(
            out=psC[:, 304:352], in_=bc4[:, par * Q:(par + 1) * Q],
            identity=identf[0:48, 0:48]),
          waits=[("qSY", bc_ld), ("sDVE", prev_psc_free)], incs=[("sPE", 1)])
        bct = C("sPE")
        state["bc_cons"][par] = bct
        E("dve", lambda e: e.tensor_add(out=sm_a[:], in0=psC[:, 304 + 32:304 + 48],
                                        in1=bdtB[:]),
          waits=[("sPE", bct), ("sDVE", bcasts_done)], incs=[("sDVE", 1)])
        E("dve", lambda e: e.tensor_copy(out=sm_b[:], in_=psC[:, 304:304 + 16]),
          waits=[("sDVE", C("sDVE"))], incs=[("sDVE", 1)])
        E("dve", lambda e: e.tensor_copy(out=sm_cbf[:], in_=psC[:, 304 + 16:304 + 32]),
          waits=[("sDVE", C("sDVE"))], incs=[("sDVE", 1)])
        smalls0 = C("sDVE")
        E("dve", lambda e: e.tensor_scalar_max(out=sm_dlt[:], in0=sm_a[:], scalar1=0.0),
          waits=[("sDVE", smalls0)], incs=[("sDVE", 1)])
        r0 = C("sDVE")
        E("act", lambda e: e.activation(out=sm_lnb[:], in_=sm_a[:], func=AF.Abs),
          waits=[("sDVE", smalls0)], incs=[("sACT", 1)])
        E("act", lambda e: e.activation(out=sm_lnb[:], in_=sm_lnb[:], func=AF.Exp,
                                        scale=-1.0),
          waits=[("sACT", C("sACT"))], incs=[("sACT", 1)])
        E("act", lambda e: e.activation(out=sm_lnb[:], in_=sm_lnb[:], func=AF.Ln,
                                        bias=1.0),
          waits=[("sACT", C("sACT"))], incs=[("sACT", 1)])
        spl = C("sACT")
        E("dve", lambda e: e.tensor_add(out=sm_dlt[:], in0=sm_dlt[:], in1=sm_lnb[:]),
          waits=[("sACT", spl), ("sDVE", r0)], incs=[("sDVE", 1)])
        dlt_done = C("sDVE")
        E("dve", lambda e: e.tensor_mul(out=sm_a[:], in0=sm_dlt[:], in1=aB[:]),
          waits=[("sDVE", dlt_done)], incs=[("sDVE", 1)])
        dal_done = C("sDVE")
        E("dve", lambda e: e.tensor_mul(out=sm_dB[:], in0=sm_dlt[:], in1=sm_b[:]),
          waits=[("sDVE", dlt_done)], incs=[("sDVE", 1)])
        db_done = C("sDVE")
        E("act", lambda e: e.activation(out=sm_lnb[:], in_=sm_dB[:], func=AF.Abs),
          waits=[("sDVE", db_done)], incs=[("sACT", 1)])
        E("act", lambda e: e.activation(out=sm_lnb[:], in_=sm_lnb[:], func=AF.Ln),
          waits=[("sACT", C("sACT"))], incs=[("sACT", 1)])
        lnb_done = C("sACT")
        E("act", lambda e, par=par: e.activation(
            out=sm_sgn[:, par * NS:(par + 1) * NS], in_=sm_dB[:], func=AF.Sign),
          waits=[("sDVE", db_done)], incs=[("sACT", 1)])
        sgn_done = C("sACT")
        E("pe", lambda e: e.matmul(out=psC[:, 0:16], lhsT=triu[:], rhs=sm_a[:],
                                   start=True, stop=True),
          waits=[("sDVE", dal_done)], incs=[("sPE", 1)])
        E("pe", lambda e: e.matmul(out=psC[0:16, 16:144], lhsT=sm_a[:], rhs=triu[:],
                                   start=True, stop=True),
          waits=[("sPE", C("sPE"))], incs=[("sPE", 1)])
        E("pe", lambda e: e.matmul(out=psC[0:16, 144:145], lhsT=sm_a[:], rhs=onescol[:],
                                   start=True, stop=True),
          waits=[("sPE", C("sPE"))], incs=[("sPE", 1)])
        E("pe", lambda e: e.matmul(out=psC[0:1, 145:161], lhsT=onescol[:], rhs=sm_a[:],
                                   start=True, stop=True),
          waits=[("sPE", C("sPE"))], incs=[("sPE", 1)])
        cum_mm = C("sPE")
        E("dve", lambda e: e.tensor_copy(out=sm_cum[:], in_=psC[:, 0:16]),
          waits=[("sPE", cum_mm)], incs=[("sDVE", 1)])
        E("dve", lambda e: e.tensor_copy(out=cumT_e[:], in_=psC[0:16, 16:144]),
          waits=[("sDVE", C("sDVE"))], incs=[("sDVE", 1)])
        E("dve", lambda e: e.tensor_copy(out=clF[:], in_=psC[0:1, 145:161]),
          waits=[("sDVE", C("sDVE"))], incs=[("sDVE", 1)])
        cum_ev = C("sDVE")
        E("act", lambda e, par=par: e.activation(
            out=dec_c[:, par:par + 1], in_=psC[0:16, 144:145], func=AF.Exp),
          waits=[("sPE", cum_mm)], incs=[("sACT", 1)])
        dec_done = C("sACT")
        E("pe", lambda e: e.matmul(out=psC[:, 161:177], lhsT=ones1f[:], rhs=clF[:],
                                   start=True, stop=True),
          waits=[("sDVE", cum_ev), ("sACT", dec_done)], incs=[("sPE", 1)])
        bclmm = C("sPE")
        E("dve", lambda e: e.tensor_sub(out=sm_b[:], in0=psC[:, 161:177], in1=sm_cum[:]),
          waits=[("sPE", bclmm), ("sDVE", cum_ev)], incs=[("sDVE", 1)])
        bed = C("sDVE")
        E("act", lambda e: e.activation(out=sm_b[:], in_=sm_b[:], func=AF.Exp),
          waits=[("sDVE", bed)], incs=[("sACT", 1)])
        be2 = C("sACT")
        E("dve", lambda e, par=par: e.tensor_mul(
            out=btok[:, par * NS:(par + 1) * NS], in0=sm_dB[:], in1=sm_b[:]),
          waits=[("sACT", be2), ("sDVE", db_done)], incs=[("sDVE", 1)])
        btok_done = C("sDVE")
        E("act", lambda e: e.activation(out=sm_a[:], in_=sm_cum[:], func=AF.Exp),
          waits=[("sDVE", cum_ev)], incs=[("sACT", 1)])
        ecd = C("sACT")
        E("dve", lambda e: e.tensor_mul(out=sm_ctl[:], in0=sm_cbf[:], in1=sm_a[:]),
          waits=[("sACT", ecd)], incs=[("sDVE", 1)])
        ctl = C("sDVE")
        E("pe", lambda e: e.transpose(out=psC[0:16, 177:305], in_=sm_ctl[:],
                                      identity=identf[:]),
          waits=[("sDVE", ctl), ("sDVE", bed), ("sACT", dec_done)],
          incs=[("sPE", 1)])
        ctTm = C("sPE")
        E("dve", lambda e, par=par: e.tensor_copy(
            out=ctT[:, par * Q:(par + 1) * Q], in_=psC[0:16, 177:305]),
          waits=[("sPE", ctTm)], incs=[("sDVE", 1)])
        ctT_done = C("sDVE")
        E("pe", lambda e: e.transpose(out=psT[0:16, 0:128], in_=sm_cbf[:],
                                      identity=identb[:]),
          waits=[("sDVE", smalls0), ("sDVE", psT_evac[0])], incs=[("sPE", 1)])
        cTm = C("sPE")
        E("dve", lambda e: e.tensor_copy(out=cb16[:], in_=psT[0:16, 0:128]),
          waits=[("sPE", cTm)], incs=[("sDVE", 1)])
        psT_evac[0] = C("sDVE")
        cb_done = C("sDVE")
        E("sy", lambda e: e.dma_start(out=fcumH, in_=cumT_e[:]),
          waits=[("sDVE", cum_ev), ("sPE", state["flt_cons"])], incs=[("qSY", 16)])
        E("sy", lambda e: e.dma_start(
            out=flt[:], in_=fcumH.rearrange("n t -> (n t)")[None, :]),
          waits=[("qSY", C("qSY"))], incs=[("qSY", 16)])
        flt_ld = C("qSY")
        E("sy", lambda e: e.dma_start(out=fcH, in_=cb16[:]),
          waits=[("sDVE", cb_done), ("sPE", state["fltc_cons"])], incs=[("qSY", 16)])
        E("sy", lambda e: e.dma_start(
            out=fltc[:], in_=fcH.rearrange("n t -> (n t)")[None, :]),
          waits=[("qSY", C("qSY"))], incs=[("qSY", 16)])
        fltc_ld = C("qSY")
        E("dve", lambda e, par=par: e.tensor_sub(
            out=sm_mbias[:, par * NS:(par + 1) * NS], in0=sm_lnb[:], in1=sm_cum[:]),
          waits=[("sACT", lnb_done), ("sDVE", cum_ev)], incs=[("sDVE", 1)])
        mbias_done = C("sDVE")
        SM[c] = dict(flt_ld=flt_ld, fltc_ld=fltc_ld, mbias=mbias_done,
                     sgn=sgn_done, ctT=ctT_done, btok=btok_done, dec=dec_done)

    emit_stats(0)
    for c in range(NCHUNK):
        hg = c // 2
        szpar = hg % 2
        par = c % 2
        st_m = SM[c]
        E("sy", lambda e, c=c: e.dma_start(out=ut4[:], in_=utokH[c]),
          waits=[("qPL", utok_stored[c]), ("sPE", state["utok_cons"])],
          incs=[("qSY", 16)])
        ut_ld = C("qSY")
        E("sy", lambda e, c=c: e.dma_start(
            out=uf4[:], in_=ufmH[c].rearrange("p et t -> p (et t)")),
          waits=[("qPL", ufm_stored[c // 4]), ("sDVE", state["ufm_cons"])],
          incs=[("qSY", 16)])
        uf_ld = C("qSY")

        for et in range(NT):
            wts = [("sDVE", st_m["ctT"]), ("sDVE", state["prev_h_done"])]
            if et == 0:
                wts.append(("sACT", state["psB_z_done"]))
                wts.append(("sDVE", state["psB_evac_dve"]))
            E("pe", lambda e, et=et, par=par: e.matmul(
                out=psB[:, et * 128:(et + 1) * 128],
                lhsT=hst[:, et * 128:(et + 1) * 128],
                rhs=ctT[:, par * Q:(par + 1) * Q],
                start=(et % 4 == 0), stop=False),
              waits=wts, incs=[("sPE", 1)])
        y2done = C("sPE")

        for hh in range(2):
            for jj in range(2):
                wts = [("qSY", st_m["flt_ld"])]
                if hh == 0 and jj == 0:
                    wts.append(("sDVE", state["psA_free"]))
                if hh == 1 and jj == 0:
                    wts.append(("sDVE", C("sDVE")))
                E("pe", lambda e, hh=hh, jj=jj: e.matmul(
                    out=psA[:, jj * 512:(jj + 1) * 512], lhsT=ones1f[:],
                    rhs=flt[0:1, hh * 1024 + jj * 512:hh * 1024 + (jj + 1) * 512],
                    start=True, stop=True),
                  waits=wts, incs=[("sPE", 1)])
            bctm = C("sPE")
            state["flt_cons"] = bctm
            E("dve", lambda e, hh=hh: e.tensor_add(
                out=mb1[:, hh * 1024:(hh + 1) * 1024].rearrange("p (n t) -> p n t", n=8),
                in0=psA[:, 0:1024].rearrange("p (n t) -> p n t", n=8),
                in1=mneg[:, None, :].broadcast_to([128, 8, 128])),
              waits=[("sPE", bctm)], incs=[("sDVE", 1)])
            tmpd = C("sDVE")
            for ni in range(8):
                n = hh * 8 + ni
                E("act", lambda e, n=n, par=par: e.activation(
                    out=mb2[:, n * 128:(n + 1) * 128],
                    in_=mb1[:, n * 128:(n + 1) * 128], func=AF.Exp,
                    bias=sm_mbias[:, par * NS + n:par * NS + n + 1]),
                  waits=[("sDVE", tmpd), ("sDVE", st_m["mbias"])], incs=[("sACT", 1)])
            expa = C("sACT")
            E("pool", lambda e, hh=hh, par=par: e.tensor_mul(
                out=mb1[:, hh * 1024:(hh + 1) * 1024].rearrange(
                    "p (n t) -> p n t", n=8),
                in0=mb2[:, hh * 1024:(hh + 1) * 1024].rearrange(
                    "p (n t) -> p n t", n=8),
                in1=sm_sgn[:, par * NS + hh * 8:par * NS + (hh + 1) * 8][:, :, None
                    ].broadcast_to([128, 8, 128])),
              waits=[("sACT", expa), ("sACT", st_m["sgn"])], incs=[("sPOOL", 1)])
            expd = C("sPOOL")
            for jj in range(2):
                E("pe", lambda e, hh=hh, jj=jj: e.matmul(
                    out=psA[:, jj * 512:(jj + 1) * 512], lhsT=ones1b[:],
                    rhs=fltc[0:1, hh * 1024 + jj * 512:hh * 1024 + (jj + 1) * 512],
                    start=True, stop=True),
                  waits=([("qSY", st_m["fltc_ld"]), ("sDVE", tmpd)]
                         if jj == 0 else []),
                  incs=[("sPE", 1)])
            bccm = C("sPE")
            state["fltc_cons"] = bccm
            E("dve", lambda e, hh=hh: e.tensor_mul(
                out=mb2[:, hh * 1024:(hh + 1) * 1024],
                in0=mb1[:, hh * 1024:(hh + 1) * 1024], in1=psA[:, 0:1024]),
              waits=[("sPE", bccm), ("sPOOL", expd)], incs=[("sDVE", 1)])
        X1_done = C("sDVE")
        state["prev_X1_done"] = X1_done
        E("dve", lambda e: e.tensor_reduce(
            out=msb[:], in_=mb2[:].rearrange("p (n t) -> p t n", n=16),
            axis=AX.X, op=ALU.add),
          waits=[("sDVE", X1_done)], incs=[("sDVE", 1)])
        E("dve", lambda e: e.tensor_copy(out=mbf[:], in_=msb[:]),
          waits=[("sDVE", C("sDVE"))], incs=[("sDVE", 1)])
        mbf_done = C("sDVE")

        if c + 1 < NCHUNK:
            emit_stats(c + 1)

        for et in range(NT):
            E("pe", lambda e, et=et: e.matmul(
                out=psB[:, et * 128:(et + 1) * 128],
                lhsT=ut4[:, et * 128:(et + 1) * 128], rhs=mbf[:],
                start=False, stop=True),
              waits=[("sDVE", mbf_done), ("qSY", ut_ld), ("sPE", y2done)],
              incs=[("sPE", 1)])
        y1done = C("sPE")

        hmarks = []
        for rr in range(4):
            base = (rr % 2) * 512
            wts = [("qSY", ut_ld), ("sDVE", st_m["btok"]), ("sDVE", X1_done)]
            if rr >= 2:
                wts.append(("sDVE", hmarks[rr - 2]))
            E("pe", lambda e, rr=rr, base=base, par=par: e.matmul(
                out=psA[0:16, base:base + 512], lhsT=btok[:, par * NS:(par + 1) * NS],
                rhs=ut4[:, rr * 512:(rr + 1) * 512], start=True, stop=True),
              waits=wts, incs=[("sPE", 1)])
            dhm = C("sPE")
            E("dve", lambda e, rr=rr, base=base, par=par: e.scalar_tensor_tensor(
                out=hst[:, rr * 512:(rr + 1) * 512],
                in0=hst[:, rr * 512:(rr + 1) * 512], scalar=dec_c[:, par:par + 1],
                in1=psA[0:16, base:base + 512], op0=ALU.mult, op1=ALU.add),
              waits=[("sPE", dhm), ("sACT", st_m["dec"]), ("sPE", y2done)],
              incs=[("sDVE", 1)])
            hmarks.append(C("sDVE"))
        h_done = C("sDVE")
        state["prev_h_done"] = h_done
        state["psA_free"] = h_done
        utok_cons_mark = C("sPE")
        state["utok_cons"] = utok_cons_mark

        for et in range(NT):
            E("dve", lambda e, et=et: e.scalar_tensor_tensor(
                out=mb1[:, et * 128:(et + 1) * 128],
                in0=uf4[:, et * 128:(et + 1) * 128], scalar=dfm[:, et:et + 1],
                in1=psB[:, et * 128:(et + 1) * 128], op0=ALU.mult, op1=ALU.add),
              waits=[("sPE", y1done), ("qSY", uf_ld), ("sPOOL", expd)],
              incs=[("sDVE", 1)])
            dd = C("sDVE")
            E("pool", lambda e, et=et, c=c, szpar=szpar: e.tensor_mul(
                out=yzb[:, et * 128:(et + 1) * 128],
                in0=mb1[:, et * 128:(et + 1) * 128],
                in1=szf[:, szpar * NT * 256 + et * 256 + (c % 2) * 128:
                        szpar * NT * 256 + et * 256 + (c % 2) * 128 + 128]),
              waits=[("sDVE", dd), ("sACT", sz_ready[hg]),
                     ("qSY", state["yzb_stored"])],
              incs=[("sPOOL", 1)])
        psB_evac = C("sDVE")
        state["psB_evac_dve"] = psB_evac
        yz_done = C("sPOOL")
        szf_cons[szpar] = yz_done
        state["ufm_cons"] = psB_evac
        E("sy", lambda e, c=c: e.dma_start(out=yzH[c], in_=yzb[:]),
          waits=[("sPOOL", yz_done)], incs=[("qSY", 16)])
        state["yzb_stored"] = C("qSY")

        if c % 2 == 1 and hg + 1 < NHG:
            nxt = 1 - szpar
            sz_ready[hg + 1] = emit_zhalf(hg + 1, nxt, psB_evac, szf_cons[nxt])
            state["psB_z_done"] = sz_ready[hg + 1]
        else:
            state["psB_z_done"] = C("sACT")

    if PHASES == "p4":
        return finalize()
    b.barrier()

    CAP = 1024
    xsb = at("xsb", 10240, [128, 2 * DM], BF16)
    xsTb = at("xsTb", 14336, [128, 2 * DM], BF16)
    xnb5 = at("xnb5", 18432, [128, 2 * DM], BF16)
    cnt8s = at("cnt8s", 22848, [8, NCHUNK], F32)
    z832 = at("z832", 23040, [8, NCHUNK], F32)
    bceF = at("bceF", 23296, [1, 8 * NCHUNK], F32)
    bce32 = at("bce32", 24256, [NCHUNK, 8], F32)
    part1 = at("part1", 24384, [128, NCHUNK], F32)
    ohall = at("ohall", 24512, [128, NCHUNK * 8], F32)

    psc_read = 0
    for c in range(NCHUNK):
        for kc in range(8):
            wts = [("sDVE", psc_read)] if kc == 0 else []
            E("pe", lambda e, c=c, kc=kc: e.matmul(
                out=psC[:, 0:8],
                lhsT=xnT[:, kc * L + c * 128:kc * L + (c + 1) * 128],
                rhs=gwT[:, kc * 8:(kc + 1) * 8],
                start=(kc == 0), stop=(kc == 7)),
              waits=wts, incs=([("sPE", 1)] if kc == 7 else []))
        lg = C("sPE")
        E("dve", lambda e: e.reduce_max(out=rmaxb[:], in_=psC[:, 0:8], axis=AX.X),
          waits=[("sPE", lg)], incs=[("sDVE", 1)])
        rm = C("sDVE")
        E("dve", lambda e: e.tensor_single_scalar(
            out=oh8[:], in_=psC[:, 0:8], scalar=rmaxb[:], op=ALU.is_ge),
          waits=[("sDVE", rm)], incs=[("sDVE", 1)])
        E("dve", lambda e: e.memset(z8[:], 0.0),
          waits=[("sDVE", C("sDVE"))], incs=[("sDVE", 1)])
        E("dve", lambda e: e.tensor_tensor_scan(
            out=cum8[:], data0=oh8[:], data1=z8[:], initial=0.0,
            op0=ALU.add, op1=ALU.add),
          waits=[("sDVE", C("sDVE"))], incs=[("sDVE", 1)])
        E("dve", lambda e: e.tensor_single_scalar(
            out=cum8[:], in_=cum8[:], scalar=1.0, op=ALU.is_le),
          waits=[("sDVE", C("sDVE"))], incs=[("sDVE", 1)])
        E("dve", lambda e, c=c: e.tensor_mul(
            out=ohall[:, c * 8:(c + 1) * 8], in0=oh8[:], in1=cum8[:]),
          waits=[("sDVE", C("sDVE"))], incs=[("sDVE", 1)])
        ohf_done = C("sDVE")
        E("act", lambda e: e.activation(out=rwb[:], in_=rmaxb[:], func=AF.Tanh,
                                        scale=0.5),
          waits=[("sDVE", rm)], incs=[("sACT", 1)])
        E("dve", lambda e, c=c: e.tensor_scalar(
            out=rwall[:, c:c + 1], in0=rwb[:], scalar1=0.5, scalar2=0.5,
            op0=ALU.mult, op1=ALU.add),
          waits=[("sACT", C("sACT"))], incs=[("sDVE", 1)])
        E("pe", lambda e, c=c: e.matmul(out=psC[:, 8:16], lhsT=tris[:],
                                        rhs=ohall[:, c * 8:(c + 1) * 8],
                                        start=True, stop=True),
          waits=[("sDVE", ohf_done)], incs=[("sPE", 1)])
        E("pe", lambda e, c=c: e.matmul(out=psC[0:8, 24 + c:25 + c],
                                        lhsT=ohall[:, c * 8:(c + 1) * 8],
                                        rhs=onescol[:], start=True, stop=True),
          waits=[("sPE", C("sPE"))], incs=[("sPE", 1)])
        rkm = C("sPE")
        E("dve", lambda e: e.tensor_add(out=oh8[:] if False else cum8[:],
                                        in0=psC[:, 8:16], in1=ecapB[:]),
          waits=[("sPE", rkm)], incs=[("sDVE", 1)])
        E("dve", lambda e: e.tensor_tensor(out=cum8[:], in0=cum8[:], in1=limB[:],
                                           op=ALU.min),
          waits=[("sDVE", C("sDVE"))], incs=[("sDVE", 1)])
        E("dve", lambda e, c=c: e.tensor_mul(out=cum8[:], in0=cum8[:],
                                             in1=ohall[:, c * 8:(c + 1) * 8]),
          waits=[("sDVE", C("sDVE"))], incs=[("sDVE", 1)])
        E("dve", lambda e, c=c: e.tensor_reduce(
            out=part1[:, c:c + 1], in_=cum8[:], axis=AX.X, op=ALU.add),
          waits=[("sDVE", C("sDVE"))], incs=[("sDVE", 1)])
        psc_read = C("sDVE")

    E("dve", lambda e: e.tensor_copy(out=cnt8s[:], in_=psC[0:8, 24:24 + NCHUNK]),
      waits=[("sDVE", psc_read), ("sPE", C("sPE"))], incs=[("sDVE", 1)])
    E("dve", lambda e: e.memset(z832[:], 0.0),
      waits=[("sDVE", C("sDVE"))], incs=[("sDVE", 1)])
    E("dve", lambda e: e.tensor_tensor_scan(
        out=z832[:], data0=cnt8s[:], data1=z832[:], initial=0.0,
        op0=ALU.add, op1=ALU.add),
      waits=[("sDVE", C("sDVE"))], incs=[("sDVE", 1)])
    E("dve", lambda e: e.tensor_sub(out=cnt8s[:], in0=z832[:], in1=cnt8s[:]),
      waits=[("sDVE", C("sDVE"))], incs=[("sDVE", 1)])
    bce_done = C("sDVE")
    E("pe", lambda e: e.transpose(out=psC[0:NCHUNK, 72:80], in_=cnt8s[:],
                                  identity=identf[0:8, 0:8]),
      waits=[("sDVE", bce_done)], incs=[("sPE", 1)])
    bcet = C("sPE")
    E("dve", lambda e: e.tensor_copy(out=bce32[:], in_=psC[0:NCHUNK, 72:80]),
      waits=[("sPE", bcet)], incs=[("sDVE", 1)])
    bce32d = C("sDVE")
    E("pool", lambda e: e.dma_start(out=bceH, in_=bce32[:]),
      waits=[("sDVE", bce32d)], incs=[("qPL", 16)])
    E("pool", lambda e: e.dma_start(
        out=bceF[:], in_=bceH.rearrange("c e -> (c e)")[None, :]),
      waits=[("qPL", C("qPL"))], incs=[("qPL", 16)])
    bceF_ld = C("qPL")
    slot_marks = []
    for c in range(NCHUNK):
        E("pe", lambda e, c=c: e.matmul(
            out=psC[:, 64:72], lhsT=ones1f[:], rhs=bceF[0:1, c * 8:(c + 1) * 8],
            start=True, stop=True),
          waits=[("qPL", bceF_ld), ("sDVE", C("sDVE"))], incs=[("sPE", 1)])
        bcm = C("sPE")
        E("dve", lambda e, c=c: e.tensor_mul(
            out=oh8[:], in0=psC[:, 64:72], in1=ohall[:, c * 8:(c + 1) * 8]),
          waits=[("sPE", bcm)], incs=[("sDVE", 1)])
        E("dve", lambda e: e.tensor_reduce(out=rmaxb[:], in_=oh8[:], axis=AX.X,
                                           op=ALU.add),
          waits=[("sDVE", C("sDVE"))], incs=[("sDVE", 1)])
        E("dve", lambda e, c=c: e.tensor_add(out=rmaxb[:], in0=rmaxb[:],
                                             in1=part1[:, c:c + 1]),
          waits=[("sDVE", C("sDVE"))], incs=[("sDVE", 1)])
        E("dve", lambda e, c=c: e.tensor_copy(out=slotI[:, c:c + 1], in_=rmaxb[:]),
          waits=[("sDVE", C("sDVE"))], incs=[("sDVE", 1)])
        slot_marks.append(C("sDVE"))
    slots_done = C("sDVE")

    xs_scattered = 0
    xnb5_war = [0, 0]
    xsb_war = [0, 0]
    for c in range(NCHUNK):
        par = c % 2
        for g2 in range(2):
            for s4 in range(4):
                kc = g2 * 4 + s4
                E("pe", lambda e, kc=kc, c=c, s4=s4: e.transpose(
                    out=psT[:, s4 * 128:(s4 + 1) * 128],
                    in_=xnT[:, kc * L + c * 128:kc * L + (c + 1) * 128],
                    identity=identb[:]),
                  waits=[("sDVE", psT_evac[0]), ("sDVE", xnb5_war[par])],
                  incs=[("sPE", 1)])
            td = C("sPE")
            for s4 in range(4):
                kc = g2 * 4 + s4
                E("dve", lambda e, kc=kc, s4=s4, par=par: e.tensor_copy(
                    out=xnb5[:, par * DM + kc * 128:par * DM + (kc + 1) * 128],
                    in_=psT[:, s4 * 128:(s4 + 1) * 128]),
                  waits=[("sPE", td)], incs=[("sDVE", 1)])
            psT_evac[0] = C("sDVE")
        xn_tr = C("sDVE")
        E("dve", lambda e, c=c, par=par: e.tensor_scalar_mul(
            out=xsb[:, par * DM:(par + 1) * DM], in0=xnb5[:, par * DM:(par + 1) * DM],
            scalar1=rwall[:, c:c + 1]),
          waits=[("sDVE", xn_tr), ("qPL", xsb_war[par]), ("sDVE", slots_done)],
          incs=[("sDVE", 1)])
        xnb5_war[par] = C("sDVE")
        xsd = C("sDVE")
        if SKIP_SCATTER:
            E("pool", lambda e, c=c: e.dma_start(
                out=xsH[c * 128:(c + 1) * 128, :],
                in_=xsb[:, par * DM:(par + 1) * DM]),
              waits=[("sDVE", xsd)], incs=[("qPL", 16)])
        else:
            E("pool", lambda e, c=c, par=par: e.indirect_dma_start(
                out=xsH, out_offset=bass.IndirectOffsetOnAxis(slotI[:, c:c + 1], 0),
                in_=xsb[:, par * DM:(par + 1) * DM], in_offset=None),
              waits=[("sDVE", xsd)], incs=[("qPL", 16)])
        xsb_war[par] = C("qPL")
        xs_scattered = C("qPL")

    def load_expert_sp(e_idx, slot, war_pe):
        cm = [C("sPOOL")]
        slot_cast = [C("sPOOL"), C("sPOOL")]
        marks = []
        for (wsrc, dst_base, nkc, wid) in [
            (wgT_h, 0, 8, 512), (wuT_h, 4096, 8, 512), (wdT_h, 8192, 4, 1024)]:
            for kc in range(nkc):
                sl = stage[:, (kc % 2) * 1024:(kc % 2) * 1024 + wid]
                E("sy", lambda e, e_idx=e_idx, kc=kc, sl=sl, wsrc=wsrc: e.dma_start(
                    out=sl, in_=wsrc[e_idx, kc * 128:(kc + 1) * 128, :]),
                  waits=[("sPOOL", slot_cast[kc % 2])], incs=[("qSY", 16)])
                if dst_base < 8192:
                    E("pool", lambda e, kc=kc, slot=slot, sl=sl, dst_base=dst_base,
                      wid=wid: e.tensor_copy(
                        out=WA[:, slot * 8192 + dst_base + kc * wid:
                                slot * 8192 + dst_base + (kc + 1) * wid], in_=sl),
                      waits=[("qSY", C("qSY")), ("sPE", war_pe)], incs=[("sPOOL", 1)])
                else:
                    E("pool", lambda e, kc=kc, slot=slot, sl=sl, wid=wid: e.tensor_copy(
                        out=WB[:, slot * 4096 + kc * wid:slot * 4096 + (kc + 1) * wid],
                        in_=sl),
                      waits=[("qSY", C("qSY")), ("sPE", war_pe)], incs=[("sPOOL", 1)])
                cm[0] = C("sPOOL")
                slot_cast[kc % 2] = C("sPOOL")
            marks.append(cm[0])
        return marks

    ew = {}
    ew[0] = load_expert_sp(0, 0, 0)
    arena_pe = [0, 0]
    gs_stored = 0
    xsb_war5 = [0, 0]
    down_evac = 0
    for e_i in range(8):
        slot = e_i % 2
        wm = ew[e_i]
        if e_i + 1 < 8:
            ew[e_i + 1] = load_expert_sp(e_i + 1, 1 - slot, arena_pe[1 - slot])
        for r in range(8):
            par = r % 2
            E("sy", lambda e, e_i=e_i, r=r, par=par: e.dma_start(
                out=xsb[:, par * DM:(par + 1) * DM],
                in_=xsH[e_i * CAP + r * 128:e_i * CAP + (r + 1) * 128, :]),
              waits=[("qPL", xs_scattered), ("sPE", xsb_war5[par])],
              incs=[("qSY", 16)])
            xs_ld = C("qSY")
            for g in range(2):
                for s4 in range(4):
                    fc = g * 4 + s4
                    E("pe", lambda e, fc=fc, s4=s4, par=par: e.transpose(
                        out=psT[:, s4 * 128:(s4 + 1) * 128],
                        in_=xsb[:, par * DM + fc * 128:par * DM + (fc + 1) * 128],
                        identity=identb[:]),
                      waits=[("qSY", xs_ld), ("sDVE", psT_evac[0])],
                      incs=[("sPE", 1)])
                td = C("sPE")
                for s4 in range(4):
                    fc = g * 4 + s4
                    E("dve", lambda e, fc=fc, s4=s4, par=par: e.tensor_copy(
                        out=xsTb[:, par * DM + fc * 128:par * DM + (fc + 1) * 128],
                        in_=psT[:, s4 * 128:(s4 + 1) * 128]),
                      waits=[("sPE", td)], incs=[("sDVE", 1)])
                psT_evac[0] = C("sDVE")
            xsT_done = C("sDVE")
            for kc in range(8):
                wts = [("sPOOL", wm[0]), ("sDVE", xsT_done)]
                if kc == 0:
                    wts.append(("sACT", C("sACT")))
                E("pe", lambda e, kc=kc, slot=slot, par=par: e.matmul(
                    out=psB[:, 1024:1536],
                    lhsT=xsTb[:, par * DM + kc * 128:par * DM + (kc + 1) * 128],
                    rhs=WA[:, slot * 8192 + kc * 512:slot * 8192 + (kc + 1) * 512],
                    start=(kc == 0), stop=(kc == 7)),
                  waits=wts, incs=([("sPE", 1)] if kc == 7 else []))
            gmm = C("sPE")
            for kc in range(8):
                wts = [("sPOOL", wm[1]), ("sDVE", xsT_done)]
                if kc == 0:
                    wts.append(("sDVE", C("sDVE")))
                E("pe", lambda e, kc=kc, slot=slot, par=par: e.matmul(
                    out=psB[:, 1536:2048],
                    lhsT=xsTb[:, par * DM + kc * 128:par * DM + (kc + 1) * 128],
                    rhs=WA[:, slot * 8192 + 4096 + kc * 512:
                            slot * 8192 + 4096 + (kc + 1) * 512],
                    start=(kc == 0), stop=(kc == 7)),
                  waits=wts, incs=([("sPE", 1)] if kc == 7 else []))
            umm = C("sPE")
            xsb_war5[par] = umm
            E("act", lambda e: e.activation(out=sgb[:], in_=psB[:, 1024:1536],
                                            func=AF.Silu),
              waits=[("sPE", gmm)], incs=[("sACT", 1)])
            sgd = C("sACT")
            E("act", lambda e: e.activation(out=upb[:], in_=psB[:, 1536:2048],
                                            func=AF.Copy),
              waits=[("sPE", umm)], incs=[("sACT", 1)])
            upd = C("sACT")
            E("dve", lambda e: e.tensor_mul(out=pbf_[:], in0=sgb[:], in1=upb[:]),
              waits=[("sACT", sgd), ("sACT", upd), ("sPE", C("sPE"))],
              incs=[("sDVE", 1)])
            pd = C("sDVE")
            for tq in range(4):
                E("pe", lambda e, tq=tq: e.transpose(
                    out=psT[:, tq * 128:(tq + 1) * 128],
                    in_=pbf_[:, tq * 128:(tq + 1) * 128], identity=identb[:]),
                  waits=[("sDVE", pd), ("sDVE", psT_evac[0])], incs=[("sPE", 1)])
            td = C("sPE")
            for tq in range(4):
                E("dve", lambda e, tq=tq: e.tensor_copy(
                    out=pTb[:, tq * 128:(tq + 1) * 128],
                    in_=psT[:, tq * 128:(tq + 1) * 128]),
                  waits=[("sPE", td)], incs=[("sDVE", 1)])
            psT_evac[0] = C("sDVE")
            ptd = C("sDVE")
            for kd in range(4):
                for jj in range(2):
                    wts = [("sDVE", ptd), ("sPOOL", wm[2])]
                    if kd == 0 and jj == 0:
                        wts.append(("sACT", down_evac))
                    E("pe", lambda e, kd=kd, jj=jj, slot=slot: e.matmul(
                        out=psB[:, jj * 512:(jj + 1) * 512],
                        lhsT=pTb[:, kd * 128:(kd + 1) * 128],
                        rhs=WB[:, slot * 4096 + kd * 1024 + jj * 512:
                                slot * 4096 + kd * 1024 + (jj + 1) * 512],
                        start=(kd == 0), stop=(kd == 3)),
                      waits=wts, incs=([("sPE", 1)] if (kd == 3 and jj == 1) else []))
            dmm = C("sPE")
            arena_pe[slot] = dmm
            E("act", lambda e: e.activation(out=ffnacc[:].bitcast(BF16)[:, 0:DM],
                                            in_=psB[:, 0:1024], func=AF.Silu),
              waits=[("sPE", dmm), ("qSC", gs_stored)], incs=[("sACT", 1)])
            gsa = C("sACT")
            down_evac = gsa
            E("act", lambda e, e_i=e_i, r=r: e.dma_start(
                out=gsH[e_i * CAP + r * 128:e_i * CAP + (r + 1) * 128, :],
                in_=ffnacc[:].bitcast(BF16)[:, 0:DM]),
              waits=[("sACT", gsa)], incs=[("qSC", 16)])
            gs_stored = C("qSC")
    gs_all = gs_stored
    if PHASES == "pb" and DBG_SCRATCH:
        E("dve", lambda e: e.tensor_copy(out=sgb[:, 0:32], in_=slotI[:]),
          waits=[("sDVE", C("sDVE"))], incs=[("sDVE", 1)])
        sf = C("sDVE")
        E("sy", lambda e: e.dma_start(out=out_h[0:128, 0:32], in_=part1[:]),
          waits=[("sDVE", sf)], incs=[("qSC", 16)])
        E("sy", lambda e: e.dma_start(out=out_h[0:128, 32:288], in_=ohall[:]),
          waits=[("sDVE", sf)], incs=[("qSC", 16)])
        E("sy", lambda e: e.dma_start(out=out_h[0:128, 288:320], in_=rwall[:, 0:32]),
          waits=[("sDVE", sf)], incs=[("qSC", 16)])
        E("sy", lambda e: e.dma_start(out=out_h[0:128, 320:352], in_=sgb[:, 0:32]),
          waits=[("sDVE", sf)], incs=[("qSC", 16)])
        E("sy", lambda e: e.dma_start(out=out_h[128:129, 0:256], in_=bceF[:]),
          waits=[("sDVE", sf)], incs=[("qSC", 16)])
        for rr in range(8):
            E("sy", lambda e, rr=rr: e.dma_start(
                out=out_h[512 + rr * 128:512 + (rr + 1) * 128, 0:512].bitcast(BF16),
                in_=xsH[rr * 1024:rr * 1024 + 128, :]),
              waits=[("qPL", xs_scattered)], incs=[("qSC", 16)])
        for nm, srcap, dst0 in [("xn0", xnH[0], 2000), ("xs0", xsH[0:128, :], 2128),
                                ("gs0", gsH[0:128, :], 2256), ("xs1", xsH[1024:1152, :], 2384)]:
            E("sy", lambda e, srcap=srcap: e.dma_start(
                out=xsb[:, 0:DM], in_=srcap),
              waits=[("qSC", gs_all), ("qPL", C("qPL")), ("sDVE", C("sDVE")),
                     ("qSY", C("qSY")), ("sPE", C("sPE")), ("sACT", C("sACT")),
                     ("sPOOL", C("sPOOL"))],
              incs=[("qSY", 16)])
            E("dve", lambda e: e.tensor_copy(out=ffnacc[:], in_=xsb[:, 0:DM]),
              waits=[("qSY", C("qSY"))], incs=[("sDVE", 1)])
            E("sy", lambda e, dst0=dst0: e.dma_start(
                out=out_h[dst0:dst0 + 128, :], in_=ffnacc[:]),
              waits=[("sDVE", C("sDVE"))], incs=[("qSC", 16)])
        b.wait("sy", [("qSC", C("qSC"))])
    if DBG_SCRATCH:
        E("sy", lambda e: e.dma_start(out=xsD, in_=xsH),
          waits=[("qPL", xs_scattered)], incs=[("qSY", 16)])
        E("sy", lambda e: e.dma_start(out=gsD, in_=gsH),
          waits=[("qSC", gs_all)], incs=[("qSY", 16)])
        b.cnt["qSY"] += 0

    if PHASES == "pb":
        return finalize()
    b.barrier()

    cast_mark = C("sPOOL")
    for et in range(NT):
        sl = stage[:, (et % 2) * 1024:(et % 2 + 1) * 1024]
        wts = [("sPOOL", cast_mark)] if et >= 2 else []
        E("sy", lambda e, et=et, sl=sl: e.dma_start(
            out=sl, in_=woutT_h[et * 128:(et + 1) * 128, :]),
          waits=wts, incs=[("qSY", 16)])
        E("pool", lambda e, et=et, sl=sl: e.tensor_copy(
            out=WA[:, et * 1024:(et + 1) * 1024], in_=sl),
          waits=[("qSY", C("qSY"))], incs=[("sPOOL", 1)])
        cast_mark = C("sPOOL")
    wout_done = cast_mark

    out_stored = 0
    mm5_done = [0, 0]
    evacd5 = [0, 0]
    for c in range(NCHUNK):
        par = c % 2
        E("sy", lambda e, c=c, par=par: e.dma_start(
            out=yzin[:, par * DI:(par + 1) * DI], in_=yzH[c]),
          waits=[("sPE", mm5_done[par])], incs=[("qSY", 16)])
        yz_ld = C("qSY")
        E("sy", lambda e, c=c, par=par: e.dma_start(
            out=xt5[:, par * DM:(par + 1) * DM], in_=x_h[c * 128:(c + 1) * 128, :]),
          waits=[("sDVE", evacd5[par])], incs=[("qSY", 16)])
        x_ld = C("qSY")
        E("pool", lambda e, c=c, par=par: e.indirect_dma_start(
            out=g5[:, par * DM:(par + 1) * DM], out_offset=None,
            in_=gsH, in_offset=bass.IndirectOffsetOnAxis(slotI[:, c:c + 1], 0)),
          waits=[("qSC", gs_all), ("sDVE", evacd5[par])], incs=[("qPL", 16)])
        g_ld = C("qPL")
        for et in range(NT):
            for jj in range(2):
                wts = [("qSY", yz_ld), ("sPOOL", wout_done)]
                if et == 0 and jj == 0:
                    wts.append(("sDVE", evacd5[par]))
                E("pe", lambda e, et=et, par=par, jj=jj: e.matmul(
                    out=psB[:, par * 1024 + jj * 512:par * 1024 + (jj + 1) * 512],
                    lhsT=yzin[:, par * DI + et * 128:par * DI + (et + 1) * 128],
                    rhs=WA[:, et * 1024 + jj * 512:et * 1024 + (jj + 1) * 512],
                    start=(et == 0), stop=(et == NT - 1)),
                  waits=wts, incs=([("sPE", 1)] if (et == NT - 1 and jj == 1) else []))
        smm = C("sPE")
        mm5_done[par] = smm
        E("dve", lambda e, par=par: e.tensor_mul(
            out=outb[:, par * DM:(par + 1) * DM],
            in0=psB[:, par * 1024:(par + 1) * 1024],
            in1=g5[:, par * DM:(par + 1) * DM]),
          waits=[("sPE", smm), ("qPL", g_ld), ("qSC", out_stored)],
          incs=[("sDVE", 1)])
        E("dve", lambda e, par=par: e.tensor_add(
            out=outb[:, par * DM:(par + 1) * DM],
            in0=outb[:, par * DM:(par + 1) * DM],
            in1=xt5[:, par * DM:(par + 1) * DM]),
          waits=[("sDVE", C("sDVE")), ("qSY", x_ld)], incs=[("sDVE", 1)])
        evacd5[par] = C("sDVE")
        E("act", lambda e, c=c, par=par: e.dma_start(
            out=out_h[c * 128:(c + 1) * 128, :], in_=outb[:, par * DM:(par + 1) * DM]),
          waits=[("sDVE", evacd5[par])], incs=[("qSC", 16)])
        out_stored = C("qSC")

    b.wait("sy", [("qSC", out_stored)])
    b.wait("act", [("qSC", out_stored)])

    return finalize()


_prog_cache = {}


def prep_inputs(L0_attn_x, L0_attn_W_in, L0_attn_conv_w, L0_attn_conv_b, L0_attn_W_x,
                L0_attn_W_dt, L0_attn_b_dt, L0_attn_A_log, L0_attn_D, L0_attn_W_out,
                L0_ffn_gate_w, L0_ffn_w_gate, L0_ffn_w_up, L0_ffn_w_down):
    x = np.ascontiguousarray(np.asarray(L0_attn_x, np.float32))
    winT = np.ascontiguousarray(np.asarray(L0_attn_W_in, np.float32).T)
    wxdtT = np.ascontiguousarray(
        np.concatenate([np.asarray(L0_attn_W_x, np.float32),
                        np.asarray(L0_attn_W_dt, np.float32)], 0).T)
    woutT = np.ascontiguousarray(np.asarray(L0_attn_W_out, np.float32).T)
    cw = np.asarray(L0_attn_conv_w, np.float32)[:, 0, :]
    convw = np.ascontiguousarray(
        cw.reshape(NT, 128, 4).transpose(1, 0, 2).reshape(128, NT * 4))
    convb = np.ascontiguousarray(
        np.asarray(L0_attn_conv_b, np.float32).reshape(NT, 128).T)
    dfm = np.ascontiguousarray(np.asarray(L0_attn_D, np.float32).reshape(NT, 128).T)
    bdt = np.asarray(L0_attn_b_dt, np.float32)[None, :]
    alog = np.asarray(L0_attn_A_log, np.float32)[None, :]
    gwT = np.ascontiguousarray(np.asarray(L0_ffn_gate_w, np.float32).T)
    wgT = np.ascontiguousarray(np.asarray(L0_ffn_w_gate, np.float32).transpose(0, 2, 1))
    wuT = np.ascontiguousarray(np.asarray(L0_ffn_w_up, np.float32).transpose(0, 2, 1))
    wdT = np.ascontiguousarray(np.asarray(L0_ffn_w_down, np.float32).transpose(0, 2, 1))

    shared = dict(winT=winT, wxdtT=wxdtT, woutT=woutT, convw=convw, convb=convb,
                  dfm=dfm, bdt=bdt, alog=alog, gwT=gwT, wgT=wgT, wuT=wuT, wdT=wdT)
    return shared, x


def kernel(**inputs):
    shared, x = prep_inputs(**inputs)
    nc = build_program()
    in_maps = [dict(shared, x=np.ascontiguousarray(x[bb])) for bb in range(8)]
    res = run_bass_kernel_spmd(nc, in_maps, core_ids=list(range(8)))
    return np.stack([r["out"] for r in res.results], axis=0)


# version 12
# speedup vs baseline: 1.3547x; 1.3547x over previous
import numpy as np
from collections import defaultdict

import concourse.bass as bass
import concourse.mybir as mybir
from concourse.bass_utils import run_bass_kernel_spmd

F32 = mybir.dt.float32
BF16 = mybir.dt.bfloat16
AF = mybir.ActivationFunctionType
ALU = mybir.AluOpType
AX = mybir.AxisListType

L = 4096
DM = 1024
DI = 2048
NS = 16
Q = 128
NCHUNK = L // Q
NT = 16
TW = 512
NTW = L // TW
NHG = L // 256
PHASES = "all"
P0_LIMIT = NCHUNK
P0_SKIP = ""
SKIP_SCATTER = False
DBG_SCRATCH = False


class Builder:
    ENGS = ("sy", "pe", "act", "dve", "pool")

    def __init__(self):
        self.q = {e: [] for e in self.ENGS}
        self.cnt = defaultdict(int)
        self.sem = None

    def emit(self, eng, fn, waits=(), incs=()):
        waits = [(s, v) for (s, v) in waits if v > 0]
        for (s, v) in waits:
            assert v <= self.cnt[s], f"wait on future count {s} {v} > {self.cnt[s]}"
        incs = [(s, n) for (s, n) in incs]
        for (s, n) in incs:
            self.cnt[s] += n

        def cl(ec, _fn=fn, _w=waits, _i=incs):
            for (s, v) in _w:
                ec.wait_ge(self.sem[s], v)
            inst = _fn(ec)
            for (s, n) in _i:
                inst.then_inc(self.sem[s], n)
        self.q[eng].append(cl)
        return {s: self.cnt[s] for s, _ in incs}

    def c(self, s):
        return self.cnt[s]

    def wait(self, eng, waits):
        def cl(ec, _w=list(waits)):
            for (s, v) in _w:
                ec.wait_ge(self.sem[s], v)
        self.q[eng].append(cl)

    def barrier(self):
        targets = {}
        for e in self.ENGS:
            self.cnt["b" + e] += 1
            targets[e] = self.cnt["b" + e]
        for e in self.ENGS:
            def cl(ec, _e=e, _t=dict(targets)):
                ec.sem_inc(self.sem["b" + _e], 1)
                for oe, v in _t.items():
                    ec.wait_ge(self.sem["b" + oe], v)
            self.q[e].append(cl)


def build_program():
    nc = bass.Bass("TRN2", target_bir_lowering=False, debug=False)
    b = Builder()
    E, C = b.emit, b.c

    x_h = nc.dram_tensor("x", [L, DM], F32, kind="ExternalInput").ap()
    winT_h = nc.dram_tensor("winT", [DM, 2 * DI], F32, kind="ExternalInput").ap()
    wxdtT_h = nc.dram_tensor("wxdtT", [DI, 48], F32, kind="ExternalInput").ap()
    woutT_h = nc.dram_tensor("woutT", [DI, DM], F32, kind="ExternalInput").ap()
    convw_h = nc.dram_tensor("convw", [128, NT * 4], F32, kind="ExternalInput").ap()
    convb_h = nc.dram_tensor("convb", [128, NT], F32, kind="ExternalInput").ap()
    dfm_h = nc.dram_tensor("dfm", [128, NT], F32, kind="ExternalInput").ap()
    bdt_h = nc.dram_tensor("bdt", [1, NS], F32, kind="ExternalInput").ap()
    alog_h = nc.dram_tensor("alog", [1, NS], F32, kind="ExternalInput").ap()
    gwT_h = nc.dram_tensor("gwT", [DM, 8], F32, kind="ExternalInput").ap()
    wgT_h = nc.dram_tensor("wgT", [8, DM, 512], F32, kind="ExternalInput").ap()
    wuT_h = nc.dram_tensor("wuT", [8, DM, 512], F32, kind="ExternalInput").ap()
    wdT_h = nc.dram_tensor("wdT", [8, 512, DM], F32, kind="ExternalInput").ap()
    out_h = nc.dram_tensor("out", [L, DM], F32, kind="ExternalOutput").ap()

    KSC = "ExternalOutput" if DBG_SCRATCH else "Internal"
    ufmH = nc.dram_tensor("ufmH", [NCHUNK, 128, NT, Q], BF16, kind=KSC).ap()
    utokH = nc.dram_tensor("utokH", [NCHUNK, 128, DI], BF16, kind=KSC).ap()
    bcH = nc.dram_tensor("bcH", [48, L], F32, kind=KSC).ap()
    yzH = nc.dram_tensor("yzH", [NCHUNK, 128, DI], BF16, kind=KSC).ap()
    gH = nc.dram_tensor("gH", [NCHUNK, 128, DM], BF16, kind=KSC).ap()
    xnH = nc.dram_tensor("xnH", [NCHUNK, 128, DM], BF16, kind="Internal").ap()
    xsH = nc.dram_tensor("xsH", [8 * 1024, DM], BF16, kind="Internal").ap()
    gsH = nc.dram_tensor("gsH", [8 * 1024, DM], BF16, kind="Internal").ap()
    xsD = nc.dram_tensor("xsD", [8 * 1024, DM], BF16, kind=KSC).ap() if DBG_SCRATCH else None
    gsD = nc.dram_tensor("gsD", [8 * 1024, DM], BF16, kind=KSC).ap() if DBG_SCRATCH else None
    bceH = nc.dram_tensor("bceH", [NCHUNK, 8], F32, kind="Internal").ap()
    slotH = nc.dram_tensor("slotH", [128, NCHUNK], mybir.dt.int32, kind=KSC).ap()
    rtD = nc.dram_tensor("rtD", [128, 640], F32, kind=KSC).ap()
    partH = nc.dram_tensor("partH", [NCHUNK, 128, DM], F32, kind="Internal").ap()
    wzHd = nc.dram_tensor("wzHd", [DM, DI], BF16, kind="Internal").ap()
    fcumH = nc.dram_tensor("fcumH", [NS, Q], F32, kind="Internal").ap()
    fcH = nc.dram_tensor("fcH", [NS, Q], BF16, kind="Internal").ap()

    def sb(name, shape, dtype):
        return nc.alloc_sbuf_tensor(name, shape, dtype)

    xnT = sb("xnT", [128, 8 * L], BF16)
    hst = sb("hst", [NS, DI], F32)
    triu = sb("triu", [128, 128], F32)
    mneg = sb("mneg", [128, 128], F32)
    identb = sb("identb", [128, 128], BF16)
    identf = sb("identf", [128, 128], F32)
    onescol = sb("onescol", [128, 1], F32)
    ones1f = sb("ones1f", [1, 128], F32)
    ones1b = sb("ones1b", [1, 128], BF16)
    convw = sb("convw_s", [128, NT * 4], F32)
    convb = sb("convb_s", [128, NT], F32)
    dfm = sb("dfm_s", [128, NT], F32)
    bdt_r = sb("bdt_r", [1, NS], F32)
    alog_r = sb("alog_r", [1, NS], F32)
    arow = sb("arow", [1, NS], F32)
    bdtB = sb("bdtB", [128, NS], F32)
    aB = sb("aB", [128, NS], F32)
    wxdtT = sb("wxdtT_s", [128, NT * 48], BF16)
    gwT = sb("gwT_s", [128, 8 * 8], BF16)
    rwall = sb("rwall", [128, NCHUNK * 8], F32)
    slotI = sb("slotI", [128, NCHUNK], mybir.dt.int32)
    tris = sb("tris", [128, 128], F32)
    ecapB = sb("ecapB", [128, 8], F32)
    limB = sb("limB", [128, 8], F32)
    stage = sb("stage", [128, 2048], F32)
    WA = sb("WA", [128, 16 * 1024], BF16)
    WB = sb("WB", [128, 16 * 1024], BF16)

    SCR = 52 * 1024
    scr = sb("scr", [128, SCR], mybir.dt.uint8)
    scr_addr = nc.lookup_mloc(scr).addr

    def at(name, off, shape, dtype):
        assert off % 64 == 0
        nbytes = int(np.prod(shape[1:])) * mybir.dt.size(dtype)
        assert off + nbytes <= SCR, f"{name} overflows scratch: {off}+{nbytes}"
        return nc.alloc_sbuf_tensor_at(name, shape, dtype, offset=scr_addr + off)

    xt0 = at("xt0", 0, [128, 2 * DM], F32)
    xnb = at("xnb", 8192, [128, DM], BF16)
    st1 = at("st1", 10240, [128, 1], F32)
    st2 = at("st2", 10304, [128, 1], F32)
    st3 = at("st3", 10368, [128, 1], F32)
    tileb = at("tileb", 0, [128, 2 * (TW + 3)], F32)
    convacc = at("convacc", 4224, [128, 2 * TW], F32)
    tails = at("tails", 8320, [128, NT * 3], F32)
    bcev = at("bcev", 8576, [48, TW], F32)
    ufm16 = at("ufm16", 10624, [128, NT * TW], BF16)
    utokb = at("utokb", 27008, [128, 2 * DI], BF16)
    convacc2 = at("convacc2", 35200, [128, 2 * TW], F32)
    tbuf2 = at("tbuf2", 39296, [128, TW], F32)
    szf = at("szf", 0, [128, 2 * NT * 256], BF16)
    ut4 = at("ut4", 16384, [128, DI], BF16)
    uf4 = at("uf4", 20480, [128, DI], BF16)
    mb1 = at("mb1", 24576, [128, NS * Q], F32)
    mb2 = at("mb2", 32768, [128, NS * Q], BF16)
    yzb = at("yzb", 36864, [128, DI], BF16)
    flt = at("flt", 40960, [1, NS * Q], F32)
    fltc = at("fltc", 49152, [1, NS * Q], BF16)
    sm_cum = sb("sm_cum", [128, NS], F32)
    sm_mbias = sb("sm_mbias", [128, 2 * NS], F32)
    sm_dB = sb("sm_dB", [128, NS], F32)
    sm_lnb = sb("sm_lnb", [128, NS], F32)
    sm_sgn = sb("sm_sgn", [128, 2 * NS], F32)
    sm_dlt = sb("sm_dlt", [128, NS], F32)
    sm_a = sb("sm_a", [128, NS], F32)
    sm_b = sb("sm_b", [128, NS], F32)
    sm_cbf = sb("sm_cbf", [128, NS], BF16)
    sm_ctl = sb("sm_ctl", [128, NS], F32)
    cumT_e = sb("cumT_e", [NS, Q], F32)
    cb16 = sb("cb16", [NS, Q], BF16)
    ctT = sb("ctT", [NS, 2 * Q], F32)
    btok = sb("btok", [128, 2 * NS], BF16)
    dec_c = sb("dec_c", [NS, 2], F32)
    clF = sb("clF", [1, NS], F32)
    msb = sb("msb", [128, Q], F32)
    mbf = sb("mbf", [128, Q], BF16)
    bc4 = sb("bc4", [48, 2 * Q], F32)
    sgb = at("sgb", 0, [128, 512], F32)
    upb = at("upb", 2048, [128, 512], F32)
    pbf_ = at("pbf_", 4096, [128, 512], BF16)
    pTb = at("pTb", 5120, [128, 512], BF16)
    ffnacc = at("ffnacc", 6144, [128, DM], F32)
    rmaxb = at("rmaxb", 22528, [128, 1], F32)
    rwb = at("rwb", 22592, [128, 1], F32)
    oh8 = at("oh8", 22656, [128, 8], F32)
    cum8 = at("cum8", 22720, [128, 8], F32)
    z8 = at("z8", 22784, [128, 8], F32)
    yzin = at("yzin", 24576, [128, 2 * DI], BF16)
    outb = at("outb", 32768, [128, 2 * DM], F32)
    xt5 = at("xt5", 40960, [128, 2 * DM], F32)
    g5 = at("g5", 49152, [128, 2 * DM], BF16)

    psA = nc.alloc_psum_tensor("psA", [128, 1024], F32)
    psB = nc.alloc_psum_tensor("psB", [128, 2048], F32)
    psC = nc.alloc_psum_tensor("psC", [128, 512], F32)
    psT = nc.alloc_psum_tensor("psT", [128, 1024], BF16)

    for sn in (["qSY", "qSC", "qPL", "sPE", "sACT", "sDVE", "sPOOL"] +
               ["b" + e for e in Builder.ENGS]):
        pass
    b.sem = {sn: nc.alloc_semaphore(sn) for sn in
             ["qSY", "qSC", "qPL", "sPE", "sACT", "sDVE", "sPOOL"] +
             ["b" + e for e in Builder.ENGS]}

    def finalize():
        b.wait("sy", [("qPL", C("qPL")), ("qSY", C("qSY")), ("qSC", C("qSC"))])
        b.barrier()
        with nc.Block() as block:
            @block.sync
            def _(eng):
                for cl in b.q["sy"]:
                    cl(eng)

            @block.tensor
            def _(eng):
                for cl in b.q["pe"]:
                    cl(eng)

            @block.scalar
            def _(eng):
                for cl in b.q["act"]:
                    cl(eng)

            @block.vector
            def _(eng):
                for cl in b.q["dve"]:
                    cl(eng)

            @block.gpsimd
            def _(eng):
                for cl in b.q["pool"]:
                    cl(eng)
        return nc

    for ap_sb, ap_h in [(convw, convw_h), (convb, convb_h), (dfm, dfm_h),
                        (bdt_r, bdt_h), (alog_r, alog_h)]:
        E("sy", lambda e, a=ap_sb, h=ap_h: e.dma_start(out=a[:], in_=h),
          incs=[("qSY", 16)])
    small_in = C("qSY")

    E("pool", lambda e: e.memset(triu[:], 1.0))
    E("pool", lambda e: e.affine_select(out=triu[:], in_=triu[:], compare_op=ALU.is_ge,
                                        fill=0.0, base=0, pattern=[[1, 128]],
                                        channel_multiplier=-1))
    E("pool", lambda e: e.memset(mneg[:], 0.0))
    E("pool", lambda e: e.affine_select(out=mneg[:], in_=mneg[:], compare_op=ALU.is_ge,
                                        fill=-1e9, base=0, pattern=[[1, 128]],
                                        channel_multiplier=-1))
    E("pool", lambda e: e.memset(identb[:], 0.0))
    E("pool", lambda e: e.affine_select(out=identb[:], in_=identb[:],
                                        compare_op=ALU.not_equal, fill=1.0, base=0,
                                        pattern=[[-1, 128]], channel_multiplier=1))
    E("pool", lambda e: e.memset(identf[:], 0.0))
    E("pool", lambda e: e.affine_select(out=identf[:], in_=identf[:],
                                        compare_op=ALU.not_equal, fill=1.0, base=0,
                                        pattern=[[-1, 128]], channel_multiplier=1))
    E("pool", lambda e: e.memset(onescol[:], 1.0))
    E("pool", lambda e: e.memset(ones1f[:], 1.0))
    E("pool", lambda e: e.memset(ones1b[:], 1.0))
    E("pool", lambda e: e.memset(tris[:], 1.0))
    E("pool", lambda e: e.affine_select(out=tris[:], in_=tris[:], compare_op=ALU.is_gt,
                                        fill=0.0, base=0, pattern=[[1, 128]],
                                        channel_multiplier=-1))
    E("pool", lambda e: e.iota(out=ecapB[:], pattern=[[1024, 8]], base=0,
                               channel_multiplier=0,
                               allow_small_or_imprecise_dtypes=True))
    E("pool", lambda e: e.iota(out=limB[:], pattern=[[1024, 8]], base=1023,
                               channel_multiplier=0,
                               allow_small_or_imprecise_dtypes=True))
    E("pool", lambda e: e.memset(hst[:], 0.0), incs=[("sPOOL", 1)])
    consts_done = C("sPOOL")

    E("act", lambda e: e.activation(out=arow[:], in_=alog_r[:], func=AF.Exp),
      waits=[("qSY", small_in)], incs=[("sACT", 1)])
    E("dve", lambda e: e.tensor_scalar_mul(out=arow[:], in0=arow[:], scalar1=-1.0),
      waits=[("sACT", C("sACT"))], incs=[("sDVE", 1)])
    arow_done = C("sDVE")
    E("pe", lambda e: e.matmul(out=psC[:, 0:NS], lhsT=ones1f[:], rhs=bdt_r[:],
                               start=True, stop=True),
      waits=[("qSY", small_in), ("sPOOL", consts_done)], incs=[("sPE", 1)])
    E("pe", lambda e: e.matmul(out=psC[:, NS:2 * NS], lhsT=ones1f[:], rhs=arow[:],
                               start=True, stop=True),
      waits=[("sDVE", arow_done)], incs=[("sPE", 1)])
    bc_mm = C("sPE")
    E("dve", lambda e: e.tensor_copy(out=bdtB[:], in_=psC[:, 0:NS]),
      waits=[("sPE", bc_mm)], incs=[("sDVE", 1)])
    E("dve", lambda e: e.tensor_copy(out=aB[:], in_=psC[:, NS:2 * NS]),
      incs=[("sDVE", 1)])
    bcasts_done = C("sDVE")

    cast_mark = 0
    for kc in range(NT):
        sl = stage[:, (kc % 2) * 1024:(kc % 2) * 1024 + 48]
        wts = [("sPOOL", cast_mark)] if kc >= 2 else []
        E("sy", lambda e, kc=kc, sl=sl: e.dma_start(
            out=sl, in_=wxdtT_h[kc * 128:(kc + 1) * 128, :]),
          waits=wts, incs=[("qSY", 16)])
        E("pool", lambda e, kc=kc, sl=sl: e.tensor_copy(
            out=wxdtT[:, kc * 48:(kc + 1) * 48], in_=sl),
          waits=[("qSY", C("qSY"))], incs=[("sPOOL", 1)])
        cast_mark = C("sPOOL")
    for kc in range(8):
        sl = stage[:, (kc % 2) * 1024:(kc % 2) * 1024 + 8]
        E("sy", lambda e, kc=kc, sl=sl: e.dma_start(
            out=sl, in_=gwT_h[kc * 128:(kc + 1) * 128, :]),
          waits=[("sPOOL", cast_mark)], incs=[("qSY", 16)])
        E("pool", lambda e, kc=kc, sl=sl: e.tensor_copy(
            out=gwT[:, kc * 8:(kc + 1) * 8], in_=sl),
          waits=[("qSY", C("qSY"))], incs=[("sPOOL", 1)])
        cast_mark = C("sPOOL")

    if PHASES == "consts":
        return finalize()
    psT_evac = [0, 0, 0, 0]
    xn_cons = [0, 0]
    xnb_stored = 0
    for c in range(min(NCHUNK, P0_LIMIT)):
        par = c % 2
        xs = xt0[:, par * DM:(par + 1) * DM]
        wts = [("sDVE", xn_cons[par])] if c >= 2 else []
        E("sy", lambda e, c=c, xs=xs: e.dma_start(out=xs, in_=x_h[c * 128:(c + 1) * 128, :]),
          waits=wts, incs=[("qSY", 16)])
        ldc = C("qSY")
        if "l" in P0_SKIP:
            xnd = 0
        else:
         if True:
            E("dve", lambda e, xs=xs: e.tensor_reduce(out=st1[:], in_=xs, axis=AX.X,
                                                      op=ALU.add),
                  waits=[("qSY", ldc)], incs=[("sDVE", 1)])
            E("dve", lambda e: e.tensor_scalar_mul(out=st1[:], in0=st1[:], scalar1=1.0 / DM),
              waits=[("sDVE", C("sDVE"))], incs=[("sDVE", 1)])
            mdone = C("sDVE")
            E("act", lambda e, xs=xs: e.activation(out=psC[:, 0:512],
                                                   in_=xs[:, 0:512], func=AF.Square,
                                                   accum_out=st2[:]),
              waits=[("qSY", ldc)], incs=[("sACT", 1)])
            E("act", lambda e, xs=xs: e.activation(out=psC[:, 0:512], in_=xs[:, 512:1024],
                                                   func=AF.Square, accum_out=st3[:]),
              waits=[("sACT", C("sACT"))], incs=[("sACT", 1)])
            sqd = C("sACT")
            E("dve", lambda e: e.tensor_add(out=st2[:], in0=st2[:], in1=st3[:]),
              waits=[("sACT", sqd), ("sDVE", mdone)], incs=[("sDVE", 1)])
            E("dve", lambda e: e.tensor_mul(out=st3[:], in0=st1[:], in1=st1[:]),
              waits=[("sDVE", C("sDVE"))], incs=[("sDVE", 1)])
            E("dve", lambda e: e.tensor_scalar(out=st3[:], in0=st3[:], scalar1=-1.0,
                                               scalar2=1e-5, op0=ALU.mult, op1=ALU.add),
              waits=[("sDVE", C("sDVE"))], incs=[("sDVE", 1)])
            vb = C("sDVE")
            E("act", lambda e: e.activation(out=st3[:], in_=st2[:], func=AF.Ln,
                                            bias=st3[:], scale=1.0 / DM),
              waits=[("sDVE", vb)], incs=[("sACT", 1)])
            E("act", lambda e: e.activation(out=st2[:], in_=st3[:], func=AF.Exp, scale=-0.5),
              waits=[("sACT", C("sACT"))], incs=[("sACT", 1)])
            rstd = C("sACT")
            aff = C("sDVE")
            E("dve", lambda e, xs=xs: e.tensor_scalar(out=xnb[:], in0=xs, scalar1=st1[:],
                                                      scalar2=st2[:], op0=ALU.subtract,
                                                      op1=ALU.mult),
              waits=[("sDVE", aff), ("sACT", rstd)], incs=[("sDVE", 1)])
            xnd = C("sDVE")
        if "t" in P0_SKIP:
            xn_cons[par] = xnd if isinstance(xnd, int) else 0
            continue
        for g in range(2):
            for sl4 in range(4):
                kc = g * 4 + sl4
                tdst = psT[:, sl4 * 128:(sl4 + 1) * 128]
                E("pe", lambda e, kc=kc, tdst=tdst: e.transpose(
                    out=tdst, in_=xnb[:, kc * 128:(kc + 1) * 128], identity=identb[:]),
                  waits=[("sDVE", xnd), ("sDVE", psT_evac[0]),
                         ("sPOOL", consts_done)],
                  incs=[("sPE", 1)])
            td = C("sPE")
            for sl4 in range(4):
                kc = g * 4 + sl4
                tdst = psT[:, sl4 * 128:(sl4 + 1) * 128]
                E("dve", lambda e, kc=kc, c=c, tdst=tdst: e.tensor_copy(
                    out=xnT[:, kc * L + c * 128:kc * L + (c + 1) * 128], in_=tdst),
                  waits=[("sPE", td)], incs=[("sDVE", 1)])
            psT_evac[0] = C("sDVE")
        xn_cons[par] = xnd

    if PHASES == "p0":
        return finalize()
    b.barrier()

    slot_cast = [0, 0]
    idx = 0
    for kc in range(8):
        for hh in range(2):
            par = idx % 2
            sl = stage[:, par * 1024:(par + 1) * 1024]
            wts = [("sPOOL", slot_cast[par])] if slot_cast[par] else []
            E("sy", lambda e, kc=kc, hh=hh, sl=sl: e.dma_start(
                out=sl, in_=winT_h[kc * 128:(kc + 1) * 128, hh * 1024:(hh + 1) * 1024]),
              waits=wts, incs=[("qSY", 16)])
            E("pool", lambda e, kc=kc, hh=hh, sl=sl: e.tensor_copy(
                out=WA[:, kc * DI + hh * 1024:kc * DI + (hh + 1) * 1024], in_=sl),
              waits=[("qSY", C("qSY"))], incs=[("sPOOL", 1)])
            slot_cast[par] = C("sPOOL")
            idx += 1
    winxi_done = C("sPOOL")

    E("pool", lambda e: e.memset(tails[:], 0.0), incs=[("sPOOL", 1)])
    tails_zeroed = C("sPOOL")
    utok_stored = [0] * NCHUNK
    ufm_stored = [0] * NTW
    bc_stored = [0] * NTW
    psA_free = [0, 0]
    tile_free = [0, 0]
    ufm_cons_pe = 0
    ufm_cons_pl = 0
    utokb_free = [0, 0]
    for T in range(NTW):
        silu_marks = [0] * NT
        tapend_marks = [0] * NT
        for ft in range(NT):
            par = ft % 2
            tb = tileb[:, par * (TW + 3):(par + 1) * (TW + 3)]
            psl = psA[:, par * 512:(par + 1) * 512]
            for kc in range(8):
                wts = []
                if kc == 0:
                    wts = [("sPOOL", winxi_done), ("sDVE", psA_free[par]),
                           ("bsy", C("bsy"))]
                E("pe", lambda e, T=T, ft=ft, kc=kc, psl=psl: e.matmul(
                    out=psl,
                    lhsT=WA[:, kc * DI + ft * 128:kc * DI + (ft + 1) * 128],
                    rhs=xnT[:, kc * L + T * TW:kc * L + (T + 1) * TW],
                    start=(kc == 0), stop=(kc == 7)),
                  waits=wts, incs=([("sPE", 1)] if kc == 7 else []))
            mmd = C("sPE")
            E("dve", lambda e, ft=ft, tb=tb: e.tensor_copy(
                out=tb[:, 0:3], in_=tails[:, ft * 3:(ft + 1) * 3]),
              waits=[("sACT", tile_free[par]), ("sPOOL", tails_zeroed)],
              incs=[("sDVE", 1)])
            E("dve", lambda e, tb=tb, psl=psl: e.tensor_copy(out=tb[:, 3:3 + TW], in_=psl),
              waits=[("sPE", mmd)], incs=[("sDVE", 1)])
            psA_free[par] = C("sDVE")
            E("dve", lambda e, ft=ft, tb=tb: e.tensor_copy(
                out=tails[:, ft * 3:(ft + 1) * 3], in_=tb[:, TW:TW + 3]),
              waits=[("sDVE", C("sDVE"))], incs=[("sDVE", 1)])
            ca = convacc[:, par * TW:(par + 1) * TW]
            E("dve", lambda e, ft=ft, tb=tb, ca=ca: e.tensor_scalar_mul(
                out=ca, in0=tb[:, 0:TW], scalar1=convw[:, ft * 4:ft * 4 + 1]),
              waits=[("sDVE", C("sDVE")), ("sACT", tile_free[par])], incs=[("sDVE", 1)])
            for k in range(1, 4):
                E("dve", lambda e, ft=ft, k=k, tb=tb, ca=ca: e.scalar_tensor_tensor(
                    out=ca, in0=tb[:, k:k + TW],
                    scalar=convw[:, ft * 4 + k:ft * 4 + k + 1],
                    in1=ca, op0=ALU.mult, op1=ALU.add),
                  waits=[("sDVE", C("sDVE"))], incs=[("sDVE", 1)])
            tapd = C("sDVE")
            tapend_marks[ft] = tapd
            E("act", lambda e, ft=ft, ca=ca: e.activation(
                out=ufm16[:, ft * TW:(ft + 1) * TW], in_=ca, func=AF.Silu,
                bias=convb[:, ft:ft + 1]),
              waits=[("sDVE", tapd), ("sPE", ufm_cons_pe), ("qPL", ufm_cons_pl)],
              incs=[("sACT", 1)])
            silu_marks[ft] = C("sACT")
            tile_free[par] = C("sACT")
        for ft in range(NT):
            E("pe", lambda e, ft=ft: e.matmul(
                out=psC[0:48, 0:512],
                lhsT=wxdtT[:, ft * 48:(ft + 1) * 48],
                rhs=ufm16[:, ft * TW:(ft + 1) * TW],
                start=(ft == 0), stop=(ft == NT - 1)),
              waits=[("sACT", silu_marks[ft])] +
                    ([("sDVE", C("sDVE"))] if ft == 0 else []),
              incs=([("sPE", 1)] if ft == NT - 1 else []))
        bcmm = C("sPE")
        E("dve", lambda e: e.tensor_copy(out=bcev[:], in_=psC[0:48, 0:512]),
          waits=[("sPE", bcmm)], incs=[("sDVE", 1)])
        bev = C("sDVE")
        E("pool", lambda e, T=T: e.dma_start(out=bcH[:, T * TW:(T + 1) * TW], in_=bcev[:]),
          waits=[("sDVE", bev)], incs=[("qPL", 16)])
        bc_stored[T] = C("qPL")
        for ft in range(NT):
            E("pool", lambda e, T=T, ft=ft: e.dma_start(
                out=ufmH[T * 4:(T + 1) * 4, :, ft, :].rearrange("c p t -> p c t"),
                in_=ufm16[:, ft * TW:(ft + 1) * TW].rearrange("p (c t) -> p c t", c=4)),
              waits=[("sACT", silu_marks[ft])], incs=[("qPL", 16)])
        ufm_stored[T] = C("qPL")
        for tc in range(4):
            c = T * 4 + tc
            spar = tc % 2
            ub = utokb[:, spar * DI:(spar + 1) * DI]
            for g in range(4):
                for sl4 in range(4):
                    ft = g * 4 + sl4
                    tdst = psT[:, sl4 * 128:(sl4 + 1) * 128]
                    E("pe", lambda e, ft=ft, tc=tc, tdst=tdst: e.transpose(
                        out=tdst,
                        in_=ufm16[:, ft * TW + tc * 128:ft * TW + (tc + 1) * 128],
                        identity=identb[:]),
                      waits=[("sACT", silu_marks[ft]), ("sDVE", psT_evac[0])],
                      incs=[("sPE", 1)])
                td = C("sPE")
                for sl4 in range(4):
                    ft = g * 4 + sl4
                    tdst = psT[:, sl4 * 128:(sl4 + 1) * 128]
                    E("dve", lambda e, ft=ft, ub=ub, tdst=tdst: e.tensor_copy(
                        out=ub[:, ft * 128:(ft + 1) * 128], in_=tdst),
                      waits=[("sPE", td), ("qPL", utokb_free[spar])],
                      incs=[("sDVE", 1)])
                psT_evac[0] = C("sDVE")
            utd = C("sDVE")
            E("pool", lambda e, c=c, ub=ub: e.dma_start(out=utokH[c], in_=ub),
              waits=[("sDVE", utd)], incs=[("qPL", 16)])
            utok_stored[c] = C("qPL")
            utokb_free[spar] = C("qPL")
        ufm_cons_pe = C("sPE")
        ufm_cons_pl = C("qPL")

    if PHASES == "p2":
        return finalize()
    b.barrier()

    slot_cast = [C("sPOOL"), C("sPOOL")]
    idx = 0
    for kc in range(8):
        for hh in range(2):
            par = idx % 2
            sl = stage[:, par * 1024:(par + 1) * 1024]
            E("sy", lambda e, kc=kc, hh=hh, sl=sl: e.dma_start(
                out=sl, in_=winT_h[kc * 128:(kc + 1) * 128,
                                   DI + hh * 1024:DI + (hh + 1) * 1024]),
              waits=[("sPOOL", slot_cast[par])], incs=[("qSY", 16)])
            E("pool", lambda e, kc=kc, hh=hh, sl=sl: e.tensor_copy(
                out=WB[:, kc * DI + hh * 1024:kc * DI + (hh + 1) * 1024], in_=sl),
              waits=[("qSY", C("qSY"))], incs=[("sPOOL", 1)])
            slot_cast[par] = C("sPOOL")
            idx += 1
    winz_done = C("sPOOL")

    if PHASES == "pg":
        return finalize()
    b.barrier()

    def emit_zhalf(hg, szpar, after_psB_free_dve, szf_war_pool=0):
        prev_silu = [0, 0]
        for rnd in range(4):
            base = (rnd % 2) * 1024
            for fti in range(4):
                ft = rnd * 4 + fti
                for kc in range(8):
                    wts = []
                    if fti == 0 and kc == 0:
                        wts = [("sPOOL", winz_done), ("sDVE", after_psB_free_dve),
                               ("sACT", prev_silu[rnd % 2])]
                    E("pe", lambda e, hg=hg, ft=ft, fti=fti, kc=kc, base=base: e.matmul(
                        out=psB[:, base + fti * 256:base + (fti + 1) * 256],
                        lhsT=WB[:, kc * DI + ft * 128:kc * DI + (ft + 1) * 128],
                        rhs=xnT[:, kc * L + hg * 256:kc * L + (hg + 1) * 256],
                        start=(kc == 0), stop=(kc == 7)),
                      waits=wts, incs=([("sPE", 1)] if kc == 7 else []))
            zmm = C("sPE")
            for fti in range(4):
                ft = rnd * 4 + fti
                E("act", lambda e, ft=ft, fti=fti, szpar=szpar, base=base: e.activation(
                    out=szf[:, szpar * NT * 256 + ft * 256:
                            szpar * NT * 256 + (ft + 1) * 256],
                    in_=psB[:, base + fti * 256:base + (fti + 1) * 256], func=AF.Silu),
                  waits=[("sPE", zmm), ("sPOOL", szf_war_pool)], incs=[("sACT", 1)])
            prev_silu[rnd % 2] = C("sACT")
        return C("sACT")

    sz_ready = [0] * NHG
    sz_ready[0] = emit_zhalf(0, 0, 0)
    szf_cons = [0, 0]

    state = {
        "utok_cons": 0, "ufm_cons": 0, "bc_cons": [0, 0],
        "flt_cons": 0, "fltc_cons": 0, "yzb_stored": 0,
        "psB_evac_dve": 0, "psB_z_done": C("sACT"),
        "prev_h_done": 0, "prev_X1_done": 0, "psA_free": 0,
    }
    SM = {}

    def emit_stats(c):
        par = c % 2
        E("sy", lambda e, c=c, par=par: e.dma_start(
            out=bc4[:, par * Q:(par + 1) * Q], in_=bcH[:, c * Q:(c + 1) * Q]),
          waits=[("qPL", bc_stored[c // 4]), ("sPE", state["bc_cons"][par])],
          incs=[("qSY", 16)])
        bc_ld = C("qSY")
        prev_psc_free = SM[c - 1]["ctT"] if (c - 1) in SM else 0
        E("pe", lambda e, par=par: e.transpose(
            out=psC[:, 304:352], in_=bc4[:, par * Q:(par + 1) * Q],
            identity=identf[0:48, 0:48]),
          waits=[("qSY", bc_ld), ("sDVE", prev_psc_free)], incs=[("sPE", 1)])
        bct = C("sPE")
        state["bc_cons"][par] = bct
        E("dve", lambda e: e.tensor_add(out=sm_a[:], in0=psC[:, 304 + 32:304 + 48],
                                        in1=bdtB[:]),
          waits=[("sPE", bct), ("sDVE", bcasts_done)], incs=[("sDVE", 1)])
        E("dve", lambda e: e.tensor_copy(out=sm_b[:], in_=psC[:, 304:304 + 16]),
          waits=[("sDVE", C("sDVE"))], incs=[("sDVE", 1)])
        E("dve", lambda e: e.tensor_copy(out=sm_cbf[:], in_=psC[:, 304 + 16:304 + 32]),
          waits=[("sDVE", C("sDVE"))], incs=[("sDVE", 1)])
        smalls0 = C("sDVE")
        E("dve", lambda e: e.tensor_scalar_max(out=sm_dlt[:], in0=sm_a[:], scalar1=0.0),
          waits=[("sDVE", smalls0)], incs=[("sDVE", 1)])
        r0 = C("sDVE")
        E("act", lambda e: e.activation(out=sm_lnb[:], in_=sm_a[:], func=AF.Abs),
          waits=[("sDVE", smalls0)], incs=[("sACT", 1)])
        E("act", lambda e: e.activation(out=sm_lnb[:], in_=sm_lnb[:], func=AF.Exp,
                                        scale=-1.0),
          waits=[("sACT", C("sACT"))], incs=[("sACT", 1)])
        E("act", lambda e: e.activation(out=sm_lnb[:], in_=sm_lnb[:], func=AF.Ln,
                                        bias=1.0),
          waits=[("sACT", C("sACT"))], incs=[("sACT", 1)])
        spl = C("sACT")
        E("dve", lambda e: e.tensor_add(out=sm_dlt[:], in0=sm_dlt[:], in1=sm_lnb[:]),
          waits=[("sACT", spl), ("sDVE", r0)], incs=[("sDVE", 1)])
        dlt_done = C("sDVE")
        E("dve", lambda e: e.tensor_mul(out=sm_a[:], in0=sm_dlt[:], in1=aB[:]),
          waits=[("sDVE", dlt_done)], incs=[("sDVE", 1)])
        dal_done = C("sDVE")
        E("dve", lambda e: e.tensor_mul(out=sm_dB[:], in0=sm_dlt[:], in1=sm_b[:]),
          waits=[("sDVE", dlt_done)], incs=[("sDVE", 1)])
        db_done = C("sDVE")
        E("act", lambda e: e.activation(out=sm_lnb[:], in_=sm_dB[:], func=AF.Abs),
          waits=[("sDVE", db_done)], incs=[("sACT", 1)])
        E("act", lambda e: e.activation(out=sm_lnb[:], in_=sm_lnb[:], func=AF.Ln),
          waits=[("sACT", C("sACT"))], incs=[("sACT", 1)])
        lnb_done = C("sACT")
        E("act", lambda e, par=par: e.activation(
            out=sm_sgn[:, par * NS:(par + 1) * NS], in_=sm_dB[:], func=AF.Sign),
          waits=[("sDVE", db_done)], incs=[("sACT", 1)])
        sgn_done = C("sACT")
        E("pe", lambda e: e.matmul(out=psC[:, 0:16], lhsT=triu[:], rhs=sm_a[:],
                                   start=True, stop=True),
          waits=[("sDVE", dal_done)], incs=[("sPE", 1)])
        E("pe", lambda e: e.matmul(out=psC[0:16, 16:144], lhsT=sm_a[:], rhs=triu[:],
                                   start=True, stop=True),
          waits=[("sPE", C("sPE"))], incs=[("sPE", 1)])
        E("pe", lambda e: e.matmul(out=psC[0:16, 144:145], lhsT=sm_a[:], rhs=onescol[:],
                                   start=True, stop=True),
          waits=[("sPE", C("sPE"))], incs=[("sPE", 1)])
        E("pe", lambda e: e.matmul(out=psC[0:1, 145:161], lhsT=onescol[:], rhs=sm_a[:],
                                   start=True, stop=True),
          waits=[("sPE", C("sPE"))], incs=[("sPE", 1)])
        cum_mm = C("sPE")
        E("dve", lambda e: e.tensor_copy(out=sm_cum[:], in_=psC[:, 0:16]),
          waits=[("sPE", cum_mm)], incs=[("sDVE", 1)])
        E("dve", lambda e: e.tensor_copy(out=cumT_e[:], in_=psC[0:16, 16:144]),
          waits=[("sDVE", C("sDVE"))], incs=[("sDVE", 1)])
        E("dve", lambda e: e.tensor_copy(out=clF[:], in_=psC[0:1, 145:161]),
          waits=[("sDVE", C("sDVE"))], incs=[("sDVE", 1)])
        cum_ev = C("sDVE")
        E("act", lambda e, par=par: e.activation(
            out=dec_c[:, par:par + 1], in_=psC[0:16, 144:145], func=AF.Exp),
          waits=[("sPE", cum_mm)], incs=[("sACT", 1)])
        dec_done = C("sACT")
        E("pe", lambda e: e.matmul(out=psC[:, 161:177], lhsT=ones1f[:], rhs=clF[:],
                                   start=True, stop=True),
          waits=[("sDVE", cum_ev), ("sACT", dec_done)], incs=[("sPE", 1)])
        bclmm = C("sPE")
        E("dve", lambda e: e.tensor_sub(out=sm_b[:], in0=psC[:, 161:177], in1=sm_cum[:]),
          waits=[("sPE", bclmm), ("sDVE", cum_ev)], incs=[("sDVE", 1)])
        bed = C("sDVE")
        E("act", lambda e: e.activation(out=sm_b[:], in_=sm_b[:], func=AF.Exp),
          waits=[("sDVE", bed)], incs=[("sACT", 1)])
        be2 = C("sACT")
        E("dve", lambda e, par=par: e.tensor_mul(
            out=btok[:, par * NS:(par + 1) * NS], in0=sm_dB[:], in1=sm_b[:]),
          waits=[("sACT", be2), ("sDVE", db_done)], incs=[("sDVE", 1)])
        btok_done = C("sDVE")
        E("act", lambda e: e.activation(out=sm_a[:], in_=sm_cum[:], func=AF.Exp),
          waits=[("sDVE", cum_ev)], incs=[("sACT", 1)])
        ecd = C("sACT")
        E("dve", lambda e: e.tensor_mul(out=sm_ctl[:], in0=sm_cbf[:], in1=sm_a[:]),
          waits=[("sACT", ecd)], incs=[("sDVE", 1)])
        ctl = C("sDVE")
        E("pe", lambda e: e.transpose(out=psC[0:16, 177:305], in_=sm_ctl[:],
                                      identity=identf[:]),
          waits=[("sDVE", ctl), ("sDVE", bed), ("sACT", dec_done)],
          incs=[("sPE", 1)])
        ctTm = C("sPE")
        E("dve", lambda e, par=par: e.tensor_copy(
            out=ctT[:, par * Q:(par + 1) * Q], in_=psC[0:16, 177:305]),
          waits=[("sPE", ctTm)], incs=[("sDVE", 1)])
        ctT_done = C("sDVE")
        E("pe", lambda e: e.transpose(out=psT[0:16, 0:128], in_=sm_cbf[:],
                                      identity=identb[:]),
          waits=[("sDVE", smalls0), ("sDVE", psT_evac[0])], incs=[("sPE", 1)])
        cTm = C("sPE")
        E("dve", lambda e: e.tensor_copy(out=cb16[:], in_=psT[0:16, 0:128]),
          waits=[("sPE", cTm)], incs=[("sDVE", 1)])
        psT_evac[0] = C("sDVE")
        cb_done = C("sDVE")
        E("sy", lambda e: e.dma_start(out=fcumH, in_=cumT_e[:]),
          waits=[("sDVE", cum_ev), ("sPE", state["flt_cons"])], incs=[("qSY", 16)])
        E("sy", lambda e: e.dma_start(
            out=flt[:], in_=fcumH.rearrange("n t -> (n t)")[None, :]),
          waits=[("qSY", C("qSY"))], incs=[("qSY", 16)])
        flt_ld = C("qSY")
        E("sy", lambda e: e.dma_start(out=fcH, in_=cb16[:]),
          waits=[("sDVE", cb_done), ("sPE", state["fltc_cons"])], incs=[("qSY", 16)])
        E("sy", lambda e: e.dma_start(
            out=fltc[:], in_=fcH.rearrange("n t -> (n t)")[None, :]),
          waits=[("qSY", C("qSY"))], incs=[("qSY", 16)])
        fltc_ld = C("qSY")
        E("dve", lambda e, par=par: e.tensor_sub(
            out=sm_mbias[:, par * NS:(par + 1) * NS], in0=sm_lnb[:], in1=sm_cum[:]),
          waits=[("sACT", lnb_done), ("sDVE", cum_ev)], incs=[("sDVE", 1)])
        mbias_done = C("sDVE")
        SM[c] = dict(flt_ld=flt_ld, fltc_ld=fltc_ld, mbias=mbias_done,
                     sgn=sgn_done, ctT=ctT_done, btok=btok_done, dec=dec_done)

    emit_stats(0)
    for c in range(NCHUNK):
        hg = c // 2
        szpar = hg % 2
        par = c % 2
        st_m = SM[c]
        E("sy", lambda e, c=c: e.dma_start(out=ut4[:], in_=utokH[c]),
          waits=[("qPL", utok_stored[c]), ("sPE", state["utok_cons"])],
          incs=[("qSY", 16)])
        ut_ld = C("qSY")
        E("sy", lambda e, c=c: e.dma_start(
            out=uf4[:], in_=ufmH[c].rearrange("p et t -> p (et t)")),
          waits=[("qPL", ufm_stored[c // 4]), ("sDVE", state["ufm_cons"])],
          incs=[("qSY", 16)])
        uf_ld = C("qSY")

        for et in range(NT):
            wts = [("sDVE", st_m["ctT"]), ("sDVE", state["prev_h_done"])]
            if et == 0:
                wts.append(("sACT", state["psB_z_done"]))
                wts.append(("sDVE", state["psB_evac_dve"]))
            E("pe", lambda e, et=et, par=par: e.matmul(
                out=psB[:, et * 128:(et + 1) * 128],
                lhsT=hst[:, et * 128:(et + 1) * 128],
                rhs=ctT[:, par * Q:(par + 1) * Q],
                start=(et % 4 == 0), stop=False),
              waits=wts, incs=[("sPE", 1)])
        y2done = C("sPE")

        for hh in range(2):
            for jj in range(2):
                wts = [("qSY", st_m["flt_ld"])]
                if hh == 0 and jj == 0:
                    wts.append(("sDVE", state["psA_free"]))
                if hh == 1 and jj == 0:
                    wts.append(("sDVE", C("sDVE")))
                E("pe", lambda e, hh=hh, jj=jj: e.matmul(
                    out=psA[:, jj * 512:(jj + 1) * 512], lhsT=ones1f[:],
                    rhs=flt[0:1, hh * 1024 + jj * 512:hh * 1024 + (jj + 1) * 512],
                    start=True, stop=True),
                  waits=wts, incs=[("sPE", 1)])
            bctm = C("sPE")
            state["flt_cons"] = bctm
            E("dve", lambda e, hh=hh: e.tensor_add(
                out=mb1[:, hh * 1024:(hh + 1) * 1024].rearrange("p (n t) -> p n t", n=8),
                in0=psA[:, 0:1024].rearrange("p (n t) -> p n t", n=8),
                in1=mneg[:, None, :].broadcast_to([128, 8, 128])),
              waits=[("sPE", bctm)], incs=[("sDVE", 1)])
            tmpd = C("sDVE")
            for ni in range(8):
                n = hh * 8 + ni
                E("act", lambda e, n=n, par=par: e.activation(
                    out=mb2[:, n * 128:(n + 1) * 128],
                    in_=mb1[:, n * 128:(n + 1) * 128], func=AF.Exp,
                    bias=sm_mbias[:, par * NS + n:par * NS + n + 1]),
                  waits=[("sDVE", tmpd), ("sDVE", st_m["mbias"])], incs=[("sACT", 1)])
            expa = C("sACT")
            E("pool", lambda e, hh=hh, par=par: e.tensor_mul(
                out=mb1[:, hh * 1024:(hh + 1) * 1024].rearrange(
                    "p (n t) -> p n t", n=8),
                in0=mb2[:, hh * 1024:(hh + 1) * 1024].rearrange(
                    "p (n t) -> p n t", n=8),
                in1=sm_sgn[:, par * NS + hh * 8:par * NS + (hh + 1) * 8][:, :, None
                    ].broadcast_to([128, 8, 128])),
              waits=[("sACT", expa), ("sACT", st_m["sgn"])], incs=[("sPOOL", 1)])
            expd = C("sPOOL")
            for jj in range(2):
                E("pe", lambda e, hh=hh, jj=jj: e.matmul(
                    out=psA[:, jj * 512:(jj + 1) * 512], lhsT=ones1b[:],
                    rhs=fltc[0:1, hh * 1024 + jj * 512:hh * 1024 + (jj + 1) * 512],
                    start=True, stop=True),
                  waits=([("qSY", st_m["fltc_ld"]), ("sDVE", tmpd)]
                         if jj == 0 else []),
                  incs=[("sPE", 1)])
            bccm = C("sPE")
            state["fltc_cons"] = bccm
            E("dve", lambda e, hh=hh: e.tensor_mul(
                out=mb2[:, hh * 1024:(hh + 1) * 1024],
                in0=mb1[:, hh * 1024:(hh + 1) * 1024], in1=psA[:, 0:1024]),
              waits=[("sPE", bccm), ("sPOOL", expd)], incs=[("sDVE", 1)])
        X1_done = C("sDVE")
        state["prev_X1_done"] = X1_done
        E("dve", lambda e: e.tensor_reduce(
            out=msb[:], in_=mb2[:].rearrange("p (n t) -> p t n", n=16),
            axis=AX.X, op=ALU.add),
          waits=[("sDVE", X1_done)], incs=[("sDVE", 1)])
        E("dve", lambda e: e.tensor_copy(out=mbf[:], in_=msb[:]),
          waits=[("sDVE", C("sDVE"))], incs=[("sDVE", 1)])
        mbf_done = C("sDVE")

        if c + 1 < NCHUNK:
            emit_stats(c + 1)

        for et in range(NT):
            E("pe", lambda e, et=et: e.matmul(
                out=psB[:, et * 128:(et + 1) * 128],
                lhsT=ut4[:, et * 128:(et + 1) * 128], rhs=mbf[:],
                start=False, stop=True),
              waits=[("sDVE", mbf_done), ("qSY", ut_ld), ("sPE", y2done)],
              incs=[("sPE", 1)])
        y1done = C("sPE")

        hmarks = []
        for rr in range(4):
            base = (rr % 2) * 512
            wts = [("qSY", ut_ld), ("sDVE", st_m["btok"]), ("sDVE", X1_done)]
            if rr >= 2:
                wts.append(("sDVE", hmarks[rr - 2]))
            E("pe", lambda e, rr=rr, base=base, par=par: e.matmul(
                out=psA[0:16, base:base + 512], lhsT=btok[:, par * NS:(par + 1) * NS],
                rhs=ut4[:, rr * 512:(rr + 1) * 512], start=True, stop=True),
              waits=wts, incs=[("sPE", 1)])
            dhm = C("sPE")
            E("dve", lambda e, rr=rr, base=base, par=par: e.scalar_tensor_tensor(
                out=hst[:, rr * 512:(rr + 1) * 512],
                in0=hst[:, rr * 512:(rr + 1) * 512], scalar=dec_c[:, par:par + 1],
                in1=psA[0:16, base:base + 512], op0=ALU.mult, op1=ALU.add),
              waits=[("sPE", dhm), ("sACT", st_m["dec"]), ("sPE", y2done)],
              incs=[("sDVE", 1)])
            hmarks.append(C("sDVE"))
        h_done = C("sDVE")
        state["prev_h_done"] = h_done
        state["psA_free"] = h_done
        utok_cons_mark = C("sPE")
        state["utok_cons"] = utok_cons_mark

        for et in range(NT):
            E("dve", lambda e, et=et: e.scalar_tensor_tensor(
                out=mb1[:, et * 128:(et + 1) * 128],
                in0=uf4[:, et * 128:(et + 1) * 128], scalar=dfm[:, et:et + 1],
                in1=psB[:, et * 128:(et + 1) * 128], op0=ALU.mult, op1=ALU.add),
              waits=[("sPE", y1done), ("qSY", uf_ld), ("sPOOL", expd)],
              incs=[("sDVE", 1)])
            dd = C("sDVE")
            E("pool", lambda e, et=et, c=c, szpar=szpar: e.tensor_mul(
                out=yzb[:, et * 128:(et + 1) * 128],
                in0=mb1[:, et * 128:(et + 1) * 128],
                in1=szf[:, szpar * NT * 256 + et * 256 + (c % 2) * 128:
                        szpar * NT * 256 + et * 256 + (c % 2) * 128 + 128]),
              waits=[("sDVE", dd), ("sACT", sz_ready[hg]),
                     ("qSY", state["yzb_stored"])],
              incs=[("sPOOL", 1)])
        psB_evac = C("sDVE")
        state["psB_evac_dve"] = psB_evac
        yz_done = C("sPOOL")
        szf_cons[szpar] = yz_done
        state["ufm_cons"] = psB_evac
        E("sy", lambda e, c=c: e.dma_start(out=yzH[c], in_=yzb[:]),
          waits=[("sPOOL", yz_done)], incs=[("qSY", 16)])
        state["yzb_stored"] = C("qSY")

        if c % 2 == 1 and hg + 1 < NHG:
            nxt = 1 - szpar
            sz_ready[hg + 1] = emit_zhalf(hg + 1, nxt, psB_evac, szf_cons[nxt])
            state["psB_z_done"] = sz_ready[hg + 1]
        else:
            state["psB_z_done"] = C("sACT")

    if PHASES == "p4":
        return finalize()
    b.barrier()

    CAP = 1024
    xsb = at("xsb", 10240, [128, 2 * DM], BF16)
    xsTb = at("xsTb", 14336, [128, 2 * DM], BF16)
    xnb5 = at("xnb5", 18432, [128, 2 * DM], BF16)
    cnt8s = at("cnt8s", 22848, [8, NCHUNK], F32)
    z832 = at("z832", 23040, [8, NCHUNK], F32)
    bceF = at("bceF", 23296, [1, 8 * NCHUNK], F32)
    bce32 = at("bce32", 24256, [NCHUNK, 8], F32)
    part1 = at("part1", 24384, [128, NCHUNK], F32)
    ohall = at("ohall", 24512, [128, NCHUNK * 8], F32)

    psc_read = 0
    for c in range(NCHUNK):
        for kc in range(8):
            wts = [("sDVE", psc_read)] if kc == 0 else []
            E("pe", lambda e, c=c, kc=kc: e.matmul(
                out=psC[:, 0:8],
                lhsT=xnT[:, kc * L + c * 128:kc * L + (c + 1) * 128],
                rhs=gwT[:, kc * 8:(kc + 1) * 8],
                start=(kc == 0), stop=(kc == 7)),
              waits=wts, incs=([("sPE", 1)] if kc == 7 else []))
        lg = C("sPE")
        E("dve", lambda e: e.reduce_max(out=rmaxb[:], in_=psC[:, 0:8], axis=AX.X),
          waits=[("sPE", lg)], incs=[("sDVE", 1)])
        rm = C("sDVE")
        E("dve", lambda e: e.tensor_single_scalar(
            out=oh8[:], in_=psC[:, 0:8], scalar=rmaxb[:], op=ALU.is_ge),
          waits=[("sDVE", rm)], incs=[("sDVE", 1)])
        E("dve", lambda e: e.memset(z8[:], 0.0),
          waits=[("sDVE", C("sDVE"))], incs=[("sDVE", 1)])
        E("dve", lambda e: e.tensor_tensor_scan(
            out=cum8[:], data0=oh8[:], data1=z8[:], initial=0.0,
            op0=ALU.add, op1=ALU.add),
          waits=[("sDVE", C("sDVE"))], incs=[("sDVE", 1)])
        E("dve", lambda e: e.tensor_single_scalar(
            out=cum8[:], in_=cum8[:], scalar=1.0, op=ALU.is_le),
          waits=[("sDVE", C("sDVE"))], incs=[("sDVE", 1)])
        E("dve", lambda e, c=c: e.tensor_mul(
            out=ohall[:, c * 8:(c + 1) * 8], in0=oh8[:], in1=cum8[:]),
          waits=[("sDVE", C("sDVE"))], incs=[("sDVE", 1)])
        ohf_done = C("sDVE")
        E("act", lambda e: e.activation(out=rwb[:], in_=rmaxb[:], func=AF.Tanh,
                                        scale=0.5),
          waits=[("sDVE", rm)], incs=[("sACT", 1)])
        E("dve", lambda e, c=c: e.tensor_scalar(
            out=rwall[:, c:c + 1], in0=rwb[:], scalar1=0.5, scalar2=0.5,
            op0=ALU.mult, op1=ALU.add),
          waits=[("sACT", C("sACT"))], incs=[("sDVE", 1)])
        E("pe", lambda e, c=c: e.matmul(out=psC[:, 8:16], lhsT=tris[:],
                                        rhs=ohall[:, c * 8:(c + 1) * 8],
                                        start=True, stop=True),
          waits=[("sDVE", ohf_done)], incs=[("sPE", 1)])
        E("pe", lambda e, c=c: e.matmul(out=psC[0:8, 24 + c:25 + c],
                                        lhsT=ohall[:, c * 8:(c + 1) * 8],
                                        rhs=onescol[:], start=True, stop=True),
          waits=[("sPE", C("sPE"))], incs=[("sPE", 1)])
        rkm = C("sPE")
        E("dve", lambda e: e.tensor_add(out=oh8[:] if False else cum8[:],
                                        in0=psC[:, 8:16], in1=ecapB[:]),
          waits=[("sPE", rkm)], incs=[("sDVE", 1)])
        E("dve", lambda e: e.tensor_tensor(out=cum8[:], in0=cum8[:], in1=limB[:],
                                           op=ALU.min),
          waits=[("sDVE", C("sDVE"))], incs=[("sDVE", 1)])
        E("dve", lambda e, c=c: e.tensor_mul(out=cum8[:], in0=cum8[:],
                                             in1=ohall[:, c * 8:(c + 1) * 8]),
          waits=[("sDVE", C("sDVE"))], incs=[("sDVE", 1)])
        E("dve", lambda e, c=c: e.tensor_reduce(
            out=part1[:, c:c + 1], in_=cum8[:], axis=AX.X, op=ALU.add),
          waits=[("sDVE", C("sDVE"))], incs=[("sDVE", 1)])
        psc_read = C("sDVE")

    E("dve", lambda e: e.tensor_copy(out=cnt8s[:], in_=psC[0:8, 24:24 + NCHUNK]),
      waits=[("sDVE", psc_read), ("sPE", C("sPE"))], incs=[("sDVE", 1)])
    E("dve", lambda e: e.memset(z832[:], 0.0),
      waits=[("sDVE", C("sDVE"))], incs=[("sDVE", 1)])
    E("dve", lambda e: e.tensor_tensor_scan(
        out=z832[:], data0=cnt8s[:], data1=z832[:], initial=0.0,
        op0=ALU.add, op1=ALU.add),
      waits=[("sDVE", C("sDVE"))], incs=[("sDVE", 1)])
    E("dve", lambda e: e.tensor_sub(out=cnt8s[:], in0=z832[:], in1=cnt8s[:]),
      waits=[("sDVE", C("sDVE"))], incs=[("sDVE", 1)])
    bce_done = C("sDVE")
    E("pe", lambda e: e.transpose(out=psC[0:NCHUNK, 72:80], in_=cnt8s[:],
                                  identity=identf[0:8, 0:8]),
      waits=[("sDVE", bce_done)], incs=[("sPE", 1)])
    bcet = C("sPE")
    E("dve", lambda e: e.tensor_copy(out=bce32[:], in_=psC[0:NCHUNK, 72:80]),
      waits=[("sPE", bcet)], incs=[("sDVE", 1)])
    bce32d = C("sDVE")
    E("pool", lambda e: e.dma_start(out=bceH, in_=bce32[:]),
      waits=[("sDVE", bce32d)], incs=[("qPL", 16)])
    E("pool", lambda e: e.dma_start(
        out=bceF[:], in_=bceH.rearrange("c e -> (c e)")[None, :]),
      waits=[("qPL", C("qPL"))], incs=[("qPL", 16)])
    bceF_ld = C("qPL")
    slot_marks = []
    for c in range(NCHUNK):
        E("pe", lambda e, c=c: e.matmul(
            out=psC[:, 64:72], lhsT=ones1f[:], rhs=bceF[0:1, c * 8:(c + 1) * 8],
            start=True, stop=True),
          waits=[("qPL", bceF_ld), ("sDVE", C("sDVE"))], incs=[("sPE", 1)])
        bcm = C("sPE")
        E("dve", lambda e, c=c: e.tensor_mul(
            out=oh8[:], in0=psC[:, 64:72], in1=ohall[:, c * 8:(c + 1) * 8]),
          waits=[("sPE", bcm)], incs=[("sDVE", 1)])
        E("dve", lambda e: e.tensor_reduce(out=rmaxb[:], in_=oh8[:], axis=AX.X,
                                           op=ALU.add),
          waits=[("sDVE", C("sDVE"))], incs=[("sDVE", 1)])
        E("dve", lambda e, c=c: e.tensor_add(out=rmaxb[:], in0=rmaxb[:],
                                             in1=part1[:, c:c + 1]),
          waits=[("sDVE", C("sDVE"))], incs=[("sDVE", 1)])
        E("dve", lambda e, c=c: e.tensor_copy(out=slotI[:, c:c + 1], in_=rmaxb[:]),
          waits=[("sDVE", C("sDVE"))], incs=[("sDVE", 1)])
        slot_marks.append(C("sDVE"))
    slots_done = C("sDVE")

    xs_scattered = 0
    xnb5_war = [0, 0]
    xsb_war = [0, 0]
    for c in range(NCHUNK):
        par = c % 2
        for g2 in range(2):
            for s4 in range(4):
                kc = g2 * 4 + s4
                E("pe", lambda e, kc=kc, c=c, s4=s4: e.transpose(
                    out=psT[:, s4 * 128:(s4 + 1) * 128],
                    in_=xnT[:, kc * L + c * 128:kc * L + (c + 1) * 128],
                    identity=identb[:]),
                  waits=[("sDVE", psT_evac[0]), ("sDVE", xnb5_war[par])],
                  incs=[("sPE", 1)])
            td = C("sPE")
            for s4 in range(4):
                kc = g2 * 4 + s4
                E("dve", lambda e, kc=kc, s4=s4, par=par: e.tensor_copy(
                    out=xnb5[:, par * DM + kc * 128:par * DM + (kc + 1) * 128],
                    in_=psT[:, s4 * 128:(s4 + 1) * 128]),
                  waits=[("sPE", td)], incs=[("sDVE", 1)])
            psT_evac[0] = C("sDVE")
        xn_tr = C("sDVE")
        E("dve", lambda e, c=c, par=par: e.tensor_scalar_mul(
            out=xsb[:, par * DM:(par + 1) * DM], in0=xnb5[:, par * DM:(par + 1) * DM],
            scalar1=rwall[:, c:c + 1]),
          waits=[("sDVE", xn_tr), ("qPL", xsb_war[par]), ("sDVE", slots_done)],
          incs=[("sDVE", 1)])
        xnb5_war[par] = C("sDVE")
        xsd = C("sDVE")
        if SKIP_SCATTER:
            E("pool", lambda e, c=c: e.dma_start(
                out=xsH[c * 128:(c + 1) * 128, :],
                in_=xsb[:, par * DM:(par + 1) * DM]),
              waits=[("sDVE", xsd)], incs=[("qPL", 16)])
        else:
            E("pool", lambda e, c=c, par=par: e.indirect_dma_start(
                out=xsH, out_offset=bass.IndirectOffsetOnAxis(slotI[:, c:c + 1], 0),
                in_=xsb[:, par * DM:(par + 1) * DM], in_offset=None),
              waits=[("sDVE", xsd)], incs=[("qPL", 16)])
        xsb_war[par] = C("qPL")
        xs_scattered = C("qPL")

    def load_expert_sp(e_idx, slot, war_pe):
        cm = [C("sPOOL")]
        slot_cast = [C("sPOOL"), C("sPOOL")]
        marks = []
        for (wsrc, dst_base, nkc, wid) in [
            (wgT_h, 0, 8, 512), (wuT_h, 4096, 8, 512), (wdT_h, 8192, 4, 1024)]:
            for kc in range(nkc):
                sl = stage[:, (kc % 2) * 1024:(kc % 2) * 1024 + wid]
                E("sy", lambda e, e_idx=e_idx, kc=kc, sl=sl, wsrc=wsrc: e.dma_start(
                    out=sl, in_=wsrc[e_idx, kc * 128:(kc + 1) * 128, :]),
                  waits=[("sPOOL", slot_cast[kc % 2])], incs=[("qSY", 16)])
                if dst_base < 8192:
                    E("pool", lambda e, kc=kc, slot=slot, sl=sl, dst_base=dst_base,
                      wid=wid: e.tensor_copy(
                        out=WA[:, slot * 8192 + dst_base + kc * wid:
                                slot * 8192 + dst_base + (kc + 1) * wid], in_=sl),
                      waits=[("qSY", C("qSY")), ("sPE", war_pe)], incs=[("sPOOL", 1)])
                else:
                    E("pool", lambda e, kc=kc, slot=slot, sl=sl, wid=wid: e.tensor_copy(
                        out=WB[:, slot * 4096 + kc * wid:slot * 4096 + (kc + 1) * wid],
                        in_=sl),
                      waits=[("qSY", C("qSY")), ("sPE", war_pe)], incs=[("sPOOL", 1)])
                cm[0] = C("sPOOL")
                slot_cast[kc % 2] = C("sPOOL")
            marks.append(cm[0])
        return marks

    ew = {}
    ew[0] = load_expert_sp(0, 0, 0)
    arena_pe = [0, 0]
    gs_stored = 0
    xsb_war5 = [0, 0]
    down_evac = 0
    for e_i in range(8):
        slot = e_i % 2
        wm = ew[e_i]
        if e_i + 1 < 8:
            ew[e_i + 1] = load_expert_sp(e_i + 1, 1 - slot, arena_pe[1 - slot])
        for r in range(8):
            par = r % 2
            E("sy", lambda e, e_i=e_i, r=r, par=par: e.dma_start(
                out=xsb[:, par * DM:(par + 1) * DM],
                in_=xsH[e_i * CAP + r * 128:e_i * CAP + (r + 1) * 128, :]),
              waits=[("qPL", xs_scattered), ("sPE", xsb_war5[par])],
              incs=[("qSY", 16)])
            xs_ld = C("qSY")
            for g in range(2):
                for s4 in range(4):
                    fc = g * 4 + s4
                    E("pe", lambda e, fc=fc, s4=s4, par=par: e.transpose(
                        out=psT[:, s4 * 128:(s4 + 1) * 128],
                        in_=xsb[:, par * DM + fc * 128:par * DM + (fc + 1) * 128],
                        identity=identb[:]),
                      waits=[("qSY", xs_ld), ("sDVE", psT_evac[0])],
                      incs=[("sPE", 1)])
                td = C("sPE")
                for s4 in range(4):
                    fc = g * 4 + s4
                    E("dve", lambda e, fc=fc, s4=s4, par=par: e.tensor_copy(
                        out=xsTb[:, par * DM + fc * 128:par * DM + (fc + 1) * 128],
                        in_=psT[:, s4 * 128:(s4 + 1) * 128]),
                      waits=[("sPE", td)], incs=[("sDVE", 1)])
                psT_evac[0] = C("sDVE")
            xsT_done = C("sDVE")
            for kc in range(8):
                wts = [("sPOOL", wm[0]), ("sDVE", xsT_done)]
                if kc == 0:
                    wts.append(("sACT", C("sACT")))
                E("pe", lambda e, kc=kc, slot=slot, par=par: e.matmul(
                    out=psB[:, 1024:1536],
                    lhsT=xsTb[:, par * DM + kc * 128:par * DM + (kc + 1) * 128],
                    rhs=WA[:, slot * 8192 + kc * 512:slot * 8192 + (kc + 1) * 512],
                    start=(kc == 0), stop=(kc == 7)),
                  waits=wts, incs=([("sPE", 1)] if kc == 7 else []))
            gmm = C("sPE")
            for kc in range(8):
                wts = [("sPOOL", wm[1]), ("sDVE", xsT_done)]
                if kc == 0:
                    wts.append(("sDVE", C("sDVE")))
                E("pe", lambda e, kc=kc, slot=slot, par=par: e.matmul(
                    out=psB[:, 1536:2048],
                    lhsT=xsTb[:, par * DM + kc * 128:par * DM + (kc + 1) * 128],
                    rhs=WA[:, slot * 8192 + 4096 + kc * 512:
                            slot * 8192 + 4096 + (kc + 1) * 512],
                    start=(kc == 0), stop=(kc == 7)),
                  waits=wts, incs=([("sPE", 1)] if kc == 7 else []))
            umm = C("sPE")
            xsb_war5[par] = umm
            E("act", lambda e: e.activation(out=sgb[:], in_=psB[:, 1024:1536],
                                            func=AF.Silu),
              waits=[("sPE", gmm)], incs=[("sACT", 1)])
            sgd = C("sACT")
            E("act", lambda e: e.activation(out=upb[:], in_=psB[:, 1536:2048],
                                            func=AF.Copy),
              waits=[("sPE", umm)], incs=[("sACT", 1)])
            upd = C("sACT")
            E("dve", lambda e: e.tensor_mul(out=pbf_[:], in0=sgb[:], in1=upb[:]),
              waits=[("sACT", sgd), ("sACT", upd), ("sPE", C("sPE"))],
              incs=[("sDVE", 1)])
            pd = C("sDVE")
            for tq in range(4):
                E("pe", lambda e, tq=tq: e.transpose(
                    out=psT[:, tq * 128:(tq + 1) * 128],
                    in_=pbf_[:, tq * 128:(tq + 1) * 128], identity=identb[:]),
                  waits=[("sDVE", pd), ("sDVE", psT_evac[0])], incs=[("sPE", 1)])
            td = C("sPE")
            for tq in range(4):
                E("dve", lambda e, tq=tq: e.tensor_copy(
                    out=pTb[:, tq * 128:(tq + 1) * 128],
                    in_=psT[:, tq * 128:(tq + 1) * 128]),
                  waits=[("sPE", td)], incs=[("sDVE", 1)])
            psT_evac[0] = C("sDVE")
            ptd = C("sDVE")
            for kd in range(4):
                for jj in range(2):
                    wts = [("sDVE", ptd), ("sPOOL", wm[2])]
                    if kd == 0 and jj == 0:
                        wts.append(("sACT", down_evac))
                    E("pe", lambda e, kd=kd, jj=jj, slot=slot: e.matmul(
                        out=psB[:, jj * 512:(jj + 1) * 512],
                        lhsT=pTb[:, kd * 128:(kd + 1) * 128],
                        rhs=WB[:, slot * 4096 + kd * 1024 + jj * 512:
                                slot * 4096 + kd * 1024 + (jj + 1) * 512],
                        start=(kd == 0), stop=(kd == 3)),
                      waits=wts, incs=([("sPE", 1)] if (kd == 3 and jj == 1) else []))
            dmm = C("sPE")
            arena_pe[slot] = dmm
            E("act", lambda e: e.activation(out=ffnacc[:].bitcast(BF16)[:, 0:DM],
                                            in_=psB[:, 0:1024], func=AF.Silu),
              waits=[("sPE", dmm), ("qSC", gs_stored)], incs=[("sACT", 1)])
            gsa = C("sACT")
            down_evac = gsa
            E("act", lambda e, e_i=e_i, r=r: e.dma_start(
                out=gsH[e_i * CAP + r * 128:e_i * CAP + (r + 1) * 128, :],
                in_=ffnacc[:].bitcast(BF16)[:, 0:DM]),
              waits=[("sACT", gsa)], incs=[("qSC", 16)])
            gs_stored = C("qSC")
    gs_all = gs_stored
    if PHASES == "pb" and DBG_SCRATCH:
        E("dve", lambda e: e.tensor_copy(out=sgb[:, 0:32], in_=slotI[:]),
          waits=[("sDVE", C("sDVE"))], incs=[("sDVE", 1)])
        sf = C("sDVE")
        E("sy", lambda e: e.dma_start(out=out_h[0:128, 0:32], in_=part1[:]),
          waits=[("sDVE", sf)], incs=[("qSC", 16)])
        E("sy", lambda e: e.dma_start(out=out_h[0:128, 32:288], in_=ohall[:]),
          waits=[("sDVE", sf)], incs=[("qSC", 16)])
        E("sy", lambda e: e.dma_start(out=out_h[0:128, 288:320], in_=rwall[:, 0:32]),
          waits=[("sDVE", sf)], incs=[("qSC", 16)])
        E("sy", lambda e: e.dma_start(out=out_h[0:128, 320:352], in_=sgb[:, 0:32]),
          waits=[("sDVE", sf)], incs=[("qSC", 16)])
        E("sy", lambda e: e.dma_start(out=out_h[128:129, 0:256], in_=bceF[:]),
          waits=[("sDVE", sf)], incs=[("qSC", 16)])
        for rr in range(8):
            E("sy", lambda e, rr=rr: e.dma_start(
                out=out_h[512 + rr * 128:512 + (rr + 1) * 128, 0:512].bitcast(BF16),
                in_=xsH[rr * 1024:rr * 1024 + 128, :]),
              waits=[("qPL", xs_scattered)], incs=[("qSC", 16)])
        for nm, srcap, dst0 in [("xn0", xnH[0], 2000), ("xs0", xsH[0:128, :], 2128),
                                ("gs0", gsH[0:128, :], 2256), ("xs1", xsH[1024:1152, :], 2384)]:
            E("sy", lambda e, srcap=srcap: e.dma_start(
                out=xsb[:, 0:DM], in_=srcap),
              waits=[("qSC", gs_all), ("qPL", C("qPL")), ("sDVE", C("sDVE")),
                     ("qSY", C("qSY")), ("sPE", C("sPE")), ("sACT", C("sACT")),
                     ("sPOOL", C("sPOOL"))],
              incs=[("qSY", 16)])
            E("dve", lambda e: e.tensor_copy(out=ffnacc[:], in_=xsb[:, 0:DM]),
              waits=[("qSY", C("qSY"))], incs=[("sDVE", 1)])
            E("sy", lambda e, dst0=dst0: e.dma_start(
                out=out_h[dst0:dst0 + 128, :], in_=ffnacc[:]),
              waits=[("sDVE", C("sDVE"))], incs=[("qSC", 16)])
        b.wait("sy", [("qSC", C("qSC"))])
    if DBG_SCRATCH:
        E("sy", lambda e: e.dma_start(out=xsD, in_=xsH),
          waits=[("qPL", xs_scattered)], incs=[("qSY", 16)])
        E("sy", lambda e: e.dma_start(out=gsD, in_=gsH),
          waits=[("qSC", gs_all)], incs=[("qSY", 16)])
        b.cnt["qSY"] += 0

    if PHASES == "pb":
        return finalize()
    b.barrier()

    slot_cast = [C("sPOOL"), C("sPOOL")]
    for et in range(NT):
        par = et % 2
        sl = stage[:, par * 1024:(par + 1) * 1024]
        E("sy", lambda e, et=et, sl=sl: e.dma_start(
            out=sl, in_=woutT_h[et * 128:(et + 1) * 128, :]),
          waits=[("sPOOL", slot_cast[par])], incs=[("qSY", 16)])
        E("pool", lambda e, et=et, sl=sl: e.tensor_copy(
            out=WA[:, et * 1024:(et + 1) * 1024], in_=sl),
          waits=[("qSY", C("qSY"))], incs=[("sPOOL", 1)])
        slot_cast[par] = C("sPOOL")
    wout_done = C("sPOOL")

    out_stored = 0
    mm5_done = [0, 0]
    evacd5 = [0, 0]
    for c in range(NCHUNK):
        par = c % 2
        E("sy", lambda e, c=c, par=par: e.dma_start(
            out=yzin[:, par * DI:(par + 1) * DI], in_=yzH[c]),
          waits=[("sPE", mm5_done[par])], incs=[("qSY", 16)])
        yz_ld = C("qSY")
        E("sy", lambda e, c=c, par=par: e.dma_start(
            out=xt5[:, par * DM:(par + 1) * DM], in_=x_h[c * 128:(c + 1) * 128, :]),
          waits=[("sDVE", evacd5[par])], incs=[("qSY", 16)])
        x_ld = C("qSY")
        E("pool", lambda e, c=c, par=par: e.indirect_dma_start(
            out=g5[:, par * DM:(par + 1) * DM], out_offset=None,
            in_=gsH, in_offset=bass.IndirectOffsetOnAxis(slotI[:, c:c + 1], 0)),
          waits=[("qSC", gs_all), ("sDVE", evacd5[par])], incs=[("qPL", 16)])
        g_ld = C("qPL")
        for et in range(NT):
            for jj in range(2):
                wts = [("qSY", yz_ld), ("sPOOL", wout_done)]
                if et == 0 and jj == 0:
                    wts.append(("sDVE", evacd5[par]))
                E("pe", lambda e, et=et, par=par, jj=jj: e.matmul(
                    out=psB[:, par * 1024 + jj * 512:par * 1024 + (jj + 1) * 512],
                    lhsT=yzin[:, par * DI + et * 128:par * DI + (et + 1) * 128],
                    rhs=WA[:, et * 1024 + jj * 512:et * 1024 + (jj + 1) * 512],
                    start=(et == 0), stop=(et == NT - 1)),
                  waits=wts, incs=([("sPE", 1)] if (et == NT - 1 and jj == 1) else []))
        smm = C("sPE")
        mm5_done[par] = smm
        E("dve", lambda e, par=par: e.tensor_mul(
            out=outb[:, par * DM:(par + 1) * DM],
            in0=psB[:, par * 1024:(par + 1) * 1024],
            in1=g5[:, par * DM:(par + 1) * DM]),
          waits=[("sPE", smm), ("qPL", g_ld), ("qSC", out_stored)],
          incs=[("sDVE", 1)])
        E("dve", lambda e, par=par: e.tensor_add(
            out=outb[:, par * DM:(par + 1) * DM],
            in0=outb[:, par * DM:(par + 1) * DM],
            in1=xt5[:, par * DM:(par + 1) * DM]),
          waits=[("sDVE", C("sDVE")), ("qSY", x_ld)], incs=[("sDVE", 1)])
        evacd5[par] = C("sDVE")
        E("act", lambda e, c=c, par=par: e.dma_start(
            out=out_h[c * 128:(c + 1) * 128, :], in_=outb[:, par * DM:(par + 1) * DM]),
          waits=[("sDVE", evacd5[par])], incs=[("qSC", 16)])
        out_stored = C("qSC")

    b.wait("sy", [("qSC", out_stored)])
    b.wait("act", [("qSC", out_stored)])

    return finalize()


_prog_cache = {}


def prep_inputs(L0_attn_x, L0_attn_W_in, L0_attn_conv_w, L0_attn_conv_b, L0_attn_W_x,
                L0_attn_W_dt, L0_attn_b_dt, L0_attn_A_log, L0_attn_D, L0_attn_W_out,
                L0_ffn_gate_w, L0_ffn_w_gate, L0_ffn_w_up, L0_ffn_w_down):
    x = np.ascontiguousarray(np.asarray(L0_attn_x, np.float32))
    winT = np.ascontiguousarray(np.asarray(L0_attn_W_in, np.float32).T)
    wxdtT = np.ascontiguousarray(
        np.concatenate([np.asarray(L0_attn_W_x, np.float32),
                        np.asarray(L0_attn_W_dt, np.float32)], 0).T)
    woutT = np.ascontiguousarray(np.asarray(L0_attn_W_out, np.float32).T)
    cw = np.asarray(L0_attn_conv_w, np.float32)[:, 0, :]
    convw = np.ascontiguousarray(
        cw.reshape(NT, 128, 4).transpose(1, 0, 2).reshape(128, NT * 4))
    convb = np.ascontiguousarray(
        np.asarray(L0_attn_conv_b, np.float32).reshape(NT, 128).T)
    dfm = np.ascontiguousarray(np.asarray(L0_attn_D, np.float32).reshape(NT, 128).T)
    bdt = np.asarray(L0_attn_b_dt, np.float32)[None, :]
    alog = np.asarray(L0_attn_A_log, np.float32)[None, :]
    gwT = np.ascontiguousarray(np.asarray(L0_ffn_gate_w, np.float32).T)
    wgT = np.ascontiguousarray(np.asarray(L0_ffn_w_gate, np.float32).transpose(0, 2, 1))
    wuT = np.ascontiguousarray(np.asarray(L0_ffn_w_up, np.float32).transpose(0, 2, 1))
    wdT = np.ascontiguousarray(np.asarray(L0_ffn_w_down, np.float32).transpose(0, 2, 1))

    shared = dict(winT=winT, wxdtT=wxdtT, woutT=woutT, convw=convw, convb=convb,
                  dfm=dfm, bdt=bdt, alog=alog, gwT=gwT, wgT=wgT, wuT=wuT, wdT=wdT)
    return shared, x


def kernel(**inputs):
    shared, x = prep_inputs(**inputs)
    nc = build_program()
    in_maps = [dict(shared, x=np.ascontiguousarray(x[bb])) for bb in range(8)]
    res = run_bass_kernel_spmd(nc, in_maps, core_ids=list(range(8)))
    return np.stack([r["out"] for r in res.results], axis=0)


# version 14
# speedup vs baseline: 1.3739x; 1.0141x over previous
import numpy as np
from collections import defaultdict

import concourse.bass as bass
import concourse.mybir as mybir
from concourse.bass_utils import run_bass_kernel_spmd

F32 = mybir.dt.float32
BF16 = mybir.dt.bfloat16
AF = mybir.ActivationFunctionType
ALU = mybir.AluOpType
AX = mybir.AxisListType

L = 4096
DM = 1024
DI = 2048
NS = 16
Q = 128
NCHUNK = L // Q
NT = 16
TW = 512
NTW = L // TW
NHG = L // 256
PHASES = "all"
P0_LIMIT = NCHUNK
P0_SKIP = ""
SKIP_SCATTER = False
DBG_SCRATCH = False


class Builder:
    ENGS = ("sy", "pe", "act", "dve", "pool")

    def __init__(self):
        self.q = {e: [] for e in self.ENGS}
        self.cnt = defaultdict(int)
        self.sem = None

    def emit(self, eng, fn, waits=(), incs=()):
        waits = [(s, v) for (s, v) in waits if v > 0]
        for (s, v) in waits:
            assert v <= self.cnt[s], f"wait on future count {s} {v} > {self.cnt[s]}"
        incs = [(s, n) for (s, n) in incs]
        for (s, n) in incs:
            self.cnt[s] += n

        def cl(ec, _fn=fn, _w=waits, _i=incs):
            for (s, v) in _w:
                ec.wait_ge(self.sem[s], v)
            inst = _fn(ec)
            for (s, n) in _i:
                inst.then_inc(self.sem[s], n)
        self.q[eng].append(cl)
        return {s: self.cnt[s] for s, _ in incs}

    def c(self, s):
        return self.cnt[s]

    def wait(self, eng, waits):
        def cl(ec, _w=list(waits)):
            for (s, v) in _w:
                ec.wait_ge(self.sem[s], v)
        self.q[eng].append(cl)

    def barrier(self):
        targets = {}
        for e in self.ENGS:
            self.cnt["b" + e] += 1
            targets[e] = self.cnt["b" + e]
        for e in self.ENGS:
            def cl(ec, _e=e, _t=dict(targets)):
                ec.sem_inc(self.sem["b" + _e], 1)
                for oe, v in _t.items():
                    ec.wait_ge(self.sem["b" + oe], v)
            self.q[e].append(cl)


def build_program():
    nc = bass.Bass("TRN2", target_bir_lowering=False, debug=False)
    b = Builder()
    E, C = b.emit, b.c

    x_h = nc.dram_tensor("x", [L, DM], F32, kind="ExternalInput").ap()
    winT_h = nc.dram_tensor("winT", [DM, 2 * DI], F32, kind="ExternalInput").ap()
    wxdtT_h = nc.dram_tensor("wxdtT", [DI, 48], F32, kind="ExternalInput").ap()
    woutT_h = nc.dram_tensor("woutT", [DI, DM], F32, kind="ExternalInput").ap()
    convw_h = nc.dram_tensor("convw", [128, NT * 4], F32, kind="ExternalInput").ap()
    convb_h = nc.dram_tensor("convb", [128, NT], F32, kind="ExternalInput").ap()
    dfm_h = nc.dram_tensor("dfm", [128, NT], F32, kind="ExternalInput").ap()
    bdt_h = nc.dram_tensor("bdt", [1, NS], F32, kind="ExternalInput").ap()
    alog_h = nc.dram_tensor("alog", [1, NS], F32, kind="ExternalInput").ap()
    gwT_h = nc.dram_tensor("gwT", [DM, 8], F32, kind="ExternalInput").ap()
    wgT_h = nc.dram_tensor("wgT", [8, DM, 512], F32, kind="ExternalInput").ap()
    wuT_h = nc.dram_tensor("wuT", [8, DM, 512], F32, kind="ExternalInput").ap()
    wdT_h = nc.dram_tensor("wdT", [8, 512, DM], F32, kind="ExternalInput").ap()
    out_h = nc.dram_tensor("out", [L, DM], F32, kind="ExternalOutput").ap()

    KSC = "ExternalOutput" if DBG_SCRATCH else "Internal"
    ufmH = nc.dram_tensor("ufmH", [NCHUNK, 128, NT, Q], BF16, kind=KSC).ap()
    utokH = nc.dram_tensor("utokH", [NCHUNK, 128, DI], BF16, kind=KSC).ap()
    bcH = nc.dram_tensor("bcH", [48, L], F32, kind=KSC).ap()
    yzH = nc.dram_tensor("yzH", [NCHUNK, 128, DI], BF16, kind=KSC).ap()
    gH = nc.dram_tensor("gH", [NCHUNK, 128, DM], BF16, kind=KSC).ap()
    xnH = nc.dram_tensor("xnH", [NCHUNK, 128, DM], BF16, kind="Internal").ap()
    xsH = nc.dram_tensor("xsH", [8 * 1024, DM], BF16, kind="Internal").ap()
    gsH = nc.dram_tensor("gsH", [8 * 1024, DM], BF16, kind="Internal").ap()
    xsD = nc.dram_tensor("xsD", [8 * 1024, DM], BF16, kind=KSC).ap() if DBG_SCRATCH else None
    gsD = nc.dram_tensor("gsD", [8 * 1024, DM], BF16, kind=KSC).ap() if DBG_SCRATCH else None
    bceH = nc.dram_tensor("bceH", [NCHUNK, 8], F32, kind="Internal").ap()
    slotH = nc.dram_tensor("slotH", [128, NCHUNK], mybir.dt.int32, kind=KSC).ap()
    rtD = nc.dram_tensor("rtD", [128, 640], F32, kind=KSC).ap()
    partH = nc.dram_tensor("partH", [NCHUNK, 128, DM], F32, kind="Internal").ap()
    wzHd = nc.dram_tensor("wzHd", [DM, DI], BF16, kind="Internal").ap()
    fcumH = nc.dram_tensor("fcumH", [NS, Q], F32, kind="Internal").ap()
    fcH = nc.dram_tensor("fcH", [NS, Q], BF16, kind="Internal").ap()

    def sb(name, shape, dtype):
        return nc.alloc_sbuf_tensor(name, shape, dtype)

    xnT = sb("xnT", [128, 8 * L], BF16)
    hst = sb("hst", [NS, DI], F32)
    triu = sb("triu", [128, 128], F32)
    mneg = sb("mneg", [128, 128], F32)
    identb = sb("identb", [128, 128], BF16)
    identf = sb("identf", [128, 128], F32)
    onescol = sb("onescol", [128, 1], F32)
    ones1f = sb("ones1f", [1, 128], F32)
    ones1b = sb("ones1b", [1, 128], BF16)
    convw = sb("convw_s", [128, NT * 4], F32)
    convb = sb("convb_s", [128, NT], F32)
    dfm = sb("dfm_s", [128, NT], F32)
    bdt_r = sb("bdt_r", [1, NS], F32)
    alog_r = sb("alog_r", [1, NS], F32)
    arow = sb("arow", [1, NS], F32)
    bdtB = sb("bdtB", [128, NS], F32)
    aB = sb("aB", [128, NS], F32)
    wxdtT = sb("wxdtT_s", [128, NT * 48], BF16)
    gwT = sb("gwT_s", [128, 8 * 8], BF16)
    rwall = sb("rwall", [128, NCHUNK * 8], F32)
    slotI = sb("slotI", [128, NCHUNK], mybir.dt.int32)
    tris = sb("tris", [128, 128], F32)
    ecapB = sb("ecapB", [128, 8], F32)
    wdec = sb("wdec", [128, 32], F32)
    rmax4 = sb("rmax4", [128, 4], F32)
    ohb = sb("ohb", [128, 32], F32)
    limB = sb("limB", [128, 8], F32)
    stage = sb("stage", [128, 2048], F32)
    WA = sb("WA", [128, 16 * 1024], BF16)
    WB = sb("WB", [128, 16 * 1024], BF16)

    SCR = 52 * 1024
    scr = sb("scr", [128, SCR], mybir.dt.uint8)
    scr_addr = nc.lookup_mloc(scr).addr

    def at(name, off, shape, dtype):
        assert off % 64 == 0
        nbytes = int(np.prod(shape[1:])) * mybir.dt.size(dtype)
        assert off + nbytes <= SCR, f"{name} overflows scratch: {off}+{nbytes}"
        return nc.alloc_sbuf_tensor_at(name, shape, dtype, offset=scr_addr + off)

    xt0 = at("xt0", 0, [128, 2 * DM], F32)
    xnb = at("xnb", 8192, [128, DM], BF16)
    st1 = at("st1", 10240, [128, 1], F32)
    st2 = at("st2", 10304, [128, 1], F32)
    st3 = at("st3", 10368, [128, 1], F32)
    tileb = at("tileb", 0, [128, 2 * (TW + 3)], F32)
    convacc = at("convacc", 4224, [128, 2 * TW], F32)
    tails = at("tails", 8320, [128, NT * 3], F32)
    bcev = at("bcev", 8576, [48, TW], F32)
    ufm16 = at("ufm16", 10624, [128, NT * TW], BF16)
    utokb = at("utokb", 27008, [128, 2 * DI], BF16)
    convacc2 = at("convacc2", 35200, [128, 2 * TW], F32)
    tbuf2 = at("tbuf2", 39296, [128, TW], F32)
    szf = at("szf", 0, [128, 2 * NT * 256], BF16)
    ut4 = at("ut4", 16384, [128, DI], BF16)
    uf4 = at("uf4", 20480, [128, DI], BF16)
    mb1 = at("mb1", 24576, [128, NS * Q], F32)
    mb2 = at("mb2", 32768, [128, NS * Q], BF16)
    yzb = at("yzb", 36864, [128, DI], BF16)
    flt = at("flt", 40960, [1, NS * Q], F32)
    fltc = at("fltc", 49152, [1, NS * Q], BF16)
    sm_cum = sb("sm_cum", [128, NS], F32)
    sm_mbias = sb("sm_mbias", [128, 2 * NS], F32)
    sm_dB = sb("sm_dB", [128, NS], F32)
    sm_lnb = sb("sm_lnb", [128, NS], F32)
    sm_sgn = sb("sm_sgn", [128, 2 * NS], F32)
    sm_dlt = sb("sm_dlt", [128, NS], F32)
    sm_a = sb("sm_a", [128, NS], F32)
    sm_b = sb("sm_b", [128, NS], F32)
    sm_cbf = sb("sm_cbf", [128, NS], BF16)
    sm_ctl = sb("sm_ctl", [128, NS], F32)
    cumT_e = sb("cumT_e", [NS, Q], F32)
    cb16 = sb("cb16", [NS, Q], BF16)
    ctT = sb("ctT", [NS, 2 * Q], F32)
    btok = sb("btok", [128, 2 * NS], BF16)
    dec_c = sb("dec_c", [NS, 2], F32)
    clF = sb("clF", [1, NS], F32)
    msb = sb("msb", [128, Q], F32)
    mbf = sb("mbf", [128, Q], BF16)
    bc4 = sb("bc4", [48, 2 * Q], F32)
    sgb = at("sgb", 0, [128, 512], F32)
    upb = at("upb", 2048, [128, 512], F32)
    pbf_ = at("pbf_", 4096, [128, 512], BF16)
    pTb = at("pTb", 5120, [128, 512], BF16)
    ffnacc = at("ffnacc", 6144, [128, DM], F32)
    rmaxb = at("rmaxb", 22528, [128, 1], F32)
    rwb = at("rwb", 22592, [128, 1], F32)
    oh8 = at("oh8", 22656, [128, 8], F32)
    cum8 = at("cum8", 22720, [128, 8], F32)
    z8 = at("z8", 22784, [128, 8], F32)
    yzin = at("yzin", 24576, [128, 2 * DI], BF16)
    outb = at("outb", 32768, [128, 2 * DM], F32)
    xt5 = at("xt5", 40960, [128, 2 * DM], F32)
    g5 = at("g5", 49152, [128, 2 * DM], BF16)

    psA = nc.alloc_psum_tensor("psA", [128, 1024], F32)
    psB = nc.alloc_psum_tensor("psB", [128, 2048], F32)
    psC = nc.alloc_psum_tensor("psC", [128, 512], F32)
    psT = nc.alloc_psum_tensor("psT", [128, 1024], BF16)

    for sn in (["qSY", "qSC", "qPL", "sPE", "sACT", "sDVE", "sPOOL"] +
               ["b" + e for e in Builder.ENGS]):
        pass
    b.sem = {sn: nc.alloc_semaphore(sn) for sn in
             ["qSY", "qSC", "qPL", "sPE", "sACT", "sDVE", "sPOOL"] +
             ["b" + e for e in Builder.ENGS]}

    def finalize():
        b.wait("sy", [("qPL", C("qPL")), ("qSY", C("qSY")), ("qSC", C("qSC"))])
        b.barrier()
        with nc.Block() as block:
            @block.sync
            def _(eng):
                for cl in b.q["sy"]:
                    cl(eng)

            @block.tensor
            def _(eng):
                for cl in b.q["pe"]:
                    cl(eng)

            @block.scalar
            def _(eng):
                for cl in b.q["act"]:
                    cl(eng)

            @block.vector
            def _(eng):
                for cl in b.q["dve"]:
                    cl(eng)

            @block.gpsimd
            def _(eng):
                for cl in b.q["pool"]:
                    cl(eng)
        return nc

    for ap_sb, ap_h in [(convw, convw_h), (convb, convb_h), (dfm, dfm_h),
                        (bdt_r, bdt_h), (alog_r, alog_h)]:
        E("sy", lambda e, a=ap_sb, h=ap_h: e.dma_start(out=a[:], in_=h),
          incs=[("qSY", 16)])
    small_in = C("qSY")

    E("pool", lambda e: e.memset(triu[:], 1.0), incs=[("sPOOL", 1)])
    E("pool", lambda e: e.affine_select(out=triu[:], in_=triu[:], compare_op=ALU.is_ge,
                                        fill=0.0, base=0, pattern=[[1, 128]],
                                        channel_multiplier=-1),
      waits=[("sPOOL", C("sPOOL"))], incs=[("sPOOL", 1)])
    E("pool", lambda e: e.memset(mneg[:], 0.0), incs=[("sPOOL", 1)])
    E("pool", lambda e: e.affine_select(out=mneg[:], in_=mneg[:], compare_op=ALU.is_ge,
                                        fill=-1e9, base=0, pattern=[[1, 128]],
                                        channel_multiplier=-1),
      waits=[("sPOOL", C("sPOOL"))], incs=[("sPOOL", 1)])
    E("pool", lambda e: e.memset(identb[:], 0.0), incs=[("sPOOL", 1)])
    E("pool", lambda e: e.affine_select(out=identb[:], in_=identb[:],
                                        compare_op=ALU.not_equal, fill=1.0, base=0,
                                        pattern=[[-1, 128]], channel_multiplier=1),
      waits=[("sPOOL", C("sPOOL"))], incs=[("sPOOL", 1)])
    E("pool", lambda e: e.memset(identf[:], 0.0), incs=[("sPOOL", 1)])
    E("pool", lambda e: e.affine_select(out=identf[:], in_=identf[:],
                                        compare_op=ALU.not_equal, fill=1.0, base=0,
                                        pattern=[[-1, 128]], channel_multiplier=1),
      waits=[("sPOOL", C("sPOOL"))], incs=[("sPOOL", 1)])
    E("pool", lambda e: e.memset(onescol[:], 1.0))
    E("pool", lambda e: e.memset(ones1f[:], 1.0))
    E("pool", lambda e: e.memset(ones1b[:], 1.0))
    E("pool", lambda e: e.memset(tris[:], 1.0), incs=[("sPOOL", 1)])
    E("pool", lambda e: e.affine_select(out=tris[:], in_=tris[:], compare_op=ALU.is_gt,
                                        fill=0.0, base=0, pattern=[[1, 128]],
                                        channel_multiplier=-1),
      waits=[("sPOOL", C("sPOOL"))], incs=[("sPOOL", 1)])
    E("pool", lambda e: e.iota(out=ecapB[:], pattern=[[1024, 8]], base=0,
                               channel_multiplier=0,
                               allow_small_or_imprecise_dtypes=True))
    E("pool", lambda e: e.iota(out=limB[:], pattern=[[1024, 8]], base=1023,
                               channel_multiplier=0,
                               allow_small_or_imprecise_dtypes=True))
    E("pool", lambda e: e.iota(out=wdec[:], pattern=[[0, 4], [-1, 8]], base=8,
                               channel_multiplier=0,
                               allow_small_or_imprecise_dtypes=True))
    E("pool", lambda e: e.memset(hst[:], 0.0), incs=[("sPOOL", 1)])
    consts_done = C("sPOOL")

    E("act", lambda e: e.activation(out=arow[:], in_=alog_r[:], func=AF.Exp),
      waits=[("qSY", small_in)], incs=[("sACT", 1)])
    E("dve", lambda e: e.tensor_scalar_mul(out=arow[:], in0=arow[:], scalar1=-1.0),
      waits=[("sACT", C("sACT"))], incs=[("sDVE", 1)])
    arow_done = C("sDVE")
    E("pe", lambda e: e.matmul(out=psC[:, 0:NS], lhsT=ones1f[:], rhs=bdt_r[:],
                               start=True, stop=True),
      waits=[("qSY", small_in), ("sPOOL", consts_done)], incs=[("sPE", 1)])
    E("pe", lambda e: e.matmul(out=psC[:, NS:2 * NS], lhsT=ones1f[:], rhs=arow[:],
                               start=True, stop=True),
      waits=[("sDVE", arow_done)], incs=[("sPE", 1)])
    bc_mm = C("sPE")
    E("dve", lambda e: e.tensor_copy(out=bdtB[:], in_=psC[:, 0:NS]),
      waits=[("sPE", bc_mm)], incs=[("sDVE", 1)])
    E("dve", lambda e: e.tensor_copy(out=aB[:], in_=psC[:, NS:2 * NS]),
      incs=[("sDVE", 1)])
    bcasts_done = C("sDVE")

    cast_mark = 0
    for kc in range(NT):
        sl = stage[:, (kc % 2) * 1024:(kc % 2) * 1024 + 48]
        wts = [("sPOOL", cast_mark)] if kc >= 2 else []
        E("sy", lambda e, kc=kc, sl=sl: e.dma_start(
            out=sl, in_=wxdtT_h[kc * 128:(kc + 1) * 128, :]),
          waits=wts, incs=[("qSY", 16)])
        E("pool", lambda e, kc=kc, sl=sl: e.tensor_copy(
            out=wxdtT[:, kc * 48:(kc + 1) * 48], in_=sl),
          waits=[("qSY", C("qSY"))], incs=[("sPOOL", 1)])
        cast_mark = C("sPOOL")
    for kc in range(8):
        sl = stage[:, (kc % 2) * 1024:(kc % 2) * 1024 + 8]
        E("sy", lambda e, kc=kc, sl=sl: e.dma_start(
            out=sl, in_=gwT_h[kc * 128:(kc + 1) * 128, :]),
          waits=[("sPOOL", cast_mark)], incs=[("qSY", 16)])
        E("pool", lambda e, kc=kc, sl=sl: e.tensor_copy(
            out=gwT[:, kc * 8:(kc + 1) * 8], in_=sl),
          waits=[("qSY", C("qSY"))], incs=[("sPOOL", 1)])
        cast_mark = C("sPOOL")

    if PHASES == "consts":
        return finalize()
    psT_evac = [0, 0, 0, 0]
    xn_cons = [0, 0]
    xnb_stored = 0
    for c in range(min(NCHUNK, P0_LIMIT)):
        par = c % 2
        xs = xt0[:, par * DM:(par + 1) * DM]
        wts = [("sDVE", xn_cons[par])] if c >= 2 else []
        E("sy", lambda e, c=c, xs=xs: e.dma_start(out=xs, in_=x_h[c * 128:(c + 1) * 128, :]),
          waits=wts, incs=[("qSY", 16)])
        ldc = C("qSY")
        if "l" in P0_SKIP:
            xnd = 0
        else:
         if True:
            E("dve", lambda e, xs=xs: e.tensor_reduce(out=st1[:], in_=xs, axis=AX.X,
                                                      op=ALU.add),
                  waits=[("qSY", ldc)], incs=[("sDVE", 1)])
            E("dve", lambda e: e.tensor_scalar_mul(out=st1[:], in0=st1[:], scalar1=1.0 / DM),
              waits=[("sDVE", C("sDVE"))], incs=[("sDVE", 1)])
            mdone = C("sDVE")
            E("act", lambda e, xs=xs: e.activation(out=psC[:, 0:512],
                                                   in_=xs[:, 0:512], func=AF.Square,
                                                   accum_out=st2[:]),
              waits=[("qSY", ldc)], incs=[("sACT", 1)])
            E("act", lambda e, xs=xs: e.activation(out=psC[:, 0:512], in_=xs[:, 512:1024],
                                                   func=AF.Square, accum_out=st3[:]),
              waits=[("sACT", C("sACT"))], incs=[("sACT", 1)])
            sqd = C("sACT")
            E("dve", lambda e: e.tensor_add(out=st2[:], in0=st2[:], in1=st3[:]),
              waits=[("sACT", sqd), ("sDVE", mdone)], incs=[("sDVE", 1)])
            E("dve", lambda e: e.tensor_mul(out=st3[:], in0=st1[:], in1=st1[:]),
              waits=[("sDVE", C("sDVE"))], incs=[("sDVE", 1)])
            E("dve", lambda e: e.tensor_scalar(out=st3[:], in0=st3[:], scalar1=-1.0,
                                               scalar2=1e-5, op0=ALU.mult, op1=ALU.add),
              waits=[("sDVE", C("sDVE"))], incs=[("sDVE", 1)])
            vb = C("sDVE")
            E("act", lambda e: e.activation(out=st3[:], in_=st2[:], func=AF.Ln,
                                            bias=st3[:], scale=1.0 / DM),
              waits=[("sDVE", vb)], incs=[("sACT", 1)])
            E("act", lambda e: e.activation(out=st2[:], in_=st3[:], func=AF.Exp, scale=-0.5),
              waits=[("sACT", C("sACT"))], incs=[("sACT", 1)])
            rstd = C("sACT")
            aff = C("sDVE")
            E("dve", lambda e, xs=xs: e.tensor_scalar(out=xnb[:], in0=xs, scalar1=st1[:],
                                                      scalar2=st2[:], op0=ALU.subtract,
                                                      op1=ALU.mult),
              waits=[("sDVE", aff), ("sACT", rstd)], incs=[("sDVE", 1)])
            xnd = C("sDVE")
        if "t" in P0_SKIP:
            xn_cons[par] = xnd if isinstance(xnd, int) else 0
            continue
        for g in range(2):
            for sl4 in range(4):
                kc = g * 4 + sl4
                tdst = psT[:, sl4 * 128:(sl4 + 1) * 128]
                E("pe", lambda e, kc=kc, tdst=tdst: e.transpose(
                    out=tdst, in_=xnb[:, kc * 128:(kc + 1) * 128], identity=identb[:]),
                  waits=[("sDVE", xnd), ("sDVE", psT_evac[0]),
                         ("sPOOL", consts_done)],
                  incs=[("sPE", 1)])
            td = C("sPE")
            for sl4 in range(4):
                kc = g * 4 + sl4
                tdst = psT[:, sl4 * 128:(sl4 + 1) * 128]
                E("dve", lambda e, kc=kc, c=c, tdst=tdst: e.tensor_copy(
                    out=xnT[:, kc * L + c * 128:kc * L + (c + 1) * 128], in_=tdst),
                  waits=[("sPE", td)], incs=[("sDVE", 1)])
            psT_evac[0] = C("sDVE")
        xn_cons[par] = xnd

    if PHASES == "p0":
        return finalize()
    b.barrier()

    slot_cast = [0, 0]
    idx = 0
    for kc in range(8):
        for hh in range(2):
            par = idx % 2
            sl = stage[:, par * 1024:(par + 1) * 1024]
            wts = [("sPOOL", slot_cast[par])] if slot_cast[par] else []
            E("sy", lambda e, kc=kc, hh=hh, sl=sl: e.dma_start(
                out=sl, in_=winT_h[kc * 128:(kc + 1) * 128, hh * 1024:(hh + 1) * 1024]),
              waits=wts, incs=[("qSY", 16)])
            E("pool", lambda e, kc=kc, hh=hh, sl=sl: e.tensor_copy(
                out=WA[:, kc * DI + hh * 1024:kc * DI + (hh + 1) * 1024], in_=sl),
              waits=[("qSY", C("qSY"))], incs=[("sPOOL", 1)])
            slot_cast[par] = C("sPOOL")
            idx += 1
    winxi_done = C("sPOOL")

    E("pool", lambda e: e.memset(tails[:], 0.0), incs=[("sPOOL", 1)])
    tails_zeroed = C("sPOOL")
    utok_stored = [0] * NCHUNK
    ufm_stored = [0] * NTW
    bc_stored = [0] * NTW
    psA_free = [0, 0]
    tile_free = [0, 0]
    ufm_cons_pe = 0
    ufm_cons_pl = 0
    utokb_free = [0, 0]
    for T in range(NTW):
        silu_marks = [0] * NT
        tapend_marks = [0] * NT
        for ft in range(NT):
            par = ft % 2
            tb = tileb[:, par * (TW + 3):(par + 1) * (TW + 3)]
            psl = psA[:, par * 512:(par + 1) * 512]
            for kc in range(8):
                wts = []
                if kc == 0:
                    wts = [("sPOOL", winxi_done), ("sDVE", psA_free[par]),
                           ("bsy", C("bsy"))]
                E("pe", lambda e, T=T, ft=ft, kc=kc, psl=psl: e.matmul(
                    out=psl,
                    lhsT=WA[:, kc * DI + ft * 128:kc * DI + (ft + 1) * 128],
                    rhs=xnT[:, kc * L + T * TW:kc * L + (T + 1) * TW],
                    start=(kc == 0), stop=(kc == 7)),
                  waits=wts, incs=([("sPE", 1)] if kc == 7 else []))
            mmd = C("sPE")
            E("dve", lambda e, ft=ft, tb=tb: e.tensor_copy(
                out=tb[:, 0:3], in_=tails[:, ft * 3:(ft + 1) * 3]),
              waits=[("sACT", tile_free[par]), ("sPOOL", tails_zeroed)],
              incs=[("sDVE", 1)])
            E("dve", lambda e, tb=tb, psl=psl: e.tensor_copy(out=tb[:, 3:3 + TW], in_=psl),
              waits=[("sPE", mmd)], incs=[("sDVE", 1)])
            psA_free[par] = C("sDVE")
            E("dve", lambda e, ft=ft, tb=tb: e.tensor_copy(
                out=tails[:, ft * 3:(ft + 1) * 3], in_=tb[:, TW:TW + 3]),
              waits=[("sDVE", C("sDVE"))], incs=[("sDVE", 1)])
            ca = convacc[:, par * TW:(par + 1) * TW]
            E("dve", lambda e, ft=ft, tb=tb, ca=ca: e.tensor_scalar_mul(
                out=ca, in0=tb[:, 0:TW], scalar1=convw[:, ft * 4:ft * 4 + 1]),
              waits=[("sDVE", C("sDVE")), ("sACT", tile_free[par])], incs=[("sDVE", 1)])
            for k in range(1, 4):
                E("dve", lambda e, ft=ft, k=k, tb=tb, ca=ca: e.scalar_tensor_tensor(
                    out=ca, in0=tb[:, k:k + TW],
                    scalar=convw[:, ft * 4 + k:ft * 4 + k + 1],
                    in1=ca, op0=ALU.mult, op1=ALU.add),
                  waits=[("sDVE", C("sDVE"))], incs=[("sDVE", 1)])
            tapd = C("sDVE")
            tapend_marks[ft] = tapd
            E("act", lambda e, ft=ft, ca=ca: e.activation(
                out=ufm16[:, ft * TW:(ft + 1) * TW], in_=ca, func=AF.Silu,
                bias=convb[:, ft:ft + 1]),
              waits=[("sDVE", tapd), ("sPE", ufm_cons_pe), ("qPL", ufm_cons_pl)],
              incs=[("sACT", 1)])
            silu_marks[ft] = C("sACT")
            tile_free[par] = C("sACT")
        for ft in range(NT):
            E("pe", lambda e, ft=ft: e.matmul(
                out=psC[0:48, 0:512],
                lhsT=wxdtT[:, ft * 48:(ft + 1) * 48],
                rhs=ufm16[:, ft * TW:(ft + 1) * TW],
                start=(ft == 0), stop=(ft == NT - 1)),
              waits=[("sACT", silu_marks[ft])] +
                    ([("sDVE", C("sDVE"))] if ft == 0 else []),
              incs=([("sPE", 1)] if ft == NT - 1 else []))
        bcmm = C("sPE")
        E("dve", lambda e: e.tensor_copy(out=bcev[:], in_=psC[0:48, 0:512]),
          waits=[("sPE", bcmm)], incs=[("sDVE", 1)])
        bev = C("sDVE")
        E("pool", lambda e, T=T: e.dma_start(out=bcH[:, T * TW:(T + 1) * TW], in_=bcev[:]),
          waits=[("sDVE", bev)], incs=[("qPL", 16)])
        bc_stored[T] = C("qPL")
        for ft in range(NT):
            E("pool", lambda e, T=T, ft=ft: e.dma_start(
                out=ufmH[T * 4:(T + 1) * 4, :, ft, :].rearrange("c p t -> p c t"),
                in_=ufm16[:, ft * TW:(ft + 1) * TW].rearrange("p (c t) -> p c t", c=4)),
              waits=[("sACT", silu_marks[ft])], incs=[("qPL", 16)])
        ufm_stored[T] = C("qPL")
        for tc in range(4):
            c = T * 4 + tc
            spar = tc % 2
            ub = utokb[:, spar * DI:(spar + 1) * DI]
            for g in range(4):
                for sl4 in range(4):
                    ft = g * 4 + sl4
                    tdst = psT[:, sl4 * 128:(sl4 + 1) * 128]
                    E("pe", lambda e, ft=ft, tc=tc, tdst=tdst: e.transpose(
                        out=tdst,
                        in_=ufm16[:, ft * TW + tc * 128:ft * TW + (tc + 1) * 128],
                        identity=identb[:]),
                      waits=[("sACT", silu_marks[ft]), ("sDVE", psT_evac[0])],
                      incs=[("sPE", 1)])
                td = C("sPE")
                for sl4 in range(4):
                    ft = g * 4 + sl4
                    tdst = psT[:, sl4 * 128:(sl4 + 1) * 128]
                    E("dve", lambda e, ft=ft, ub=ub, tdst=tdst: e.tensor_copy(
                        out=ub[:, ft * 128:(ft + 1) * 128], in_=tdst),
                      waits=[("sPE", td), ("qPL", utokb_free[spar])],
                      incs=[("sDVE", 1)])
                psT_evac[0] = C("sDVE")
            utd = C("sDVE")
            E("pool", lambda e, c=c, ub=ub: e.dma_start(out=utokH[c], in_=ub),
              waits=[("sDVE", utd)], incs=[("qPL", 16)])
            utok_stored[c] = C("qPL")
            utokb_free[spar] = C("qPL")
        ufm_cons_pe = C("sPE")
        ufm_cons_pl = C("qPL")

    if PHASES == "p2":
        return finalize()
    b.barrier()

    slot_cast = [C("sPOOL"), C("sPOOL")]
    idx = 0
    for kc in range(8):
        for hh in range(2):
            par = idx % 2
            sl = stage[:, par * 1024:(par + 1) * 1024]
            E("sy", lambda e, kc=kc, hh=hh, sl=sl: e.dma_start(
                out=sl, in_=winT_h[kc * 128:(kc + 1) * 128,
                                   DI + hh * 1024:DI + (hh + 1) * 1024]),
              waits=[("sPOOL", slot_cast[par])], incs=[("qSY", 16)])
            E("pool", lambda e, kc=kc, hh=hh, sl=sl: e.tensor_copy(
                out=WB[:, kc * DI + hh * 1024:kc * DI + (hh + 1) * 1024], in_=sl),
              waits=[("qSY", C("qSY"))], incs=[("sPOOL", 1)])
            slot_cast[par] = C("sPOOL")
            idx += 1
    winz_done = C("sPOOL")

    if PHASES == "pg":
        return finalize()
    b.barrier()

    def emit_zhalf(hg, szpar, after_psB_free_dve, szf_war_pool=0):
        prev_silu = [0, 0]
        for rnd in range(4):
            base = (rnd % 2) * 1024
            for fti in range(4):
                ft = rnd * 4 + fti
                for kc in range(8):
                    wts = []
                    if fti == 0 and kc == 0:
                        wts = [("sPOOL", winz_done), ("sDVE", after_psB_free_dve),
                               ("sACT", prev_silu[rnd % 2])]
                    E("pe", lambda e, hg=hg, ft=ft, fti=fti, kc=kc, base=base: e.matmul(
                        out=psB[:, base + fti * 256:base + (fti + 1) * 256],
                        lhsT=WB[:, kc * DI + ft * 128:kc * DI + (ft + 1) * 128],
                        rhs=xnT[:, kc * L + hg * 256:kc * L + (hg + 1) * 256],
                        start=(kc == 0), stop=(kc == 7)),
                      waits=wts, incs=([("sPE", 1)] if kc == 7 else []))
            zmm = C("sPE")
            for fti in range(4):
                ft = rnd * 4 + fti
                E("act", lambda e, ft=ft, fti=fti, szpar=szpar, base=base: e.activation(
                    out=szf[:, szpar * NT * 256 + ft * 256:
                            szpar * NT * 256 + (ft + 1) * 256],
                    in_=psB[:, base + fti * 256:base + (fti + 1) * 256], func=AF.Silu),
                  waits=[("sPE", zmm), ("sPOOL", szf_war_pool)], incs=[("sACT", 1)])
            prev_silu[rnd % 2] = C("sACT")
        return C("sACT")

    sz_ready = [0] * NHG
    sz_ready[0] = emit_zhalf(0, 0, 0)
    szf_cons = [0, 0]

    state = {
        "utok_cons": 0, "ufm_cons": 0, "bc_cons": [0, 0],
        "flt_cons": 0, "fltc_cons": 0, "yzb_stored": 0,
        "psB_evac_dve": 0, "psB_z_done": C("sACT"),
        "prev_h_done": 0, "prev_X1_done": 0, "psA_free": 0,
    }
    SM = {}

    def emit_stats(c):
        par = c % 2
        E("sy", lambda e, c=c, par=par: e.dma_start(
            out=bc4[:, par * Q:(par + 1) * Q], in_=bcH[:, c * Q:(c + 1) * Q]),
          waits=[("qPL", bc_stored[c // 4]), ("sPE", state["bc_cons"][par])],
          incs=[("qSY", 16)])
        bc_ld = C("qSY")
        prev_psc_free = SM[c - 1]["ctT"] if (c - 1) in SM else 0
        E("pe", lambda e, par=par: e.transpose(
            out=psC[:, 304:352], in_=bc4[:, par * Q:(par + 1) * Q],
            identity=identf[0:48, 0:48]),
          waits=[("qSY", bc_ld), ("sDVE", prev_psc_free)], incs=[("sPE", 1)])
        bct = C("sPE")
        state["bc_cons"][par] = bct
        E("dve", lambda e: e.tensor_add(out=sm_a[:], in0=psC[:, 304 + 32:304 + 48],
                                        in1=bdtB[:]),
          waits=[("sPE", bct), ("sDVE", bcasts_done)], incs=[("sDVE", 1)])
        E("dve", lambda e: e.tensor_copy(out=sm_b[:], in_=psC[:, 304:304 + 16]),
          waits=[("sDVE", C("sDVE"))], incs=[("sDVE", 1)])
        E("dve", lambda e: e.tensor_copy(out=sm_cbf[:], in_=psC[:, 304 + 16:304 + 32]),
          waits=[("sDVE", C("sDVE"))], incs=[("sDVE", 1)])
        smalls0 = C("sDVE")
        E("dve", lambda e: e.tensor_scalar_max(out=sm_dlt[:], in0=sm_a[:], scalar1=0.0),
          waits=[("sDVE", smalls0)], incs=[("sDVE", 1)])
        r0 = C("sDVE")
        E("act", lambda e: e.activation(out=sm_lnb[:], in_=sm_a[:], func=AF.Abs),
          waits=[("sDVE", smalls0)], incs=[("sACT", 1)])
        E("act", lambda e: e.activation(out=sm_lnb[:], in_=sm_lnb[:], func=AF.Exp,
                                        scale=-1.0),
          waits=[("sACT", C("sACT"))], incs=[("sACT", 1)])
        E("act", lambda e: e.activation(out=sm_lnb[:], in_=sm_lnb[:], func=AF.Ln,
                                        bias=1.0),
          waits=[("sACT", C("sACT"))], incs=[("sACT", 1)])
        spl = C("sACT")
        E("dve", lambda e: e.tensor_add(out=sm_dlt[:], in0=sm_dlt[:], in1=sm_lnb[:]),
          waits=[("sACT", spl), ("sDVE", r0)], incs=[("sDVE", 1)])
        dlt_done = C("sDVE")
        E("dve", lambda e: e.tensor_mul(out=sm_a[:], in0=sm_dlt[:], in1=aB[:]),
          waits=[("sDVE", dlt_done)], incs=[("sDVE", 1)])
        dal_done = C("sDVE")
        E("dve", lambda e: e.tensor_mul(out=sm_dB[:], in0=sm_dlt[:], in1=sm_b[:]),
          waits=[("sDVE", dlt_done)], incs=[("sDVE", 1)])
        db_done = C("sDVE")
        E("act", lambda e: e.activation(out=sm_lnb[:], in_=sm_dB[:], func=AF.Abs),
          waits=[("sDVE", db_done)], incs=[("sACT", 1)])
        E("act", lambda e: e.activation(out=sm_lnb[:], in_=sm_lnb[:], func=AF.Ln),
          waits=[("sACT", C("sACT"))], incs=[("sACT", 1)])
        lnb_done = C("sACT")
        E("act", lambda e, par=par: e.activation(
            out=sm_sgn[:, par * NS:(par + 1) * NS], in_=sm_dB[:], func=AF.Sign),
          waits=[("sDVE", db_done)], incs=[("sACT", 1)])
        sgn_done = C("sACT")
        E("pe", lambda e: e.matmul(out=psC[:, 0:16], lhsT=triu[:], rhs=sm_a[:],
                                   start=True, stop=True),
          waits=[("sDVE", dal_done)], incs=[("sPE", 1)])
        E("pe", lambda e: e.matmul(out=psC[0:16, 16:144], lhsT=sm_a[:], rhs=triu[:],
                                   start=True, stop=True),
          waits=[("sPE", C("sPE"))], incs=[("sPE", 1)])
        E("pe", lambda e: e.matmul(out=psC[0:16, 144:145], lhsT=sm_a[:], rhs=onescol[:],
                                   start=True, stop=True),
          waits=[("sPE", C("sPE"))], incs=[("sPE", 1)])
        E("pe", lambda e: e.matmul(out=psC[0:1, 145:161], lhsT=onescol[:], rhs=sm_a[:],
                                   start=True, stop=True),
          waits=[("sPE", C("sPE"))], incs=[("sPE", 1)])
        cum_mm = C("sPE")
        E("dve", lambda e: e.tensor_copy(out=sm_cum[:], in_=psC[:, 0:16]),
          waits=[("sPE", cum_mm)], incs=[("sDVE", 1)])
        E("dve", lambda e: e.tensor_copy(out=cumT_e[:], in_=psC[0:16, 16:144]),
          waits=[("sDVE", C("sDVE"))], incs=[("sDVE", 1)])
        E("dve", lambda e: e.tensor_copy(out=clF[:], in_=psC[0:1, 145:161]),
          waits=[("sDVE", C("sDVE"))], incs=[("sDVE", 1)])
        cum_ev = C("sDVE")
        E("act", lambda e, par=par: e.activation(
            out=dec_c[:, par:par + 1], in_=psC[0:16, 144:145], func=AF.Exp),
          waits=[("sPE", cum_mm)], incs=[("sACT", 1)])
        dec_done = C("sACT")
        E("pe", lambda e: e.matmul(out=psC[:, 161:177], lhsT=ones1f[:], rhs=clF[:],
                                   start=True, stop=True),
          waits=[("sDVE", cum_ev), ("sACT", dec_done)], incs=[("sPE", 1)])
        bclmm = C("sPE")
        E("dve", lambda e: e.tensor_sub(out=sm_b[:], in0=psC[:, 161:177], in1=sm_cum[:]),
          waits=[("sPE", bclmm), ("sDVE", cum_ev)], incs=[("sDVE", 1)])
        bed = C("sDVE")
        E("act", lambda e: e.activation(out=sm_b[:], in_=sm_b[:], func=AF.Exp),
          waits=[("sDVE", bed)], incs=[("sACT", 1)])
        be2 = C("sACT")
        E("dve", lambda e, par=par: e.tensor_mul(
            out=btok[:, par * NS:(par + 1) * NS], in0=sm_dB[:], in1=sm_b[:]),
          waits=[("sACT", be2), ("sDVE", db_done)], incs=[("sDVE", 1)])
        btok_done = C("sDVE")
        E("act", lambda e: e.activation(out=sm_a[:], in_=sm_cum[:], func=AF.Exp),
          waits=[("sDVE", cum_ev)], incs=[("sACT", 1)])
        ecd = C("sACT")
        E("dve", lambda e: e.tensor_mul(out=sm_ctl[:], in0=sm_cbf[:], in1=sm_a[:]),
          waits=[("sACT", ecd)], incs=[("sDVE", 1)])
        ctl = C("sDVE")
        E("pe", lambda e: e.transpose(out=psC[0:16, 177:305], in_=sm_ctl[:],
                                      identity=identf[:]),
          waits=[("sDVE", ctl), ("sDVE", bed), ("sACT", dec_done)],
          incs=[("sPE", 1)])
        ctTm = C("sPE")
        E("dve", lambda e, par=par: e.tensor_copy(
            out=ctT[:, par * Q:(par + 1) * Q], in_=psC[0:16, 177:305]),
          waits=[("sPE", ctTm)], incs=[("sDVE", 1)])
        ctT_done = C("sDVE")
        E("pe", lambda e: e.transpose(out=psT[0:16, 0:128], in_=sm_cbf[:],
                                      identity=identb[:]),
          waits=[("sDVE", smalls0), ("sDVE", psT_evac[0])], incs=[("sPE", 1)])
        cTm = C("sPE")
        E("dve", lambda e: e.tensor_copy(out=cb16[:], in_=psT[0:16, 0:128]),
          waits=[("sPE", cTm)], incs=[("sDVE", 1)])
        psT_evac[0] = C("sDVE")
        cb_done = C("sDVE")
        E("sy", lambda e: e.dma_start(out=fcumH, in_=cumT_e[:]),
          waits=[("sDVE", cum_ev), ("sPE", state["flt_cons"])], incs=[("qSY", 16)])
        E("sy", lambda e: e.dma_start(
            out=flt[:], in_=fcumH.rearrange("n t -> (n t)")[None, :]),
          waits=[("qSY", C("qSY"))], incs=[("qSY", 16)])
        flt_ld = C("qSY")
        E("sy", lambda e: e.dma_start(out=fcH, in_=cb16[:]),
          waits=[("sDVE", cb_done), ("sPE", state["fltc_cons"])], incs=[("qSY", 16)])
        E("sy", lambda e: e.dma_start(
            out=fltc[:], in_=fcH.rearrange("n t -> (n t)")[None, :]),
          waits=[("qSY", C("qSY"))], incs=[("qSY", 16)])
        fltc_ld = C("qSY")
        E("dve", lambda e, par=par: e.tensor_sub(
            out=sm_mbias[:, par * NS:(par + 1) * NS], in0=sm_lnb[:], in1=sm_cum[:]),
          waits=[("sACT", lnb_done), ("sDVE", cum_ev)], incs=[("sDVE", 1)])
        mbias_done = C("sDVE")
        SM[c] = dict(flt_ld=flt_ld, fltc_ld=fltc_ld, mbias=mbias_done,
                     sgn=sgn_done, ctT=ctT_done, btok=btok_done, dec=dec_done)

    emit_stats(0)
    for c in range(NCHUNK):
        hg = c // 2
        szpar = hg % 2
        par = c % 2
        st_m = SM[c]
        E("sy", lambda e, c=c: e.dma_start(out=ut4[:], in_=utokH[c]),
          waits=[("qPL", utok_stored[c]), ("sPE", state["utok_cons"])],
          incs=[("qSY", 16)])
        ut_ld = C("qSY")
        E("sy", lambda e, c=c: e.dma_start(
            out=uf4[:], in_=ufmH[c].rearrange("p et t -> p (et t)")),
          waits=[("qPL", ufm_stored[c // 4]), ("sDVE", state["ufm_cons"])],
          incs=[("qSY", 16)])
        uf_ld = C("qSY")

        for et in range(NT):
            wts = [("sDVE", st_m["ctT"]), ("sDVE", state["prev_h_done"])]
            if et == 0:
                wts.append(("sACT", state["psB_z_done"]))
                wts.append(("sDVE", state["psB_evac_dve"]))
            E("pe", lambda e, et=et, par=par: e.matmul(
                out=psB[:, et * 128:(et + 1) * 128],
                lhsT=hst[:, et * 128:(et + 1) * 128],
                rhs=ctT[:, par * Q:(par + 1) * Q],
                start=(et % 4 == 0), stop=False),
              waits=wts, incs=[("sPE", 1)])
        y2done = C("sPE")

        for hh in range(2):
            for jj in range(2):
                wts = [("qSY", st_m["flt_ld"])]
                if hh == 0 and jj == 0:
                    wts.append(("sDVE", state["psA_free"]))
                if hh == 1 and jj == 0:
                    wts.append(("sDVE", C("sDVE")))
                E("pe", lambda e, hh=hh, jj=jj: e.matmul(
                    out=psA[:, jj * 512:(jj + 1) * 512], lhsT=ones1f[:],
                    rhs=flt[0:1, hh * 1024 + jj * 512:hh * 1024 + (jj + 1) * 512],
                    start=True, stop=True),
                  waits=wts, incs=[("sPE", 1)])
            bctm = C("sPE")
            state["flt_cons"] = bctm
            E("dve", lambda e, hh=hh: e.tensor_add(
                out=mb1[:, hh * 1024:(hh + 1) * 1024].rearrange("p (n t) -> p n t", n=8),
                in0=psA[:, 0:1024].rearrange("p (n t) -> p n t", n=8),
                in1=mneg[:, None, :].broadcast_to([128, 8, 128])),
              waits=[("sPE", bctm)], incs=[("sDVE", 1)])
            tmpd = C("sDVE")
            for ni in range(8):
                n = hh * 8 + ni
                E("act", lambda e, n=n, par=par: e.activation(
                    out=mb2[:, n * 128:(n + 1) * 128],
                    in_=mb1[:, n * 128:(n + 1) * 128], func=AF.Exp,
                    bias=sm_mbias[:, par * NS + n:par * NS + n + 1]),
                  waits=[("sDVE", tmpd), ("sDVE", st_m["mbias"])], incs=[("sACT", 1)])
            expa = C("sACT")
            E("pool", lambda e, hh=hh, par=par: e.tensor_mul(
                out=mb1[:, hh * 1024:(hh + 1) * 1024].rearrange(
                    "p (n t) -> p n t", n=8),
                in0=mb2[:, hh * 1024:(hh + 1) * 1024].rearrange(
                    "p (n t) -> p n t", n=8),
                in1=sm_sgn[:, par * NS + hh * 8:par * NS + (hh + 1) * 8][:, :, None
                    ].broadcast_to([128, 8, 128])),
              waits=[("sACT", expa), ("sACT", st_m["sgn"])], incs=[("sPOOL", 1)])
            expd = C("sPOOL")
            for jj in range(2):
                E("pe", lambda e, hh=hh, jj=jj: e.matmul(
                    out=psA[:, jj * 512:(jj + 1) * 512], lhsT=ones1b[:],
                    rhs=fltc[0:1, hh * 1024 + jj * 512:hh * 1024 + (jj + 1) * 512],
                    start=True, stop=True),
                  waits=([("qSY", st_m["fltc_ld"]), ("sDVE", tmpd)]
                         if jj == 0 else []),
                  incs=[("sPE", 1)])
            bccm = C("sPE")
            state["fltc_cons"] = bccm
            E("dve", lambda e, hh=hh: e.tensor_mul(
                out=mb2[:, hh * 1024:(hh + 1) * 1024],
                in0=mb1[:, hh * 1024:(hh + 1) * 1024], in1=psA[:, 0:1024]),
              waits=[("sPE", bccm), ("sPOOL", expd)], incs=[("sDVE", 1)])
        X1_done = C("sDVE")
        state["prev_X1_done"] = X1_done
        E("dve", lambda e: e.tensor_reduce(
            out=msb[:], in_=mb2[:].rearrange("p (n t) -> p t n", n=16),
            axis=AX.X, op=ALU.add),
          waits=[("sDVE", X1_done)], incs=[("sDVE", 1)])
        E("dve", lambda e: e.tensor_copy(out=mbf[:], in_=msb[:]),
          waits=[("sDVE", C("sDVE"))], incs=[("sDVE", 1)])
        mbf_done = C("sDVE")

        if c + 1 < NCHUNK:
            emit_stats(c + 1)

        for et in range(NT):
            E("pe", lambda e, et=et: e.matmul(
                out=psB[:, et * 128:(et + 1) * 128],
                lhsT=ut4[:, et * 128:(et + 1) * 128], rhs=mbf[:],
                start=False, stop=True),
              waits=[("sDVE", mbf_done), ("qSY", ut_ld), ("sPE", y2done)],
              incs=[("sPE", 1)])
        y1done = C("sPE")

        hmarks = []
        for rr in range(4):
            base = (rr % 2) * 512
            wts = [("qSY", ut_ld), ("sDVE", st_m["btok"]), ("sDVE", X1_done)]
            if rr >= 2:
                wts.append(("sDVE", hmarks[rr - 2]))
            E("pe", lambda e, rr=rr, base=base, par=par: e.matmul(
                out=psA[0:16, base:base + 512], lhsT=btok[:, par * NS:(par + 1) * NS],
                rhs=ut4[:, rr * 512:(rr + 1) * 512], start=True, stop=True),
              waits=wts, incs=[("sPE", 1)])
            dhm = C("sPE")
            E("dve", lambda e, rr=rr, base=base, par=par: e.scalar_tensor_tensor(
                out=hst[:, rr * 512:(rr + 1) * 512],
                in0=hst[:, rr * 512:(rr + 1) * 512], scalar=dec_c[:, par:par + 1],
                in1=psA[0:16, base:base + 512], op0=ALU.mult, op1=ALU.add),
              waits=[("sPE", dhm), ("sACT", st_m["dec"]), ("sPE", y2done)],
              incs=[("sDVE", 1)])
            hmarks.append(C("sDVE"))
        h_done = C("sDVE")
        state["prev_h_done"] = h_done
        state["psA_free"] = h_done
        utok_cons_mark = C("sPE")
        state["utok_cons"] = utok_cons_mark

        for et in range(NT):
            E("dve", lambda e, et=et: e.scalar_tensor_tensor(
                out=mb1[:, et * 128:(et + 1) * 128],
                in0=uf4[:, et * 128:(et + 1) * 128], scalar=dfm[:, et:et + 1],
                in1=psB[:, et * 128:(et + 1) * 128], op0=ALU.mult, op1=ALU.add),
              waits=[("sPE", y1done), ("qSY", uf_ld), ("sPOOL", expd)],
              incs=[("sDVE", 1)])
            dd = C("sDVE")
            E("pool", lambda e, et=et, c=c, szpar=szpar: e.tensor_mul(
                out=yzb[:, et * 128:(et + 1) * 128],
                in0=mb1[:, et * 128:(et + 1) * 128],
                in1=szf[:, szpar * NT * 256 + et * 256 + (c % 2) * 128:
                        szpar * NT * 256 + et * 256 + (c % 2) * 128 + 128]),
              waits=[("sDVE", dd), ("sACT", sz_ready[hg]),
                     ("qSY", state["yzb_stored"])],
              incs=[("sPOOL", 1)])
        psB_evac = C("sDVE")
        state["psB_evac_dve"] = psB_evac
        yz_done = C("sPOOL")
        szf_cons[szpar] = yz_done
        state["ufm_cons"] = psB_evac
        E("sy", lambda e, c=c: e.dma_start(out=yzH[c], in_=yzb[:]),
          waits=[("sPOOL", yz_done)], incs=[("qSY", 16)])
        state["yzb_stored"] = C("qSY")

        if c % 2 == 1 and hg + 1 < NHG:
            nxt = 1 - szpar
            sz_ready[hg + 1] = emit_zhalf(hg + 1, nxt, psB_evac, szf_cons[nxt])
            state["psB_z_done"] = sz_ready[hg + 1]
        else:
            state["psB_z_done"] = C("sACT")

    if PHASES == "p4":
        return finalize()
    b.barrier()

    CAP = 1024
    xsb = at("xsb", 10240, [128, 2 * DM], BF16)
    xsTb = at("xsTb", 14336, [128, 2 * DM], BF16)
    xnb5 = at("xnb5", 18432, [128, 2 * DM], BF16)
    cnt8s = at("cnt8s", 22848, [8, NCHUNK], F32)
    z832 = at("z832", 23040, [8, NCHUNK], F32)
    bceF = at("bceF", 23296, [1, 8 * NCHUNK], F32)
    bce32 = at("bce32", 24256, [NCHUNK, 8], F32)
    part1 = at("part1", 24384, [128, NCHUNK], F32)
    ohall = at("ohall", 24512, [128, NCHUNK * 8], F32)

    psc_read = 0
    for cb in range(NCHUNK // 4):
        for ci in range(4):
            c = cb * 4 + ci
            for kc in range(8):
                wts = [("sDVE", psc_read)] if (ci == 0 and kc == 0) else []
                E("pe", lambda e, c=c, ci=ci, kc=kc: e.matmul(
                    out=psC[:, ci * 8:(ci + 1) * 8],
                    lhsT=xnT[:, kc * L + c * 128:kc * L + (c + 1) * 128],
                    rhs=gwT[:, kc * 8:(kc + 1) * 8],
                    start=(kc == 0), stop=(kc == 7)),
                  waits=wts, incs=([("sPE", 1)] if kc == 7 else []))
        lg = C("sPE")
        E("dve", lambda e: e.tensor_reduce(
            out=rmax4[:], in_=psC[:, 0:32].rearrange("p (c e) -> p c e", c=4),
            axis=AX.X, op=ALU.max),
          waits=[("sPE", lg)], incs=[("sDVE", 1)])
        rm = C("sDVE")
        E("dve", lambda e: e.tensor_tensor(
            out=ohb[:].rearrange("p (c e) -> p c e", c=4),
            in0=psC[:, 0:32].rearrange("p (c e) -> p c e", c=4),
            in1=rmax4[:, :, None].broadcast_to([128, 4, 8]), op=ALU.is_ge),
          waits=[("sDVE", rm)], incs=[("sDVE", 1)])
        E("dve", lambda e: e.tensor_mul(out=ohb[:], in0=ohb[:], in1=wdec[:]),
          waits=[("sDVE", C("sDVE"))], incs=[("sDVE", 1)])
        E("dve", lambda e: e.tensor_reduce(
            out=rmax4[:], in_=ohb[:].rearrange("p (c e) -> p c e", c=4),
            axis=AX.X, op=ALU.max),
          waits=[("sDVE", C("sDVE"))], incs=[("sDVE", 1)])
        E("dve", lambda e, cb=cb: e.tensor_tensor(
            out=ohall[:, cb * 32:(cb + 1) * 32].rearrange("p (c e) -> p c e", c=4),
            in0=ohb[:].rearrange("p (c e) -> p c e", c=4),
            in1=rmax4[:, :, None].broadcast_to([128, 4, 8]), op=ALU.is_ge),
          waits=[("sDVE", C("sDVE"))], incs=[("sDVE", 1)])
        ohf_done = C("sDVE")
        E("dve", lambda e: e.tensor_reduce(
            out=rmax4[:], in_=psC[:, 0:32].rearrange("p (c e) -> p c e", c=4),
            axis=AX.X, op=ALU.max),
          waits=[("sDVE", ohf_done)], incs=[("sDVE", 1)])
        rm2 = C("sDVE")
        E("act", lambda e: e.activation(out=rmax4[:], in_=rmax4[:], func=AF.Tanh,
                                        scale=0.5),
          waits=[("sDVE", rm2)], incs=[("sACT", 1)])
        tw = C("sACT")
        E("dve", lambda e, cb=cb: e.tensor_scalar(
            out=rwall[:, cb * 4:(cb + 1) * 4], in0=rmax4[:], scalar1=0.5,
            scalar2=0.5, op0=ALU.mult, op1=ALU.add),
          waits=[("sACT", tw)], incs=[("sDVE", 1)])
        for ci in range(4):
            c = cb * 4 + ci
            E("pe", lambda e, c=c, ci=ci: e.matmul(
                out=psC[:, 32 + ci * 8:32 + (ci + 1) * 8], lhsT=tris[:],
                rhs=ohall[:, c * 8:(c + 1) * 8], start=True, stop=True),
              waits=[("sDVE", ohf_done)], incs=[("sPE", 1)])
            E("pe", lambda e, c=c: e.matmul(
                out=psC[0:8, 64 + c:65 + c], lhsT=ohall[:, c * 8:(c + 1) * 8],
                rhs=onescol[:], start=True, stop=True),
              waits=[("sPE", C("sPE"))], incs=[("sPE", 1)])
        rkm = C("sPE")
        E("dve", lambda e: e.tensor_tensor(
            out=ohb[:].rearrange("p (c e) -> p c e", c=4),
            in0=psC[:, 32:64].rearrange("p (c e) -> p c e", c=4),
            in1=ecapB[:, None, :].broadcast_to([128, 4, 8]), op=ALU.add),
          waits=[("sPE", rkm)], incs=[("sDVE", 1)])
        E("dve", lambda e: e.tensor_tensor(
            out=ohb[:].rearrange("p (c e) -> p c e", c=4),
            in0=ohb[:].rearrange("p (c e) -> p c e", c=4),
            in1=limB[:, None, :].broadcast_to([128, 4, 8]), op=ALU.min),
          waits=[("sDVE", C("sDVE"))], incs=[("sDVE", 1)])
        E("dve", lambda e, cb=cb: e.tensor_mul(
            out=ohb[:], in0=ohb[:], in1=ohall[:, cb * 32:(cb + 1) * 32]),
          waits=[("sDVE", C("sDVE"))], incs=[("sDVE", 1)])
        E("dve", lambda e, cb=cb: e.tensor_reduce(
            out=part1[:, cb * 4:(cb + 1) * 4],
            in_=ohb[:].rearrange("p (c e) -> p c e", c=4), axis=AX.X, op=ALU.add),
          waits=[("sDVE", C("sDVE"))], incs=[("sDVE", 1)])
        psc_read = C("sDVE")

    E("dve", lambda e: e.tensor_copy(out=cnt8s[:], in_=psC[0:8, 64:64 + NCHUNK]),
      waits=[("sDVE", psc_read), ("sPE", C("sPE"))], incs=[("sDVE", 1)])
    E("dve", lambda e: e.memset(z832[:], 0.0),
      waits=[("sDVE", C("sDVE"))], incs=[("sDVE", 1)])
    E("dve", lambda e: e.tensor_tensor_scan(
        out=z832[:], data0=cnt8s[:], data1=z832[:], initial=0.0,
        op0=ALU.add, op1=ALU.add),
      waits=[("sDVE", C("sDVE"))], incs=[("sDVE", 1)])
    E("dve", lambda e: e.tensor_sub(out=cnt8s[:], in0=z832[:], in1=cnt8s[:]),
      waits=[("sDVE", C("sDVE"))], incs=[("sDVE", 1)])
    bce_done = C("sDVE")
    E("pe", lambda e: e.transpose(out=psC[0:NCHUNK, 104:112], in_=cnt8s[:],
                                  identity=identf[0:8, 0:8]),
      waits=[("sDVE", bce_done)], incs=[("sPE", 1)])
    bcet = C("sPE")
    E("dve", lambda e: e.tensor_copy(out=bce32[:], in_=psC[0:NCHUNK, 104:112]),
      waits=[("sPE", bcet)], incs=[("sDVE", 1)])
    bce32d = C("sDVE")
    E("pool", lambda e: e.dma_start(out=bceH, in_=bce32[:]),
      waits=[("sDVE", bce32d)], incs=[("qPL", 16)])
    E("pool", lambda e: e.dma_start(
        out=bceF[:], in_=bceH.rearrange("c e -> (c e)")[None, :]),
      waits=[("qPL", C("qPL"))], incs=[("qPL", 16)])
    bceF_ld = C("qPL")
    slot_marks = []
    for c in range(NCHUNK):
        E("pe", lambda e, c=c: e.matmul(
            out=psC[:, 96:104], lhsT=ones1f[:], rhs=bceF[0:1, c * 8:(c + 1) * 8],
            start=True, stop=True),
          waits=[("qPL", bceF_ld), ("sDVE", C("sDVE"))], incs=[("sPE", 1)])
        bcm = C("sPE")
        E("dve", lambda e, c=c: e.tensor_mul(
            out=oh8[:], in0=psC[:, 96:104], in1=ohall[:, c * 8:(c + 1) * 8]),
          waits=[("sPE", bcm)], incs=[("sDVE", 1)])
        E("dve", lambda e: e.tensor_reduce(out=rmaxb[:], in_=oh8[:], axis=AX.X,
                                           op=ALU.add),
          waits=[("sDVE", C("sDVE"))], incs=[("sDVE", 1)])
        E("dve", lambda e, c=c: e.tensor_add(out=rmaxb[:], in0=rmaxb[:],
                                             in1=part1[:, c:c + 1]),
          waits=[("sDVE", C("sDVE"))], incs=[("sDVE", 1)])
        E("dve", lambda e, c=c: e.tensor_copy(out=slotI[:, c:c + 1], in_=rmaxb[:]),
          waits=[("sDVE", C("sDVE"))], incs=[("sDVE", 1)])
        slot_marks.append(C("sDVE"))
    slots_done = C("sDVE")

    xs_scattered = 0
    xnb5_war = [0, 0]
    xsb_war = [0, 0]
    for c in range(NCHUNK):
        par = c % 2
        for g2 in range(2):
            for s4 in range(4):
                kc = g2 * 4 + s4
                E("pe", lambda e, kc=kc, c=c, s4=s4: e.transpose(
                    out=psT[:, s4 * 128:(s4 + 1) * 128],
                    in_=xnT[:, kc * L + c * 128:kc * L + (c + 1) * 128],
                    identity=identb[:]),
                  waits=[("sDVE", psT_evac[0]), ("sDVE", xnb5_war[par])],
                  incs=[("sPE", 1)])
            td = C("sPE")
            for s4 in range(4):
                kc = g2 * 4 + s4
                E("dve", lambda e, kc=kc, s4=s4, par=par: e.tensor_copy(
                    out=xnb5[:, par * DM + kc * 128:par * DM + (kc + 1) * 128],
                    in_=psT[:, s4 * 128:(s4 + 1) * 128]),
                  waits=[("sPE", td)], incs=[("sDVE", 1)])
            psT_evac[0] = C("sDVE")
        xn_tr = C("sDVE")
        E("dve", lambda e, c=c, par=par: e.tensor_scalar_mul(
            out=xsb[:, par * DM:(par + 1) * DM], in0=xnb5[:, par * DM:(par + 1) * DM],
            scalar1=rwall[:, c:c + 1]),
          waits=[("sDVE", xn_tr), ("qPL", xsb_war[par]), ("sDVE", slots_done)],
          incs=[("sDVE", 1)])
        xnb5_war[par] = C("sDVE")
        xsd = C("sDVE")
        if SKIP_SCATTER:
            E("pool", lambda e, c=c: e.dma_start(
                out=xsH[c * 128:(c + 1) * 128, :],
                in_=xsb[:, par * DM:(par + 1) * DM]),
              waits=[("sDVE", xsd)], incs=[("qPL", 16)])
        else:
            E("pool", lambda e, c=c, par=par: e.indirect_dma_start(
                out=xsH, out_offset=bass.IndirectOffsetOnAxis(slotI[:, c:c + 1], 0),
                in_=xsb[:, par * DM:(par + 1) * DM], in_offset=None),
              waits=[("sDVE", xsd)], incs=[("qPL", 16)])
        xsb_war[par] = C("qPL")
        xs_scattered = C("qPL")

    def load_expert_sp(e_idx, slot, war_pe):
        cm = [C("sPOOL")]
        slot_cast = [C("sPOOL"), C("sPOOL")]
        marks = []
        for (wsrc, dst_base, nkc, wid) in [
            (wgT_h, 0, 8, 512), (wuT_h, 4096, 8, 512), (wdT_h, 8192, 4, 1024)]:
            for kc in range(nkc):
                sl = stage[:, (kc % 2) * 1024:(kc % 2) * 1024 + wid]
                E("sy", lambda e, e_idx=e_idx, kc=kc, sl=sl, wsrc=wsrc: e.dma_start(
                    out=sl, in_=wsrc[e_idx, kc * 128:(kc + 1) * 128, :]),
                  waits=[("sPOOL", slot_cast[kc % 2])], incs=[("qSY", 16)])
                if dst_base < 8192:
                    E("pool", lambda e, kc=kc, slot=slot, sl=sl, dst_base=dst_base,
                      wid=wid: e.tensor_copy(
                        out=WA[:, slot * 8192 + dst_base + kc * wid:
                                slot * 8192 + dst_base + (kc + 1) * wid], in_=sl),
                      waits=[("qSY", C("qSY")), ("sPE", war_pe)], incs=[("sPOOL", 1)])
                else:
                    E("pool", lambda e, kc=kc, slot=slot, sl=sl, wid=wid: e.tensor_copy(
                        out=WB[:, slot * 4096 + kc * wid:slot * 4096 + (kc + 1) * wid],
                        in_=sl),
                      waits=[("qSY", C("qSY")), ("sPE", war_pe)], incs=[("sPOOL", 1)])
                cm[0] = C("sPOOL")
                slot_cast[kc % 2] = C("sPOOL")
            marks.append(cm[0])
        return marks

    ew = {}
    ew[0] = load_expert_sp(0, 0, 0)
    arena_pe = [0, 0]
    gs_stored = 0
    xsb_war5 = [0, 0]
    down_evac = 0
    for e_i in range(8):
        slot = e_i % 2
        wm = ew[e_i]
        if e_i + 1 < 8:
            ew[e_i + 1] = load_expert_sp(e_i + 1, 1 - slot, arena_pe[1 - slot])
        for r in range(8):
            par = r % 2
            E("sy", lambda e, e_i=e_i, r=r, par=par: e.dma_start(
                out=xsb[:, par * DM:(par + 1) * DM],
                in_=xsH[e_i * CAP + r * 128:e_i * CAP + (r + 1) * 128, :]),
              waits=[("qPL", xs_scattered), ("sPE", xsb_war5[par])],
              incs=[("qSY", 16)])
            xs_ld = C("qSY")
            for g in range(2):
                for s4 in range(4):
                    fc = g * 4 + s4
                    E("pe", lambda e, fc=fc, s4=s4, par=par: e.transpose(
                        out=psT[:, s4 * 128:(s4 + 1) * 128],
                        in_=xsb[:, par * DM + fc * 128:par * DM + (fc + 1) * 128],
                        identity=identb[:]),
                      waits=[("qSY", xs_ld), ("sDVE", psT_evac[0])],
                      incs=[("sPE", 1)])
                td = C("sPE")
                for s4 in range(4):
                    fc = g * 4 + s4
                    E("dve", lambda e, fc=fc, s4=s4, par=par: e.tensor_copy(
                        out=xsTb[:, par * DM + fc * 128:par * DM + (fc + 1) * 128],
                        in_=psT[:, s4 * 128:(s4 + 1) * 128]),
                      waits=[("sPE", td)], incs=[("sDVE", 1)])
                psT_evac[0] = C("sDVE")
            xsT_done = C("sDVE")
            for kc in range(8):
                wts = [("sPOOL", wm[0]), ("sDVE", xsT_done)]
                if kc == 0:
                    wts.append(("sACT", C("sACT")))
                E("pe", lambda e, kc=kc, slot=slot, par=par: e.matmul(
                    out=psB[:, 1024:1536],
                    lhsT=xsTb[:, par * DM + kc * 128:par * DM + (kc + 1) * 128],
                    rhs=WA[:, slot * 8192 + kc * 512:slot * 8192 + (kc + 1) * 512],
                    start=(kc == 0), stop=(kc == 7)),
                  waits=wts, incs=([("sPE", 1)] if kc == 7 else []))
            gmm = C("sPE")
            for kc in range(8):
                wts = [("sPOOL", wm[1]), ("sDVE", xsT_done)]
                if kc == 0:
                    wts.append(("sDVE", C("sDVE")))
                E("pe", lambda e, kc=kc, slot=slot, par=par: e.matmul(
                    out=psB[:, 1536:2048],
                    lhsT=xsTb[:, par * DM + kc * 128:par * DM + (kc + 1) * 128],
                    rhs=WA[:, slot * 8192 + 4096 + kc * 512:
                            slot * 8192 + 4096 + (kc + 1) * 512],
                    start=(kc == 0), stop=(kc == 7)),
                  waits=wts, incs=([("sPE", 1)] if kc == 7 else []))
            umm = C("sPE")
            xsb_war5[par] = umm
            E("act", lambda e: e.activation(out=sgb[:], in_=psB[:, 1024:1536],
                                            func=AF.Silu),
              waits=[("sPE", gmm)], incs=[("sACT", 1)])
            sgd = C("sACT")
            E("act", lambda e: e.activation(out=upb[:], in_=psB[:, 1536:2048],
                                            func=AF.Copy),
              waits=[("sPE", umm)], incs=[("sACT", 1)])
            upd = C("sACT")
            E("dve", lambda e: e.tensor_mul(out=pbf_[:], in0=sgb[:], in1=upb[:]),
              waits=[("sACT", sgd), ("sACT", upd), ("sPE", C("sPE"))],
              incs=[("sDVE", 1)])
            pd = C("sDVE")
            for tq in range(4):
                E("pe", lambda e, tq=tq: e.transpose(
                    out=psT[:, tq * 128:(tq + 1) * 128],
                    in_=pbf_[:, tq * 128:(tq + 1) * 128], identity=identb[:]),
                  waits=[("sDVE", pd), ("sDVE", psT_evac[0])], incs=[("sPE", 1)])
            td = C("sPE")
            for tq in range(4):
                E("dve", lambda e, tq=tq: e.tensor_copy(
                    out=pTb[:, tq * 128:(tq + 1) * 128],
                    in_=psT[:, tq * 128:(tq + 1) * 128]),
                  waits=[("sPE", td)], incs=[("sDVE", 1)])
            psT_evac[0] = C("sDVE")
            ptd = C("sDVE")
            for kd in range(4):
                for jj in range(2):
                    wts = [("sDVE", ptd), ("sPOOL", wm[2])]
                    if kd == 0 and jj == 0:
                        wts.append(("sACT", down_evac))
                    E("pe", lambda e, kd=kd, jj=jj, slot=slot: e.matmul(
                        out=psB[:, jj * 512:(jj + 1) * 512],
                        lhsT=pTb[:, kd * 128:(kd + 1) * 128],
                        rhs=WB[:, slot * 4096 + kd * 1024 + jj * 512:
                                slot * 4096 + kd * 1024 + (jj + 1) * 512],
                        start=(kd == 0), stop=(kd == 3)),
                      waits=wts, incs=([("sPE", 1)] if (kd == 3 and jj == 1) else []))
            dmm = C("sPE")
            arena_pe[slot] = dmm
            E("act", lambda e: e.activation(out=ffnacc[:].bitcast(BF16)[:, 0:DM],
                                            in_=psB[:, 0:1024], func=AF.Silu),
              waits=[("sPE", dmm), ("qSC", gs_stored)], incs=[("sACT", 1)])
            gsa = C("sACT")
            down_evac = gsa
            E("act", lambda e, e_i=e_i, r=r: e.dma_start(
                out=gsH[e_i * CAP + r * 128:e_i * CAP + (r + 1) * 128, :],
                in_=ffnacc[:].bitcast(BF16)[:, 0:DM]),
              waits=[("sACT", gsa)], incs=[("qSC", 16)])
            gs_stored = C("qSC")
    gs_all = gs_stored
    if PHASES == "pb" and DBG_SCRATCH:
        E("dve", lambda e: e.tensor_copy(out=sgb[:, 0:32], in_=slotI[:]),
          waits=[("sDVE", C("sDVE"))], incs=[("sDVE", 1)])
        sf = C("sDVE")
        E("sy", lambda e: e.dma_start(out=out_h[0:128, 0:32], in_=part1[:]),
          waits=[("sDVE", sf)], incs=[("qSC", 16)])
        E("sy", lambda e: e.dma_start(out=out_h[0:128, 32:288], in_=ohall[:]),
          waits=[("sDVE", sf)], incs=[("qSC", 16)])
        E("sy", lambda e: e.dma_start(out=out_h[0:128, 288:320], in_=rwall[:, 0:32]),
          waits=[("sDVE", sf)], incs=[("qSC", 16)])
        E("sy", lambda e: e.dma_start(out=out_h[0:128, 320:352], in_=sgb[:, 0:32]),
          waits=[("sDVE", sf)], incs=[("qSC", 16)])
        E("sy", lambda e: e.dma_start(out=out_h[128:129, 0:256], in_=bceF[:]),
          waits=[("sDVE", sf)], incs=[("qSC", 16)])
        for rr in range(8):
            E("sy", lambda e, rr=rr: e.dma_start(
                out=out_h[512 + rr * 128:512 + (rr + 1) * 128, 0:512].bitcast(BF16),
                in_=xsH[rr * 1024:rr * 1024 + 128, :]),
              waits=[("qPL", xs_scattered)], incs=[("qSC", 16)])
        for nm, srcap, dst0 in [("xn0", xnH[0], 2000), ("xs0", xsH[0:128, :], 2128),
                                ("gs0", gsH[0:128, :], 2256), ("xs1", xsH[1024:1152, :], 2384)]:
            E("sy", lambda e, srcap=srcap: e.dma_start(
                out=xsb[:, 0:DM], in_=srcap),
              waits=[("qSC", gs_all), ("qPL", C("qPL")), ("sDVE", C("sDVE")),
                     ("qSY", C("qSY")), ("sPE", C("sPE")), ("sACT", C("sACT")),
                     ("sPOOL", C("sPOOL"))],
              incs=[("qSY", 16)])
            E("dve", lambda e: e.tensor_copy(out=ffnacc[:], in_=xsb[:, 0:DM]),
              waits=[("qSY", C("qSY"))], incs=[("sDVE", 1)])
            E("sy", lambda e, dst0=dst0: e.dma_start(
                out=out_h[dst0:dst0 + 128, :], in_=ffnacc[:]),
              waits=[("sDVE", C("sDVE"))], incs=[("qSC", 16)])
        b.wait("sy", [("qSC", C("qSC"))])
    if DBG_SCRATCH:
        E("sy", lambda e: e.dma_start(out=xsD, in_=xsH),
          waits=[("qPL", xs_scattered)], incs=[("qSY", 16)])
        E("sy", lambda e: e.dma_start(out=gsD, in_=gsH),
          waits=[("qSC", gs_all)], incs=[("qSY", 16)])
        b.cnt["qSY"] += 0

    if PHASES == "pb":
        return finalize()
    b.barrier()

    slot_cast = [C("sPOOL"), C("sPOOL")]
    for et in range(NT):
        par = et % 2
        sl = stage[:, par * 1024:(par + 1) * 1024]
        E("sy", lambda e, et=et, sl=sl: e.dma_start(
            out=sl, in_=woutT_h[et * 128:(et + 1) * 128, :]),
          waits=[("sPOOL", slot_cast[par])], incs=[("qSY", 16)])
        E("pool", lambda e, et=et, sl=sl: e.tensor_copy(
            out=WA[:, et * 1024:(et + 1) * 1024], in_=sl),
          waits=[("qSY", C("qSY"))], incs=[("sPOOL", 1)])
        slot_cast[par] = C("sPOOL")
    wout_done = C("sPOOL")

    out_stored = 0
    mm5_done = [0, 0]
    evacd5 = [0, 0]
    for c in range(NCHUNK):
        par = c % 2
        E("sy", lambda e, c=c, par=par: e.dma_start(
            out=yzin[:, par * DI:(par + 1) * DI], in_=yzH[c]),
          waits=[("sPE", mm5_done[par])], incs=[("qSY", 16)])
        yz_ld = C("qSY")
        E("sy", lambda e, c=c, par=par: e.dma_start(
            out=xt5[:, par * DM:(par + 1) * DM], in_=x_h[c * 128:(c + 1) * 128, :]),
          waits=[("sDVE", evacd5[par])], incs=[("qSY", 16)])
        x_ld = C("qSY")
        E("pool", lambda e, c=c, par=par: e.indirect_dma_start(
            out=g5[:, par * DM:(par + 1) * DM], out_offset=None,
            in_=gsH, in_offset=bass.IndirectOffsetOnAxis(slotI[:, c:c + 1], 0)),
          waits=[("qSC", gs_all), ("sDVE", evacd5[par])], incs=[("qPL", 16)])
        g_ld = C("qPL")
        for et in range(NT):
            for jj in range(2):
                wts = [("qSY", yz_ld), ("sPOOL", wout_done)]
                if et == 0 and jj == 0:
                    wts.append(("sDVE", evacd5[par]))
                E("pe", lambda e, et=et, par=par, jj=jj: e.matmul(
                    out=psB[:, par * 1024 + jj * 512:par * 1024 + (jj + 1) * 512],
                    lhsT=yzin[:, par * DI + et * 128:par * DI + (et + 1) * 128],
                    rhs=WA[:, et * 1024 + jj * 512:et * 1024 + (jj + 1) * 512],
                    start=(et == 0), stop=(et == NT - 1)),
                  waits=wts, incs=([("sPE", 1)] if (et == NT - 1 and jj == 1) else []))
        smm = C("sPE")
        mm5_done[par] = smm
        E("dve", lambda e, par=par: e.tensor_mul(
            out=outb[:, par * DM:(par + 1) * DM],
            in0=psB[:, par * 1024:(par + 1) * 1024],
            in1=g5[:, par * DM:(par + 1) * DM]),
          waits=[("sPE", smm), ("qPL", g_ld), ("qSC", out_stored)],
          incs=[("sDVE", 1)])
        E("dve", lambda e, par=par: e.tensor_add(
            out=outb[:, par * DM:(par + 1) * DM],
            in0=outb[:, par * DM:(par + 1) * DM],
            in1=xt5[:, par * DM:(par + 1) * DM]),
          waits=[("sDVE", C("sDVE")), ("qSY", x_ld)], incs=[("sDVE", 1)])
        evacd5[par] = C("sDVE")
        E("act", lambda e, c=c, par=par: e.dma_start(
            out=out_h[c * 128:(c + 1) * 128, :], in_=outb[:, par * DM:(par + 1) * DM]),
          waits=[("sDVE", evacd5[par])], incs=[("qSC", 16)])
        out_stored = C("qSC")

    b.wait("sy", [("qSC", out_stored)])
    b.wait("act", [("qSC", out_stored)])

    return finalize()


_prog_cache = {}


def prep_inputs(L0_attn_x, L0_attn_W_in, L0_attn_conv_w, L0_attn_conv_b, L0_attn_W_x,
                L0_attn_W_dt, L0_attn_b_dt, L0_attn_A_log, L0_attn_D, L0_attn_W_out,
                L0_ffn_gate_w, L0_ffn_w_gate, L0_ffn_w_up, L0_ffn_w_down):
    x = np.ascontiguousarray(np.asarray(L0_attn_x, np.float32))
    winT = np.ascontiguousarray(np.asarray(L0_attn_W_in, np.float32).T)
    wxdtT = np.ascontiguousarray(
        np.concatenate([np.asarray(L0_attn_W_x, np.float32),
                        np.asarray(L0_attn_W_dt, np.float32)], 0).T)
    woutT = np.ascontiguousarray(np.asarray(L0_attn_W_out, np.float32).T)
    cw = np.asarray(L0_attn_conv_w, np.float32)[:, 0, :]
    convw = np.ascontiguousarray(
        cw.reshape(NT, 128, 4).transpose(1, 0, 2).reshape(128, NT * 4))
    convb = np.ascontiguousarray(
        np.asarray(L0_attn_conv_b, np.float32).reshape(NT, 128).T)
    dfm = np.ascontiguousarray(np.asarray(L0_attn_D, np.float32).reshape(NT, 128).T)
    bdt = np.asarray(L0_attn_b_dt, np.float32)[None, :]
    alog = np.asarray(L0_attn_A_log, np.float32)[None, :]
    gwT = np.ascontiguousarray(np.asarray(L0_ffn_gate_w, np.float32).T)
    wgT = np.ascontiguousarray(np.asarray(L0_ffn_w_gate, np.float32).transpose(0, 2, 1))
    wuT = np.ascontiguousarray(np.asarray(L0_ffn_w_up, np.float32).transpose(0, 2, 1))
    wdT = np.ascontiguousarray(np.asarray(L0_ffn_w_down, np.float32).transpose(0, 2, 1))

    shared = dict(winT=winT, wxdtT=wxdtT, woutT=woutT, convw=convw, convb=convb,
                  dfm=dfm, bdt=bdt, alog=alog, gwT=gwT, wgT=wgT, wuT=wuT, wdT=wdT)
    return shared, x


def kernel(**inputs):
    shared, x = prep_inputs(**inputs)
    nc = build_program()
    in_maps = [dict(shared, x=np.ascontiguousarray(x[bb])) for bb in range(8)]
    res = run_bass_kernel_spmd(nc, in_maps, core_ids=list(range(8)))
    return np.stack([r["out"] for r in res.results], axis=0)


# version 15
# speedup vs baseline: 1.3802x; 1.0046x over previous
import numpy as np
from collections import defaultdict

import concourse.bass as bass
import concourse.mybir as mybir
from concourse.bass_utils import run_bass_kernel_spmd

F32 = mybir.dt.float32
BF16 = mybir.dt.bfloat16
AF = mybir.ActivationFunctionType
ALU = mybir.AluOpType
AX = mybir.AxisListType

L = 4096
DM = 1024
DI = 2048
NS = 16
Q = 128
NCHUNK = L // Q
NT = 16
TW = 512
NTW = L // TW
NHG = L // 256
PHASES = "all"
P0_LIMIT = NCHUNK
P0_SKIP = ""
SKIP_SCATTER = False
DBG_SCRATCH = False


class Builder:
    ENGS = ("sy", "pe", "act", "dve", "pool")

    def __init__(self):
        self.q = {e: [] for e in self.ENGS}
        self.cnt = defaultdict(int)
        self.sem = None

    def emit(self, eng, fn, waits=(), incs=()):
        waits = [(s, v) for (s, v) in waits if v > 0]
        for (s, v) in waits:
            assert v <= self.cnt[s], f"wait on future count {s} {v} > {self.cnt[s]}"
        incs = [(s, n) for (s, n) in incs]
        for (s, n) in incs:
            self.cnt[s] += n

        def cl(ec, _fn=fn, _w=waits, _i=incs):
            for (s, v) in _w:
                ec.wait_ge(self.sem[s], v)
            inst = _fn(ec)
            for (s, n) in _i:
                inst.then_inc(self.sem[s], n)
        self.q[eng].append(cl)
        return {s: self.cnt[s] for s, _ in incs}

    def c(self, s):
        return self.cnt[s]

    def wait(self, eng, waits):
        def cl(ec, _w=list(waits)):
            for (s, v) in _w:
                ec.wait_ge(self.sem[s], v)
        self.q[eng].append(cl)

    def barrier(self):
        targets = {}
        for e in self.ENGS:
            self.cnt["b" + e] += 1
            targets[e] = self.cnt["b" + e]
        for e in self.ENGS:
            def cl(ec, _e=e, _t=dict(targets)):
                ec.sem_inc(self.sem["b" + _e], 1)
                for oe, v in _t.items():
                    ec.wait_ge(self.sem["b" + oe], v)
            self.q[e].append(cl)


def build_program():
    nc = bass.Bass("TRN2", target_bir_lowering=False, debug=False)
    b = Builder()
    E, C = b.emit, b.c

    x_h = nc.dram_tensor("x", [L, DM], F32, kind="ExternalInput").ap()
    winT_h = nc.dram_tensor("winT", [DM, 2 * DI], F32, kind="ExternalInput").ap()
    wxdtT_h = nc.dram_tensor("wxdtT", [DI, 48], F32, kind="ExternalInput").ap()
    woutT_h = nc.dram_tensor("woutT", [DI, DM], F32, kind="ExternalInput").ap()
    convw_h = nc.dram_tensor("convw", [128, NT * 4], F32, kind="ExternalInput").ap()
    convb_h = nc.dram_tensor("convb", [128, NT], F32, kind="ExternalInput").ap()
    dfm_h = nc.dram_tensor("dfm", [128, NT], F32, kind="ExternalInput").ap()
    bdt_h = nc.dram_tensor("bdt", [1, NS], F32, kind="ExternalInput").ap()
    alog_h = nc.dram_tensor("alog", [1, NS], F32, kind="ExternalInput").ap()
    gwT_h = nc.dram_tensor("gwT", [DM, 8], F32, kind="ExternalInput").ap()
    wgT_h = nc.dram_tensor("wgT", [8, DM, 512], F32, kind="ExternalInput").ap()
    wuT_h = nc.dram_tensor("wuT", [8, DM, 512], F32, kind="ExternalInput").ap()
    wdT_h = nc.dram_tensor("wdT", [8, 512, DM], F32, kind="ExternalInput").ap()
    out_h = nc.dram_tensor("out", [L, DM], F32, kind="ExternalOutput").ap()

    KSC = "ExternalOutput" if DBG_SCRATCH else "Internal"
    ufmH = nc.dram_tensor("ufmH", [NCHUNK, 128, NT, Q], BF16, kind=KSC).ap()
    utokH = nc.dram_tensor("utokH", [NCHUNK, 128, DI], BF16, kind=KSC).ap()
    bcH = nc.dram_tensor("bcH", [48, L], F32, kind=KSC).ap()
    yzH = nc.dram_tensor("yzH", [NCHUNK, 128, DI], BF16, kind=KSC).ap()
    gH = nc.dram_tensor("gH", [NCHUNK, 128, DM], BF16, kind=KSC).ap()
    xnH = nc.dram_tensor("xnH", [NCHUNK, 128, DM], BF16, kind="Internal").ap()
    xsH = nc.dram_tensor("xsH", [8 * 1024, DM], BF16, kind="Internal").ap()
    gsH = nc.dram_tensor("gsH", [8 * 1024, DM], BF16, kind="Internal").ap()
    xsD = nc.dram_tensor("xsD", [8 * 1024, DM], BF16, kind=KSC).ap() if DBG_SCRATCH else None
    gsD = nc.dram_tensor("gsD", [8 * 1024, DM], BF16, kind=KSC).ap() if DBG_SCRATCH else None
    bceH = nc.dram_tensor("bceH", [NCHUNK, 8], F32, kind="Internal").ap()
    slotH = nc.dram_tensor("slotH", [128, NCHUNK], mybir.dt.int32, kind=KSC).ap()
    rtD = nc.dram_tensor("rtD", [128, 640], F32, kind=KSC).ap()
    partH = nc.dram_tensor("partH", [NCHUNK, 128, DM], F32, kind="Internal").ap()
    wzHd = nc.dram_tensor("wzHd", [DM, DI], BF16, kind="Internal").ap()
    fcumH = nc.dram_tensor("fcumH", [NS, Q], F32, kind="Internal").ap()
    fcH = nc.dram_tensor("fcH", [NS, Q], BF16, kind="Internal").ap()

    def sb(name, shape, dtype):
        return nc.alloc_sbuf_tensor(name, shape, dtype)

    xnT = sb("xnT", [128, 8 * L], BF16)
    hst = sb("hst", [NS, DI], F32)
    triu = sb("triu", [128, 128], F32)
    mneg = sb("mneg", [128, 128], F32)
    identb = sb("identb", [128, 128], BF16)
    identf = sb("identf", [128, 128], F32)
    onescol = sb("onescol", [128, 1], F32)
    ones1f = sb("ones1f", [1, 128], F32)
    ones1b = sb("ones1b", [1, 128], BF16)
    convw = sb("convw_s", [128, NT * 4], F32)
    convb = sb("convb_s", [128, NT], F32)
    dfm = sb("dfm_s", [128, NT], F32)
    bdt_r = sb("bdt_r", [1, NS], F32)
    alog_r = sb("alog_r", [1, NS], F32)
    arow = sb("arow", [1, NS], F32)
    bdtB = sb("bdtB", [128, NS], F32)
    aB = sb("aB", [128, NS], F32)
    wxdtT = sb("wxdtT_s", [128, NT * 48], BF16)
    gwT = sb("gwT_s", [128, 8 * 8], BF16)
    rwall = sb("rwall", [128, NCHUNK * 8], F32)
    slotI = sb("slotI", [128, NCHUNK], mybir.dt.int32)
    tris = sb("tris", [128, 128], F32)
    ecapB = sb("ecapB", [128, 8], F32)
    wdec = sb("wdec", [128, 32], F32)
    rmax4 = sb("rmax4", [128, 4], F32)
    ohb = sb("ohb", [128, 32], F32)
    limB = sb("limB", [128, 8], F32)
    stage = sb("stage", [128, 2048], F32)
    WA = sb("WA", [128, 16 * 1024], BF16)
    WB = sb("WB", [128, 16 * 1024], BF16)

    SCR = 52 * 1024
    scr = sb("scr", [128, SCR], mybir.dt.uint8)
    scr_addr = nc.lookup_mloc(scr).addr

    def at(name, off, shape, dtype):
        assert off % 64 == 0
        nbytes = int(np.prod(shape[1:])) * mybir.dt.size(dtype)
        assert off + nbytes <= SCR, f"{name} overflows scratch: {off}+{nbytes}"
        return nc.alloc_sbuf_tensor_at(name, shape, dtype, offset=scr_addr + off)

    xt0 = at("xt0", 0, [128, 2 * DM], F32)
    xnb = at("xnb", 8192, [128, DM], BF16)
    st1 = at("st1", 10240, [128, 1], F32)
    st2 = at("st2", 10304, [128, 1], F32)
    st3 = at("st3", 10368, [128, 1], F32)
    tileb = at("tileb", 0, [128, 2 * (TW + 3)], F32)
    convacc = at("convacc", 4224, [128, 2 * TW], F32)
    tails = at("tails", 8320, [128, NT * 3], F32)
    bcev = at("bcev", 8576, [48, TW], F32)
    ufm16 = at("ufm16", 10624, [128, NT * TW], BF16)
    utokb = at("utokb", 27008, [128, 2 * DI], BF16)
    convacc2 = at("convacc2", 35200, [128, 2 * TW], F32)
    tbuf2 = at("tbuf2", 39296, [128, TW], F32)
    szf = at("szf", 0, [128, 2 * NT * 256], BF16)
    ut4 = at("ut4", 16384, [128, DI], BF16)
    uf4 = at("uf4", 20480, [128, DI], BF16)
    mb1 = at("mb1", 24576, [128, NS * Q], F32)
    mb2 = at("mb2", 32768, [128, NS * Q], BF16)
    yzb = at("yzb", 36864, [128, DI], BF16)
    flt = at("flt", 40960, [1, NS * Q], F32)
    fltc = at("fltc", 49152, [1, NS * Q], BF16)
    sm_cum = sb("sm_cum", [128, NS], F32)
    sm_mbias = sb("sm_mbias", [128, 2 * NS], F32)
    sm_dB = sb("sm_dB", [128, NS], F32)
    sm_lnb = sb("sm_lnb", [128, NS], F32)
    sm_sgn = sb("sm_sgn", [128, 2 * NS], F32)
    sm_dlt = sb("sm_dlt", [128, NS], F32)
    sm_a = sb("sm_a", [128, NS], F32)
    sm_b = sb("sm_b", [128, NS], F32)
    sm_cbf = sb("sm_cbf", [128, NS], BF16)
    sm_ctl = sb("sm_ctl", [128, NS], F32)
    cumT_e = sb("cumT_e", [NS, Q], F32)
    cb16 = sb("cb16", [NS, Q], BF16)
    ctT = sb("ctT", [NS, 2 * Q], F32)
    btok = sb("btok", [128, 2 * NS], BF16)
    dec_c = sb("dec_c", [NS, 2], F32)
    clF = sb("clF", [1, NS], F32)
    msb = sb("msb", [128, Q], F32)
    mbf = sb("mbf", [128, Q], BF16)
    bc4 = sb("bc4", [48, 2 * Q], F32)
    sgb = at("sgb", 0, [128, 512], F32)
    upb = at("upb", 2048, [128, 512], F32)
    pbf_ = at("pbf_", 4096, [128, 512], BF16)
    pTb = at("pTb", 5120, [128, 512], BF16)
    ffnacc = at("ffnacc", 6144, [128, DM], F32)
    rmaxb = at("rmaxb", 22528, [128, 1], F32)
    rwb = at("rwb", 22592, [128, 1], F32)
    oh8 = at("oh8", 22656, [128, 8], F32)
    cum8 = at("cum8", 22720, [128, 8], F32)
    z8 = at("z8", 22784, [128, 8], F32)
    yzin = at("yzin", 24576, [128, 2 * DI], BF16)
    outb = at("outb", 32768, [128, 2 * DM], F32)
    xt5 = at("xt5", 40960, [128, 2 * DM], F32)
    g5 = at("g5", 49152, [128, 2 * DM], BF16)

    psA = nc.alloc_psum_tensor("psA", [128, 1024], F32)
    psB = nc.alloc_psum_tensor("psB", [128, 2048], F32)
    psC = nc.alloc_psum_tensor("psC", [128, 512], F32)
    psT = nc.alloc_psum_tensor("psT", [128, 1024], BF16)

    for sn in (["qSY", "qSC", "qPL", "sPE", "sACT", "sDVE", "sPOOL"] +
               ["b" + e for e in Builder.ENGS]):
        pass
    b.sem = {sn: nc.alloc_semaphore(sn) for sn in
             ["qSY", "qSC", "qPL", "sPE", "sACT", "sDVE", "sPOOL"] +
             ["b" + e for e in Builder.ENGS]}

    def finalize():
        b.wait("sy", [("qPL", C("qPL")), ("qSY", C("qSY")), ("qSC", C("qSC"))])
        b.barrier()
        with nc.Block() as block:
            @block.sync
            def _(eng):
                for cl in b.q["sy"]:
                    cl(eng)

            @block.tensor
            def _(eng):
                for cl in b.q["pe"]:
                    cl(eng)

            @block.scalar
            def _(eng):
                for cl in b.q["act"]:
                    cl(eng)

            @block.vector
            def _(eng):
                for cl in b.q["dve"]:
                    cl(eng)

            @block.gpsimd
            def _(eng):
                for cl in b.q["pool"]:
                    cl(eng)
        return nc

    for ap_sb, ap_h in [(convw, convw_h), (convb, convb_h), (dfm, dfm_h),
                        (bdt_r, bdt_h), (alog_r, alog_h)]:
        E("sy", lambda e, a=ap_sb, h=ap_h: e.dma_start(out=a[:], in_=h),
          incs=[("qSY", 16)])
    small_in = C("qSY")

    E("pool", lambda e: e.memset(triu[:], 1.0), incs=[("sPOOL", 1)])
    E("pool", lambda e: e.affine_select(out=triu[:], in_=triu[:], compare_op=ALU.is_ge,
                                        fill=0.0, base=0, pattern=[[1, 128]],
                                        channel_multiplier=-1),
      waits=[("sPOOL", C("sPOOL"))], incs=[("sPOOL", 1)])
    E("pool", lambda e: e.memset(mneg[:], 0.0), incs=[("sPOOL", 1)])
    E("pool", lambda e: e.affine_select(out=mneg[:], in_=mneg[:], compare_op=ALU.is_ge,
                                        fill=-1e9, base=0, pattern=[[1, 128]],
                                        channel_multiplier=-1),
      waits=[("sPOOL", C("sPOOL"))], incs=[("sPOOL", 1)])
    E("pool", lambda e: e.memset(identb[:], 0.0), incs=[("sPOOL", 1)])
    E("pool", lambda e: e.affine_select(out=identb[:], in_=identb[:],
                                        compare_op=ALU.not_equal, fill=1.0, base=0,
                                        pattern=[[-1, 128]], channel_multiplier=1),
      waits=[("sPOOL", C("sPOOL"))], incs=[("sPOOL", 1)])
    E("pool", lambda e: e.memset(identf[:], 0.0), incs=[("sPOOL", 1)])
    E("pool", lambda e: e.affine_select(out=identf[:], in_=identf[:],
                                        compare_op=ALU.not_equal, fill=1.0, base=0,
                                        pattern=[[-1, 128]], channel_multiplier=1),
      waits=[("sPOOL", C("sPOOL"))], incs=[("sPOOL", 1)])
    E("pool", lambda e: e.memset(onescol[:], 1.0))
    E("pool", lambda e: e.memset(ones1f[:], 1.0))
    E("pool", lambda e: e.memset(ones1b[:], 1.0))
    E("pool", lambda e: e.memset(tris[:], 1.0), incs=[("sPOOL", 1)])
    E("pool", lambda e: e.affine_select(out=tris[:], in_=tris[:], compare_op=ALU.is_gt,
                                        fill=0.0, base=0, pattern=[[1, 128]],
                                        channel_multiplier=-1),
      waits=[("sPOOL", C("sPOOL"))], incs=[("sPOOL", 1)])
    E("pool", lambda e: e.iota(out=ecapB[:], pattern=[[1024, 8]], base=0,
                               channel_multiplier=0,
                               allow_small_or_imprecise_dtypes=True))
    E("pool", lambda e: e.iota(out=limB[:], pattern=[[1024, 8]], base=1023,
                               channel_multiplier=0,
                               allow_small_or_imprecise_dtypes=True))
    E("pool", lambda e: e.iota(out=wdec[:], pattern=[[0, 4], [-1, 8]], base=8,
                               channel_multiplier=0,
                               allow_small_or_imprecise_dtypes=True))
    E("pool", lambda e: e.memset(hst[:], 0.0), incs=[("sPOOL", 1)])
    consts_done = C("sPOOL")

    E("act", lambda e: e.activation(out=arow[:], in_=alog_r[:], func=AF.Exp),
      waits=[("qSY", small_in)], incs=[("sACT", 1)])
    E("dve", lambda e: e.tensor_scalar_mul(out=arow[:], in0=arow[:], scalar1=-1.0),
      waits=[("sACT", C("sACT"))], incs=[("sDVE", 1)])
    arow_done = C("sDVE")
    E("pe", lambda e: e.matmul(out=psC[:, 0:NS], lhsT=ones1f[:], rhs=bdt_r[:],
                               start=True, stop=True),
      waits=[("qSY", small_in), ("sPOOL", consts_done)], incs=[("sPE", 1)])
    E("pe", lambda e: e.matmul(out=psC[:, NS:2 * NS], lhsT=ones1f[:], rhs=arow[:],
                               start=True, stop=True),
      waits=[("sDVE", arow_done)], incs=[("sPE", 1)])
    bc_mm = C("sPE")
    E("dve", lambda e: e.tensor_copy(out=bdtB[:], in_=psC[:, 0:NS]),
      waits=[("sPE", bc_mm)], incs=[("sDVE", 1)])
    E("dve", lambda e: e.tensor_copy(out=aB[:], in_=psC[:, NS:2 * NS]),
      incs=[("sDVE", 1)])
    bcasts_done = C("sDVE")

    cast_mark = 0
    for kc in range(NT):
        sl = stage[:, (kc % 2) * 1024:(kc % 2) * 1024 + 48]
        wts = [("sPOOL", cast_mark)] if kc >= 2 else []
        E("sy", lambda e, kc=kc, sl=sl: e.dma_start(
            out=sl, in_=wxdtT_h[kc * 128:(kc + 1) * 128, :]),
          waits=wts, incs=[("qSY", 16)])
        E("pool", lambda e, kc=kc, sl=sl: e.tensor_copy(
            out=wxdtT[:, kc * 48:(kc + 1) * 48], in_=sl),
          waits=[("qSY", C("qSY"))], incs=[("sPOOL", 1)])
        cast_mark = C("sPOOL")
    for kc in range(8):
        sl = stage[:, (kc % 2) * 1024:(kc % 2) * 1024 + 8]
        E("sy", lambda e, kc=kc, sl=sl: e.dma_start(
            out=sl, in_=gwT_h[kc * 128:(kc + 1) * 128, :]),
          waits=[("sPOOL", cast_mark)], incs=[("qSY", 16)])
        E("pool", lambda e, kc=kc, sl=sl: e.tensor_copy(
            out=gwT[:, kc * 8:(kc + 1) * 8], in_=sl),
          waits=[("qSY", C("qSY"))], incs=[("sPOOL", 1)])
        cast_mark = C("sPOOL")

    if PHASES == "consts":
        return finalize()
    psT_evac = [0, 0, 0, 0]
    xn_cons = [0, 0]
    xnb_stored = 0
    for c in range(min(NCHUNK, P0_LIMIT)):
        par = c % 2
        xs = xt0[:, par * DM:(par + 1) * DM]
        wts = [("sDVE", xn_cons[par])] if c >= 2 else []
        E("sy", lambda e, c=c, xs=xs: e.dma_start(out=xs, in_=x_h[c * 128:(c + 1) * 128, :]),
          waits=wts, incs=[("qSY", 16)])
        ldc = C("qSY")
        if "l" in P0_SKIP:
            xnd = 0
        else:
         if True:
            E("dve", lambda e, xs=xs: e.tensor_reduce(out=st1[:], in_=xs, axis=AX.X,
                                                      op=ALU.add),
                  waits=[("qSY", ldc)], incs=[("sDVE", 1)])
            E("dve", lambda e: e.tensor_scalar_mul(out=st1[:], in0=st1[:], scalar1=1.0 / DM),
              waits=[("sDVE", C("sDVE"))], incs=[("sDVE", 1)])
            mdone = C("sDVE")
            E("act", lambda e, xs=xs: e.activation(out=psC[:, 0:512],
                                                   in_=xs[:, 0:512], func=AF.Square,
                                                   accum_out=st2[:]),
              waits=[("qSY", ldc)], incs=[("sACT", 1)])
            E("act", lambda e, xs=xs: e.activation(out=psC[:, 0:512], in_=xs[:, 512:1024],
                                                   func=AF.Square, accum_out=st3[:]),
              waits=[("sACT", C("sACT"))], incs=[("sACT", 1)])
            sqd = C("sACT")
            E("dve", lambda e: e.tensor_add(out=st2[:], in0=st2[:], in1=st3[:]),
              waits=[("sACT", sqd), ("sDVE", mdone)], incs=[("sDVE", 1)])
            E("dve", lambda e: e.tensor_mul(out=st3[:], in0=st1[:], in1=st1[:]),
              waits=[("sDVE", C("sDVE"))], incs=[("sDVE", 1)])
            E("dve", lambda e: e.tensor_scalar(out=st3[:], in0=st3[:], scalar1=-1.0,
                                               scalar2=1e-5, op0=ALU.mult, op1=ALU.add),
              waits=[("sDVE", C("sDVE"))], incs=[("sDVE", 1)])
            vb = C("sDVE")
            E("act", lambda e: e.activation(out=st3[:], in_=st2[:], func=AF.Ln,
                                            bias=st3[:], scale=1.0 / DM),
              waits=[("sDVE", vb)], incs=[("sACT", 1)])
            E("act", lambda e: e.activation(out=st2[:], in_=st3[:], func=AF.Exp, scale=-0.5),
              waits=[("sACT", C("sACT"))], incs=[("sACT", 1)])
            rstd = C("sACT")
            aff = C("sDVE")
            E("dve", lambda e, xs=xs: e.tensor_scalar(out=xnb[:], in0=xs, scalar1=st1[:],
                                                      scalar2=st2[:], op0=ALU.subtract,
                                                      op1=ALU.mult),
              waits=[("sDVE", aff), ("sACT", rstd)], incs=[("sDVE", 1)])
            xnd = C("sDVE")
        if "t" in P0_SKIP:
            xn_cons[par] = xnd if isinstance(xnd, int) else 0
            continue
        for g in range(2):
            for sl4 in range(4):
                kc = g * 4 + sl4
                tdst = psT[:, sl4 * 128:(sl4 + 1) * 128]
                E("pe", lambda e, kc=kc, tdst=tdst: e.transpose(
                    out=tdst, in_=xnb[:, kc * 128:(kc + 1) * 128], identity=identb[:]),
                  waits=[("sDVE", xnd), ("sDVE", psT_evac[0]),
                         ("sPOOL", consts_done)],
                  incs=[("sPE", 1)])
            td = C("sPE")
            for sl4 in range(4):
                kc = g * 4 + sl4
                tdst = psT[:, sl4 * 128:(sl4 + 1) * 128]
                E("dve", lambda e, kc=kc, c=c, tdst=tdst: e.tensor_copy(
                    out=xnT[:, kc * L + c * 128:kc * L + (c + 1) * 128], in_=tdst),
                  waits=[("sPE", td)], incs=[("sDVE", 1)])
            psT_evac[0] = C("sDVE")
        xn_cons[par] = xnd

    if PHASES == "p0":
        return finalize()
    b.barrier()

    slot_cast = [0, 0]
    idx = 0
    for kc in range(8):
        for hh in range(2):
            par = idx % 2
            sl = stage[:, par * 1024:(par + 1) * 1024]
            wts = [("sPOOL", slot_cast[par])] if slot_cast[par] else []
            E("sy", lambda e, kc=kc, hh=hh, sl=sl: e.dma_start(
                out=sl, in_=winT_h[kc * 128:(kc + 1) * 128, hh * 1024:(hh + 1) * 1024]),
              waits=wts, incs=[("qSY", 16)])
            E("pool", lambda e, kc=kc, hh=hh, sl=sl: e.tensor_copy(
                out=WA[:, kc * DI + hh * 1024:kc * DI + (hh + 1) * 1024], in_=sl),
              waits=[("qSY", C("qSY"))], incs=[("sPOOL", 1)])
            slot_cast[par] = C("sPOOL")
            idx += 1
    winxi_done = C("sPOOL")

    E("pool", lambda e: e.memset(tails[:], 0.0), incs=[("sPOOL", 1)])
    tails_zeroed = C("sPOOL")
    utok_stored = [0] * NCHUNK
    ufm_stored = [0] * NTW
    bc_stored = [0] * NTW
    psA_free = [0, 0]
    tile_free = [0, 0]
    ufm_cons_pe = 0
    ufm_cons_pl = 0
    utokb_free = [0, 0]
    for T in range(NTW):
        silu_marks = [0] * NT
        tapend_marks = [0] * NT
        for ft in range(NT):
            par = ft % 2
            tb = tileb[:, par * (TW + 3):(par + 1) * (TW + 3)]
            psl = psA[:, par * 512:(par + 1) * 512]
            for kc in range(8):
                wts = []
                if kc == 0:
                    wts = [("sPOOL", winxi_done), ("sDVE", psA_free[par]),
                           ("bsy", C("bsy"))]
                E("pe", lambda e, T=T, ft=ft, kc=kc, psl=psl: e.matmul(
                    out=psl,
                    lhsT=WA[:, kc * DI + ft * 128:kc * DI + (ft + 1) * 128],
                    rhs=xnT[:, kc * L + T * TW:kc * L + (T + 1) * TW],
                    start=(kc == 0), stop=(kc == 7)),
                  waits=wts, incs=([("sPE", 1)] if kc == 7 else []))
            mmd = C("sPE")
            E("dve", lambda e, ft=ft, tb=tb: e.tensor_copy(
                out=tb[:, 0:3], in_=tails[:, ft * 3:(ft + 1) * 3]),
              waits=[("sACT", tile_free[par]), ("sPOOL", tails_zeroed)],
              incs=[("sDVE", 1)])
            E("dve", lambda e, tb=tb, psl=psl: e.tensor_copy(out=tb[:, 3:3 + TW], in_=psl),
              waits=[("sPE", mmd)], incs=[("sDVE", 1)])
            psA_free[par] = C("sDVE")
            E("dve", lambda e, ft=ft, tb=tb: e.tensor_copy(
                out=tails[:, ft * 3:(ft + 1) * 3], in_=tb[:, TW:TW + 3]),
              waits=[("sDVE", C("sDVE"))], incs=[("sDVE", 1)])
            ca = convacc[:, par * TW:(par + 1) * TW]
            E("dve", lambda e, ft=ft, tb=tb, ca=ca: e.tensor_scalar_mul(
                out=ca, in0=tb[:, 0:TW], scalar1=convw[:, ft * 4:ft * 4 + 1]),
              waits=[("sDVE", C("sDVE")), ("sACT", tile_free[par])], incs=[("sDVE", 1)])
            for k in range(1, 4):
                E("dve", lambda e, ft=ft, k=k, tb=tb, ca=ca: e.scalar_tensor_tensor(
                    out=ca, in0=tb[:, k:k + TW],
                    scalar=convw[:, ft * 4 + k:ft * 4 + k + 1],
                    in1=ca, op0=ALU.mult, op1=ALU.add),
                  waits=[("sDVE", C("sDVE"))], incs=[("sDVE", 1)])
            tapd = C("sDVE")
            tapend_marks[ft] = tapd
            E("act", lambda e, ft=ft, ca=ca: e.activation(
                out=ufm16[:, ft * TW:(ft + 1) * TW], in_=ca, func=AF.Silu,
                bias=convb[:, ft:ft + 1]),
              waits=[("sDVE", tapd), ("sPE", ufm_cons_pe), ("qPL", ufm_cons_pl)],
              incs=[("sACT", 1)])
            silu_marks[ft] = C("sACT")
            tile_free[par] = C("sACT")
        for ft in range(NT):
            E("pe", lambda e, ft=ft: e.matmul(
                out=psC[0:48, 0:512],
                lhsT=wxdtT[:, ft * 48:(ft + 1) * 48],
                rhs=ufm16[:, ft * TW:(ft + 1) * TW],
                start=(ft == 0), stop=(ft == NT - 1)),
              waits=[("sACT", silu_marks[ft])] +
                    ([("sDVE", C("sDVE"))] if ft == 0 else []),
              incs=([("sPE", 1)] if ft == NT - 1 else []))
        bcmm = C("sPE")
        E("dve", lambda e: e.tensor_copy(out=bcev[:], in_=psC[0:48, 0:512]),
          waits=[("sPE", bcmm)], incs=[("sDVE", 1)])
        bev = C("sDVE")
        E("pool", lambda e, T=T: e.dma_start(out=bcH[:, T * TW:(T + 1) * TW], in_=bcev[:]),
          waits=[("sDVE", bev)], incs=[("qPL", 16)])
        bc_stored[T] = C("qPL")
        for ft in range(NT):
            E("pool", lambda e, T=T, ft=ft: e.dma_start(
                out=ufmH[T * 4:(T + 1) * 4, :, ft, :].rearrange("c p t -> p c t"),
                in_=ufm16[:, ft * TW:(ft + 1) * TW].rearrange("p (c t) -> p c t", c=4)),
              waits=[("sACT", silu_marks[ft])], incs=[("qPL", 16)])
        ufm_stored[T] = C("qPL")
        for tc in range(4):
            c = T * 4 + tc
            spar = tc % 2
            ub = utokb[:, spar * DI:(spar + 1) * DI]
            for g in range(4):
                for sl4 in range(4):
                    ft = g * 4 + sl4
                    tdst = psT[:, sl4 * 128:(sl4 + 1) * 128]
                    E("pe", lambda e, ft=ft, tc=tc, tdst=tdst: e.transpose(
                        out=tdst,
                        in_=ufm16[:, ft * TW + tc * 128:ft * TW + (tc + 1) * 128],
                        identity=identb[:]),
                      waits=[("sACT", silu_marks[ft]), ("sDVE", psT_evac[0])],
                      incs=[("sPE", 1)])
                td = C("sPE")
                for sl4 in range(4):
                    ft = g * 4 + sl4
                    tdst = psT[:, sl4 * 128:(sl4 + 1) * 128]
                    E("dve", lambda e, ft=ft, ub=ub, tdst=tdst: e.tensor_copy(
                        out=ub[:, ft * 128:(ft + 1) * 128], in_=tdst),
                      waits=[("sPE", td), ("qPL", utokb_free[spar])],
                      incs=[("sDVE", 1)])
                psT_evac[0] = C("sDVE")
            utd = C("sDVE")
            E("pool", lambda e, c=c, ub=ub: e.dma_start(out=utokH[c], in_=ub),
              waits=[("sDVE", utd)], incs=[("qPL", 16)])
            utok_stored[c] = C("qPL")
            utokb_free[spar] = C("qPL")
        ufm_cons_pe = C("sPE")
        ufm_cons_pl = C("qPL")

    if PHASES == "p2":
        return finalize()
    b.barrier()

    slot_cast = [C("sPOOL"), C("sPOOL")]
    idx = 0
    for kc in range(8):
        for hh in range(2):
            par = idx % 2
            sl = stage[:, par * 1024:(par + 1) * 1024]
            E("sy", lambda e, kc=kc, hh=hh, sl=sl: e.dma_start(
                out=sl, in_=winT_h[kc * 128:(kc + 1) * 128,
                                   DI + hh * 1024:DI + (hh + 1) * 1024]),
              waits=[("sPOOL", slot_cast[par])], incs=[("qSY", 16)])
            E("pool", lambda e, kc=kc, hh=hh, sl=sl: e.tensor_copy(
                out=WB[:, kc * DI + hh * 1024:kc * DI + (hh + 1) * 1024], in_=sl),
              waits=[("qSY", C("qSY"))], incs=[("sPOOL", 1)])
            slot_cast[par] = C("sPOOL")
            idx += 1
    winz_done = C("sPOOL")

    if PHASES == "pg":
        return finalize()
    b.barrier()

    def emit_zhalf(hg, szpar, after_psB_free_dve, szf_war_pool=0):
        prev_silu = [0, 0]
        for rnd in range(4):
            base = (rnd % 2) * 1024
            for fti in range(4):
                ft = rnd * 4 + fti
                for kc in range(8):
                    wts = []
                    if fti == 0 and kc == 0:
                        wts = [("sPOOL", winz_done), ("sDVE", after_psB_free_dve),
                               ("sACT", prev_silu[rnd % 2])]
                    E("pe", lambda e, hg=hg, ft=ft, fti=fti, kc=kc, base=base: e.matmul(
                        out=psB[:, base + fti * 256:base + (fti + 1) * 256],
                        lhsT=WB[:, kc * DI + ft * 128:kc * DI + (ft + 1) * 128],
                        rhs=xnT[:, kc * L + hg * 256:kc * L + (hg + 1) * 256],
                        start=(kc == 0), stop=(kc == 7)),
                      waits=wts, incs=([("sPE", 1)] if kc == 7 else []))
            zmm = C("sPE")
            for fti in range(4):
                ft = rnd * 4 + fti
                E("act", lambda e, ft=ft, fti=fti, szpar=szpar, base=base: e.activation(
                    out=szf[:, szpar * NT * 256 + ft * 256:
                            szpar * NT * 256 + (ft + 1) * 256],
                    in_=psB[:, base + fti * 256:base + (fti + 1) * 256], func=AF.Silu),
                  waits=[("sPE", zmm), ("sPOOL", szf_war_pool)], incs=[("sACT", 1)])
            prev_silu[rnd % 2] = C("sACT")
        return C("sACT")

    sz_ready = [0] * NHG
    sz_ready[0] = emit_zhalf(0, 0, 0)
    szf_cons = [0, 0]

    state = {
        "utok_cons": 0, "ufm_cons": 0, "bc_cons": [0, 0],
        "flt_cons": 0, "fltc_cons": 0, "yzb_stored": 0,
        "psB_evac_dve": 0, "psB_z_done": C("sACT"),
        "prev_h_done": 0, "prev_X1_done": 0, "psA_free": 0,
    }
    SM = {}

    def emit_stats(c):
        par = c % 2
        E("sy", lambda e, c=c, par=par: e.dma_start(
            out=bc4[:, par * Q:(par + 1) * Q], in_=bcH[:, c * Q:(c + 1) * Q]),
          waits=[("qPL", bc_stored[c // 4]), ("sPE", state["bc_cons"][par])],
          incs=[("qSY", 16)])
        bc_ld = C("qSY")
        prev_psc_free = SM[c - 1]["ctT"] if (c - 1) in SM else 0
        E("pe", lambda e, par=par: e.transpose(
            out=psC[:, 304:352], in_=bc4[:, par * Q:(par + 1) * Q],
            identity=identf[0:48, 0:48]),
          waits=[("qSY", bc_ld), ("sDVE", prev_psc_free)], incs=[("sPE", 1)])
        bct = C("sPE")
        state["bc_cons"][par] = bct
        E("dve", lambda e: e.tensor_add(out=sm_a[:], in0=psC[:, 304 + 32:304 + 48],
                                        in1=bdtB[:]),
          waits=[("sPE", bct), ("sDVE", bcasts_done)], incs=[("sDVE", 1)])
        E("dve", lambda e: e.tensor_copy(out=sm_b[:], in_=psC[:, 304:304 + 16]),
          waits=[("sDVE", C("sDVE"))], incs=[("sDVE", 1)])
        E("dve", lambda e: e.tensor_copy(out=sm_cbf[:], in_=psC[:, 304 + 16:304 + 32]),
          waits=[("sDVE", C("sDVE"))], incs=[("sDVE", 1)])
        smalls0 = C("sDVE")
        E("dve", lambda e: e.tensor_scalar_max(out=sm_dlt[:], in0=sm_a[:], scalar1=0.0),
          waits=[("sDVE", smalls0)], incs=[("sDVE", 1)])
        r0 = C("sDVE")
        E("act", lambda e: e.activation(out=sm_lnb[:], in_=sm_a[:], func=AF.Abs),
          waits=[("sDVE", smalls0)], incs=[("sACT", 1)])
        E("act", lambda e: e.activation(out=sm_lnb[:], in_=sm_lnb[:], func=AF.Exp,
                                        scale=-1.0),
          waits=[("sACT", C("sACT"))], incs=[("sACT", 1)])
        E("act", lambda e: e.activation(out=sm_lnb[:], in_=sm_lnb[:], func=AF.Ln,
                                        bias=1.0),
          waits=[("sACT", C("sACT"))], incs=[("sACT", 1)])
        spl = C("sACT")
        E("dve", lambda e: e.tensor_add(out=sm_dlt[:], in0=sm_dlt[:], in1=sm_lnb[:]),
          waits=[("sACT", spl), ("sDVE", r0)], incs=[("sDVE", 1)])
        dlt_done = C("sDVE")
        E("dve", lambda e: e.tensor_mul(out=sm_a[:], in0=sm_dlt[:], in1=aB[:]),
          waits=[("sDVE", dlt_done)], incs=[("sDVE", 1)])
        dal_done = C("sDVE")
        E("dve", lambda e: e.tensor_mul(out=sm_dB[:], in0=sm_dlt[:], in1=sm_b[:]),
          waits=[("sDVE", dlt_done)], incs=[("sDVE", 1)])
        db_done = C("sDVE")
        E("act", lambda e: e.activation(out=sm_lnb[:], in_=sm_dB[:], func=AF.Abs),
          waits=[("sDVE", db_done)], incs=[("sACT", 1)])
        E("act", lambda e: e.activation(out=sm_lnb[:], in_=sm_lnb[:], func=AF.Ln),
          waits=[("sACT", C("sACT"))], incs=[("sACT", 1)])
        lnb_done = C("sACT")
        E("act", lambda e, par=par: e.activation(
            out=sm_sgn[:, par * NS:(par + 1) * NS], in_=sm_dB[:], func=AF.Sign),
          waits=[("sDVE", db_done)], incs=[("sACT", 1)])
        sgn_done = C("sACT")
        E("pe", lambda e: e.matmul(out=psC[:, 0:16], lhsT=triu[:], rhs=sm_a[:],
                                   start=True, stop=True),
          waits=[("sDVE", dal_done)], incs=[("sPE", 1)])
        E("pe", lambda e: e.matmul(out=psC[0:16, 16:144], lhsT=sm_a[:], rhs=triu[:],
                                   start=True, stop=True),
          waits=[("sPE", C("sPE"))], incs=[("sPE", 1)])
        E("pe", lambda e: e.matmul(out=psC[0:16, 144:145], lhsT=sm_a[:], rhs=onescol[:],
                                   start=True, stop=True),
          waits=[("sPE", C("sPE"))], incs=[("sPE", 1)])
        E("pe", lambda e: e.matmul(out=psC[0:1, 145:161], lhsT=onescol[:], rhs=sm_a[:],
                                   start=True, stop=True),
          waits=[("sPE", C("sPE"))], incs=[("sPE", 1)])
        cum_mm = C("sPE")
        E("dve", lambda e: e.tensor_copy(out=sm_cum[:], in_=psC[:, 0:16]),
          waits=[("sPE", cum_mm)], incs=[("sDVE", 1)])
        E("dve", lambda e: e.tensor_copy(out=cumT_e[:], in_=psC[0:16, 16:144]),
          waits=[("sDVE", C("sDVE"))], incs=[("sDVE", 1)])
        E("dve", lambda e: e.tensor_copy(out=clF[:], in_=psC[0:1, 145:161]),
          waits=[("sDVE", C("sDVE"))], incs=[("sDVE", 1)])
        cum_ev = C("sDVE")
        E("act", lambda e, par=par: e.activation(
            out=dec_c[:, par:par + 1], in_=psC[0:16, 144:145], func=AF.Exp),
          waits=[("sPE", cum_mm)], incs=[("sACT", 1)])
        dec_done = C("sACT")
        E("pe", lambda e: e.matmul(out=psC[:, 161:177], lhsT=ones1f[:], rhs=clF[:],
                                   start=True, stop=True),
          waits=[("sDVE", cum_ev), ("sACT", dec_done)], incs=[("sPE", 1)])
        bclmm = C("sPE")
        E("dve", lambda e: e.tensor_sub(out=sm_b[:], in0=psC[:, 161:177], in1=sm_cum[:]),
          waits=[("sPE", bclmm), ("sDVE", cum_ev)], incs=[("sDVE", 1)])
        bed = C("sDVE")
        E("act", lambda e: e.activation(out=sm_b[:], in_=sm_b[:], func=AF.Exp),
          waits=[("sDVE", bed)], incs=[("sACT", 1)])
        be2 = C("sACT")
        E("dve", lambda e, par=par: e.tensor_mul(
            out=btok[:, par * NS:(par + 1) * NS], in0=sm_dB[:], in1=sm_b[:]),
          waits=[("sACT", be2), ("sDVE", db_done)], incs=[("sDVE", 1)])
        btok_done = C("sDVE")
        E("act", lambda e: e.activation(out=sm_a[:], in_=sm_cum[:], func=AF.Exp),
          waits=[("sDVE", cum_ev)], incs=[("sACT", 1)])
        ecd = C("sACT")
        E("dve", lambda e: e.tensor_mul(out=sm_ctl[:], in0=sm_cbf[:], in1=sm_a[:]),
          waits=[("sACT", ecd)], incs=[("sDVE", 1)])
        ctl = C("sDVE")
        E("pe", lambda e: e.transpose(out=psC[0:16, 177:305], in_=sm_ctl[:],
                                      identity=identf[:]),
          waits=[("sDVE", ctl), ("sDVE", bed), ("sACT", dec_done)],
          incs=[("sPE", 1)])
        ctTm = C("sPE")
        E("dve", lambda e, par=par: e.tensor_copy(
            out=ctT[:, par * Q:(par + 1) * Q], in_=psC[0:16, 177:305]),
          waits=[("sPE", ctTm)], incs=[("sDVE", 1)])
        ctT_done = C("sDVE")
        E("pe", lambda e: e.transpose(out=psT[0:16, 0:128], in_=sm_cbf[:],
                                      identity=identb[:]),
          waits=[("sDVE", smalls0), ("sDVE", psT_evac[0])], incs=[("sPE", 1)])
        cTm = C("sPE")
        E("dve", lambda e: e.tensor_copy(out=cb16[:], in_=psT[0:16, 0:128]),
          waits=[("sPE", cTm)], incs=[("sDVE", 1)])
        psT_evac[0] = C("sDVE")
        cb_done = C("sDVE")
        E("sy", lambda e: e.dma_start(out=fcumH, in_=cumT_e[:]),
          waits=[("sDVE", cum_ev), ("sPE", state["flt_cons"])], incs=[("qSY", 16)])
        E("sy", lambda e: e.dma_start(
            out=flt[:], in_=fcumH.rearrange("n t -> (n t)")[None, :]),
          waits=[("qSY", C("qSY"))], incs=[("qSY", 16)])
        flt_ld = C("qSY")
        E("sy", lambda e: e.dma_start(out=fcH, in_=cb16[:]),
          waits=[("sDVE", cb_done), ("sPE", state["fltc_cons"])], incs=[("qSY", 16)])
        E("sy", lambda e: e.dma_start(
            out=fltc[:], in_=fcH.rearrange("n t -> (n t)")[None, :]),
          waits=[("qSY", C("qSY"))], incs=[("qSY", 16)])
        fltc_ld = C("qSY")
        E("dve", lambda e, par=par: e.tensor_sub(
            out=sm_mbias[:, par * NS:(par + 1) * NS], in0=sm_lnb[:], in1=sm_cum[:]),
          waits=[("sACT", lnb_done), ("sDVE", cum_ev)], incs=[("sDVE", 1)])
        mbias_done = C("sDVE")
        SM[c] = dict(flt_ld=flt_ld, fltc_ld=fltc_ld, mbias=mbias_done,
                     sgn=sgn_done, ctT=ctT_done, btok=btok_done, dec=dec_done)

    emit_stats(0)
    for c in range(NCHUNK):
        hg = c // 2
        szpar = hg % 2
        par = c % 2
        st_m = SM[c]
        E("sy", lambda e, c=c: e.dma_start(out=ut4[:], in_=utokH[c]),
          waits=[("qPL", utok_stored[c]), ("sPE", state["utok_cons"])],
          incs=[("qSY", 16)])
        ut_ld = C("qSY")
        E("sy", lambda e, c=c: e.dma_start(
            out=uf4[:], in_=ufmH[c].rearrange("p et t -> p (et t)")),
          waits=[("qPL", ufm_stored[c // 4]), ("sDVE", state["ufm_cons"])],
          incs=[("qSY", 16)])
        uf_ld = C("qSY")

        for et in range(NT):
            wts = [("sDVE", st_m["ctT"]), ("sDVE", state["prev_h_done"])]
            if et == 0:
                wts.append(("sACT", state["psB_z_done"]))
                wts.append(("sDVE", state["psB_evac_dve"]))
            E("pe", lambda e, et=et, par=par: e.matmul(
                out=psB[:, et * 128:(et + 1) * 128],
                lhsT=hst[:, et * 128:(et + 1) * 128],
                rhs=ctT[:, par * Q:(par + 1) * Q],
                start=(et % 4 == 0), stop=False),
              waits=wts, incs=[("sPE", 1)])
        y2done = C("sPE")

        for hh in range(2):
            for jj in range(2):
                wts = [("qSY", st_m["flt_ld"])]
                if hh == 0 and jj == 0:
                    wts.append(("sDVE", state["psA_free"]))
                if hh == 1 and jj == 0:
                    wts.append(("sDVE", C("sDVE")))
                E("pe", lambda e, hh=hh, jj=jj: e.matmul(
                    out=psA[:, jj * 512:(jj + 1) * 512], lhsT=ones1f[:],
                    rhs=flt[0:1, hh * 1024 + jj * 512:hh * 1024 + (jj + 1) * 512],
                    start=True, stop=True),
                  waits=wts, incs=[("sPE", 1)])
            bctm = C("sPE")
            state["flt_cons"] = bctm
            E("dve", lambda e, hh=hh: e.tensor_add(
                out=mb1[:, hh * 1024:(hh + 1) * 1024].rearrange("p (n t) -> p n t", n=8),
                in0=psA[:, 0:1024].rearrange("p (n t) -> p n t", n=8),
                in1=mneg[:, None, :].broadcast_to([128, 8, 128])),
              waits=[("sPE", bctm)], incs=[("sDVE", 1)])
            tmpd = C("sDVE")
            for ni in range(8):
                n = hh * 8 + ni
                E("act", lambda e, n=n, par=par: e.activation(
                    out=mb2[:, n * 128:(n + 1) * 128],
                    in_=mb1[:, n * 128:(n + 1) * 128], func=AF.Exp,
                    bias=sm_mbias[:, par * NS + n:par * NS + n + 1]),
                  waits=[("sDVE", tmpd), ("sDVE", st_m["mbias"])], incs=[("sACT", 1)])
            expa = C("sACT")
            E("dve", lambda e, hh=hh, par=par: e.tensor_mul(
                out=mb1[:, hh * 1024:(hh + 1) * 1024].rearrange(
                    "p (n t) -> p n t", n=8),
                in0=mb2[:, hh * 1024:(hh + 1) * 1024].rearrange(
                    "p (n t) -> p n t", n=8),
                in1=sm_sgn[:, par * NS + hh * 8:par * NS + (hh + 1) * 8][:, :, None
                    ].broadcast_to([128, 8, 128])),
              waits=[("sACT", expa), ("sACT", st_m["sgn"])], incs=[("sDVE", 1)])
            expd = ("sDVE", C("sDVE"))
            for jj in range(2):
                E("pe", lambda e, hh=hh, jj=jj: e.matmul(
                    out=psA[:, jj * 512:(jj + 1) * 512], lhsT=ones1b[:],
                    rhs=fltc[0:1, hh * 1024 + jj * 512:hh * 1024 + (jj + 1) * 512],
                    start=True, stop=True),
                  waits=([("qSY", st_m["fltc_ld"]), ("sDVE", tmpd)]
                         if jj == 0 else []),
                  incs=[("sPE", 1)])
            bccm = C("sPE")
            state["fltc_cons"] = bccm
            E("dve", lambda e, hh=hh: e.tensor_mul(
                out=mb2[:, hh * 1024:(hh + 1) * 1024],
                in0=mb1[:, hh * 1024:(hh + 1) * 1024], in1=psA[:, 0:1024]),
              waits=[("sPE", bccm), expd], incs=[("sDVE", 1)])
        X1_done = C("sDVE")
        state["prev_X1_done"] = X1_done
        E("dve", lambda e: e.tensor_reduce(
            out=msb[:], in_=mb2[:].rearrange("p (n t) -> p t n", n=16),
            axis=AX.X, op=ALU.add),
          waits=[("sDVE", X1_done)], incs=[("sDVE", 1)])
        E("dve", lambda e: e.tensor_copy(out=mbf[:], in_=msb[:]),
          waits=[("sDVE", C("sDVE"))], incs=[("sDVE", 1)])
        mbf_done = C("sDVE")

        if c + 1 < NCHUNK:
            emit_stats(c + 1)

        for et in range(NT):
            E("pe", lambda e, et=et: e.matmul(
                out=psB[:, et * 128:(et + 1) * 128],
                lhsT=ut4[:, et * 128:(et + 1) * 128], rhs=mbf[:],
                start=False, stop=True),
              waits=[("sDVE", mbf_done), ("qSY", ut_ld), ("sPE", y2done)],
              incs=[("sPE", 1)])
        y1done = C("sPE")

        hmarks = []
        for rr in range(4):
            base = (rr % 2) * 512
            wts = [("qSY", ut_ld), ("sDVE", st_m["btok"]), ("sDVE", X1_done)]
            if rr >= 2:
                wts.append(("sDVE", hmarks[rr - 2]))
            E("pe", lambda e, rr=rr, base=base, par=par: e.matmul(
                out=psA[0:16, base:base + 512], lhsT=btok[:, par * NS:(par + 1) * NS],
                rhs=ut4[:, rr * 512:(rr + 1) * 512], start=True, stop=True),
              waits=wts, incs=[("sPE", 1)])
            dhm = C("sPE")
            E("dve", lambda e, rr=rr, base=base, par=par: e.scalar_tensor_tensor(
                out=hst[:, rr * 512:(rr + 1) * 512],
                in0=hst[:, rr * 512:(rr + 1) * 512], scalar=dec_c[:, par:par + 1],
                in1=psA[0:16, base:base + 512], op0=ALU.mult, op1=ALU.add),
              waits=[("sPE", dhm), ("sACT", st_m["dec"]), ("sPE", y2done)],
              incs=[("sDVE", 1)])
            hmarks.append(C("sDVE"))
        h_done = C("sDVE")
        state["prev_h_done"] = h_done
        state["psA_free"] = h_done
        utok_cons_mark = C("sPE")
        state["utok_cons"] = utok_cons_mark

        for et in range(NT):
            E("dve", lambda e, et=et: e.scalar_tensor_tensor(
                out=mb1[:, et * 128:(et + 1) * 128],
                in0=uf4[:, et * 128:(et + 1) * 128], scalar=dfm[:, et:et + 1],
                in1=psB[:, et * 128:(et + 1) * 128], op0=ALU.mult, op1=ALU.add),
              waits=[("sPE", y1done), ("qSY", uf_ld), expd],
              incs=[("sDVE", 1)])
            dd = C("sDVE")
            E("pool", lambda e, et=et, c=c, szpar=szpar: e.tensor_mul(
                out=yzb[:, et * 128:(et + 1) * 128],
                in0=mb1[:, et * 128:(et + 1) * 128],
                in1=szf[:, szpar * NT * 256 + et * 256 + (c % 2) * 128:
                        szpar * NT * 256 + et * 256 + (c % 2) * 128 + 128]),
              waits=[("sDVE", dd), ("sACT", sz_ready[hg]),
                     ("qSY", state["yzb_stored"])],
              incs=[("sPOOL", 1)])
        psB_evac = C("sDVE")
        state["psB_evac_dve"] = psB_evac
        yz_done = C("sPOOL")
        szf_cons[szpar] = yz_done
        state["ufm_cons"] = psB_evac
        E("sy", lambda e, c=c: e.dma_start(out=yzH[c], in_=yzb[:]),
          waits=[("sPOOL", yz_done)], incs=[("qSY", 16)])
        state["yzb_stored"] = C("qSY")

        if c % 2 == 1 and hg + 1 < NHG:
            nxt = 1 - szpar
            sz_ready[hg + 1] = emit_zhalf(hg + 1, nxt, psB_evac, szf_cons[nxt])
            state["psB_z_done"] = sz_ready[hg + 1]
        else:
            state["psB_z_done"] = C("sACT")

    if PHASES == "p4":
        return finalize()
    b.barrier()

    CAP = 1024
    xsb = at("xsb", 10240, [128, 2 * DM], BF16)
    xsTb = at("xsTb", 14336, [128, 2 * DM], BF16)
    xnb5 = at("xnb5", 18432, [128, 2 * DM], BF16)
    cnt8s = at("cnt8s", 22848, [8, NCHUNK], F32)
    z832 = at("z832", 23040, [8, NCHUNK], F32)
    bceF = at("bceF", 23296, [1, 8 * NCHUNK], F32)
    bce32 = at("bce32", 24256, [NCHUNK, 8], F32)
    part1 = at("part1", 24384, [128, NCHUNK], F32)
    ohall = at("ohall", 24512, [128, NCHUNK * 8], F32)

    psc_read = 0
    for cb in range(NCHUNK // 4):
        for ci in range(4):
            c = cb * 4 + ci
            for kc in range(8):
                wts = [("sDVE", psc_read)] if (ci == 0 and kc == 0) else []
                E("pe", lambda e, c=c, ci=ci, kc=kc: e.matmul(
                    out=psC[:, ci * 8:(ci + 1) * 8],
                    lhsT=xnT[:, kc * L + c * 128:kc * L + (c + 1) * 128],
                    rhs=gwT[:, kc * 8:(kc + 1) * 8],
                    start=(kc == 0), stop=(kc == 7)),
                  waits=wts, incs=([("sPE", 1)] if kc == 7 else []))
        lg = C("sPE")
        E("dve", lambda e: e.tensor_reduce(
            out=rmax4[:], in_=psC[:, 0:32].rearrange("p (c e) -> p c e", c=4),
            axis=AX.X, op=ALU.max),
          waits=[("sPE", lg)], incs=[("sDVE", 1)])
        rm = C("sDVE")
        E("dve", lambda e: e.tensor_tensor(
            out=ohb[:].rearrange("p (c e) -> p c e", c=4),
            in0=psC[:, 0:32].rearrange("p (c e) -> p c e", c=4),
            in1=rmax4[:, :, None].broadcast_to([128, 4, 8]), op=ALU.is_ge),
          waits=[("sDVE", rm)], incs=[("sDVE", 1)])
        E("dve", lambda e: e.tensor_mul(out=ohb[:], in0=ohb[:], in1=wdec[:]),
          waits=[("sDVE", C("sDVE"))], incs=[("sDVE", 1)])
        E("dve", lambda e: e.tensor_reduce(
            out=rmax4[:], in_=ohb[:].rearrange("p (c e) -> p c e", c=4),
            axis=AX.X, op=ALU.max),
          waits=[("sDVE", C("sDVE"))], incs=[("sDVE", 1)])
        E("dve", lambda e, cb=cb: e.tensor_tensor(
            out=ohall[:, cb * 32:(cb + 1) * 32].rearrange("p (c e) -> p c e", c=4),
            in0=ohb[:].rearrange("p (c e) -> p c e", c=4),
            in1=rmax4[:, :, None].broadcast_to([128, 4, 8]), op=ALU.is_ge),
          waits=[("sDVE", C("sDVE"))], incs=[("sDVE", 1)])
        ohf_done = C("sDVE")
        E("dve", lambda e: e.tensor_reduce(
            out=rmax4[:], in_=psC[:, 0:32].rearrange("p (c e) -> p c e", c=4),
            axis=AX.X, op=ALU.max),
          waits=[("sDVE", ohf_done)], incs=[("sDVE", 1)])
        rm2 = C("sDVE")
        E("act", lambda e: e.activation(out=rmax4[:], in_=rmax4[:], func=AF.Tanh,
                                        scale=0.5),
          waits=[("sDVE", rm2)], incs=[("sACT", 1)])
        tw = C("sACT")
        E("dve", lambda e, cb=cb: e.tensor_scalar(
            out=rwall[:, cb * 4:(cb + 1) * 4], in0=rmax4[:], scalar1=0.5,
            scalar2=0.5, op0=ALU.mult, op1=ALU.add),
          waits=[("sACT", tw)], incs=[("sDVE", 1)])
        for ci in range(4):
            c = cb * 4 + ci
            E("pe", lambda e, c=c, ci=ci: e.matmul(
                out=psC[:, 32 + ci * 8:32 + (ci + 1) * 8], lhsT=tris[:],
                rhs=ohall[:, c * 8:(c + 1) * 8], start=True, stop=True),
              waits=[("sDVE", ohf_done)], incs=[("sPE", 1)])
            E("pe", lambda e, c=c: e.matmul(
                out=psC[0:8, 64 + c:65 + c], lhsT=ohall[:, c * 8:(c + 1) * 8],
                rhs=onescol[:], start=True, stop=True),
              waits=[("sPE", C("sPE"))], incs=[("sPE", 1)])
        rkm = C("sPE")
        E("dve", lambda e: e.tensor_tensor(
            out=ohb[:].rearrange("p (c e) -> p c e", c=4),
            in0=psC[:, 32:64].rearrange("p (c e) -> p c e", c=4),
            in1=ecapB[:, None, :].broadcast_to([128, 4, 8]), op=ALU.add),
          waits=[("sPE", rkm)], incs=[("sDVE", 1)])
        E("dve", lambda e: e.tensor_tensor(
            out=ohb[:].rearrange("p (c e) -> p c e", c=4),
            in0=ohb[:].rearrange("p (c e) -> p c e", c=4),
            in1=limB[:, None, :].broadcast_to([128, 4, 8]), op=ALU.min),
          waits=[("sDVE", C("sDVE"))], incs=[("sDVE", 1)])
        E("dve", lambda e, cb=cb: e.tensor_mul(
            out=ohb[:], in0=ohb[:], in1=ohall[:, cb * 32:(cb + 1) * 32]),
          waits=[("sDVE", C("sDVE"))], incs=[("sDVE", 1)])
        E("dve", lambda e, cb=cb: e.tensor_reduce(
            out=part1[:, cb * 4:(cb + 1) * 4],
            in_=ohb[:].rearrange("p (c e) -> p c e", c=4), axis=AX.X, op=ALU.add),
          waits=[("sDVE", C("sDVE"))], incs=[("sDVE", 1)])
        psc_read = C("sDVE")

    E("dve", lambda e: e.tensor_copy(out=cnt8s[:], in_=psC[0:8, 64:64 + NCHUNK]),
      waits=[("sDVE", psc_read), ("sPE", C("sPE"))], incs=[("sDVE", 1)])
    E("dve", lambda e: e.memset(z832[:], 0.0),
      waits=[("sDVE", C("sDVE"))], incs=[("sDVE", 1)])
    E("dve", lambda e: e.tensor_tensor_scan(
        out=z832[:], data0=cnt8s[:], data1=z832[:], initial=0.0,
        op0=ALU.add, op1=ALU.add),
      waits=[("sDVE", C("sDVE"))], incs=[("sDVE", 1)])
    E("dve", lambda e: e.tensor_sub(out=cnt8s[:], in0=z832[:], in1=cnt8s[:]),
      waits=[("sDVE", C("sDVE"))], incs=[("sDVE", 1)])
    bce_done = C("sDVE")
    E("pe", lambda e: e.transpose(out=psC[0:NCHUNK, 104:112], in_=cnt8s[:],
                                  identity=identf[0:8, 0:8]),
      waits=[("sDVE", bce_done)], incs=[("sPE", 1)])
    bcet = C("sPE")
    E("dve", lambda e: e.tensor_copy(out=bce32[:], in_=psC[0:NCHUNK, 104:112]),
      waits=[("sPE", bcet)], incs=[("sDVE", 1)])
    bce32d = C("sDVE")
    E("pool", lambda e: e.dma_start(out=bceH, in_=bce32[:]),
      waits=[("sDVE", bce32d)], incs=[("qPL", 16)])
    E("pool", lambda e: e.dma_start(
        out=bceF[:], in_=bceH.rearrange("c e -> (c e)")[None, :]),
      waits=[("qPL", C("qPL"))], incs=[("qPL", 16)])
    bceF_ld = C("qPL")
    slot_marks = []
    for c in range(NCHUNK):
        E("pe", lambda e, c=c: e.matmul(
            out=psC[:, 96:104], lhsT=ones1f[:], rhs=bceF[0:1, c * 8:(c + 1) * 8],
            start=True, stop=True),
          waits=[("qPL", bceF_ld), ("sDVE", C("sDVE"))], incs=[("sPE", 1)])
        bcm = C("sPE")
        E("dve", lambda e, c=c: e.tensor_mul(
            out=oh8[:], in0=psC[:, 96:104], in1=ohall[:, c * 8:(c + 1) * 8]),
          waits=[("sPE", bcm)], incs=[("sDVE", 1)])
        E("dve", lambda e: e.tensor_reduce(out=rmaxb[:], in_=oh8[:], axis=AX.X,
                                           op=ALU.add),
          waits=[("sDVE", C("sDVE"))], incs=[("sDVE", 1)])
        E("dve", lambda e, c=c: e.tensor_add(out=rmaxb[:], in0=rmaxb[:],
                                             in1=part1[:, c:c + 1]),
          waits=[("sDVE", C("sDVE"))], incs=[("sDVE", 1)])
        E("dve", lambda e, c=c: e.tensor_copy(out=slotI[:, c:c + 1], in_=rmaxb[:]),
          waits=[("sDVE", C("sDVE"))], incs=[("sDVE", 1)])
        slot_marks.append(C("sDVE"))
    slots_done = C("sDVE")

    xs_scattered = 0
    xnb5_war = [0, 0]
    xsb_war = [0, 0]
    for c in range(NCHUNK):
        par = c % 2
        for g2 in range(2):
            for s4 in range(4):
                kc = g2 * 4 + s4
                E("pe", lambda e, kc=kc, c=c, s4=s4: e.transpose(
                    out=psT[:, s4 * 128:(s4 + 1) * 128],
                    in_=xnT[:, kc * L + c * 128:kc * L + (c + 1) * 128],
                    identity=identb[:]),
                  waits=[("sDVE", psT_evac[0]), ("sDVE", xnb5_war[par])],
                  incs=[("sPE", 1)])
            td = C("sPE")
            for s4 in range(4):
                kc = g2 * 4 + s4
                E("dve", lambda e, kc=kc, s4=s4, par=par: e.tensor_copy(
                    out=xnb5[:, par * DM + kc * 128:par * DM + (kc + 1) * 128],
                    in_=psT[:, s4 * 128:(s4 + 1) * 128]),
                  waits=[("sPE", td)], incs=[("sDVE", 1)])
            psT_evac[0] = C("sDVE")
        xn_tr = C("sDVE")
        E("dve", lambda e, c=c, par=par: e.tensor_scalar_mul(
            out=xsb[:, par * DM:(par + 1) * DM], in0=xnb5[:, par * DM:(par + 1) * DM],
            scalar1=rwall[:, c:c + 1]),
          waits=[("sDVE", xn_tr), ("qPL", xsb_war[par]), ("sDVE", slots_done)],
          incs=[("sDVE", 1)])
        xnb5_war[par] = C("sDVE")
        xsd = C("sDVE")
        if SKIP_SCATTER:
            E("pool", lambda e, c=c: e.dma_start(
                out=xsH[c * 128:(c + 1) * 128, :],
                in_=xsb[:, par * DM:(par + 1) * DM]),
              waits=[("sDVE", xsd)], incs=[("qPL", 16)])
        else:
            E("pool", lambda e, c=c, par=par: e.indirect_dma_start(
                out=xsH, out_offset=bass.IndirectOffsetOnAxis(slotI[:, c:c + 1], 0),
                in_=xsb[:, par * DM:(par + 1) * DM], in_offset=None),
              waits=[("sDVE", xsd)], incs=[("qPL", 16)])
        xsb_war[par] = C("qPL")
        xs_scattered = C("qPL")

    def load_expert_sp(e_idx, slot, war_pe):
        cm = [C("sPOOL")]
        slot_cast = [C("sPOOL"), C("sPOOL")]
        marks = []
        for (wsrc, dst_base, nkc, wid) in [
            (wgT_h, 0, 8, 512), (wuT_h, 4096, 8, 512), (wdT_h, 8192, 4, 1024)]:
            for kc in range(nkc):
                sl = stage[:, (kc % 2) * 1024:(kc % 2) * 1024 + wid]
                E("sy", lambda e, e_idx=e_idx, kc=kc, sl=sl, wsrc=wsrc: e.dma_start(
                    out=sl, in_=wsrc[e_idx, kc * 128:(kc + 1) * 128, :]),
                  waits=[("sPOOL", slot_cast[kc % 2])], incs=[("qSY", 16)])
                if dst_base < 8192:
                    E("pool", lambda e, kc=kc, slot=slot, sl=sl, dst_base=dst_base,
                      wid=wid: e.tensor_copy(
                        out=WA[:, slot * 8192 + dst_base + kc * wid:
                                slot * 8192 + dst_base + (kc + 1) * wid], in_=sl),
                      waits=[("qSY", C("qSY")), ("sPE", war_pe)], incs=[("sPOOL", 1)])
                else:
                    E("pool", lambda e, kc=kc, slot=slot, sl=sl, wid=wid: e.tensor_copy(
                        out=WB[:, slot * 4096 + kc * wid:slot * 4096 + (kc + 1) * wid],
                        in_=sl),
                      waits=[("qSY", C("qSY")), ("sPE", war_pe)], incs=[("sPOOL", 1)])
                cm[0] = C("sPOOL")
                slot_cast[kc % 2] = C("sPOOL")
            marks.append(cm[0])
        return marks

    ew = {}
    ew[0] = load_expert_sp(0, 0, 0)
    arena_pe = [0, 0]
    gs_stored = 0
    xsb_war5 = [0, 0]
    down_evac = 0
    for e_i in range(8):
        slot = e_i % 2
        wm = ew[e_i]
        if e_i + 1 < 8:
            ew[e_i + 1] = load_expert_sp(e_i + 1, 1 - slot, arena_pe[1 - slot])
        for r in range(8):
            par = r % 2
            E("sy", lambda e, e_i=e_i, r=r, par=par: e.dma_start(
                out=xsb[:, par * DM:(par + 1) * DM],
                in_=xsH[e_i * CAP + r * 128:e_i * CAP + (r + 1) * 128, :]),
              waits=[("qPL", xs_scattered), ("sPE", xsb_war5[par])],
              incs=[("qSY", 16)])
            xs_ld = C("qSY")
            for g in range(2):
                for s4 in range(4):
                    fc = g * 4 + s4
                    E("pe", lambda e, fc=fc, s4=s4, par=par: e.transpose(
                        out=psT[:, s4 * 128:(s4 + 1) * 128],
                        in_=xsb[:, par * DM + fc * 128:par * DM + (fc + 1) * 128],
                        identity=identb[:]),
                      waits=[("qSY", xs_ld), ("sDVE", psT_evac[0])],
                      incs=[("sPE", 1)])
                td = C("sPE")
                for s4 in range(4):
                    fc = g * 4 + s4
                    E("dve", lambda e, fc=fc, s4=s4, par=par: e.tensor_copy(
                        out=xsTb[:, par * DM + fc * 128:par * DM + (fc + 1) * 128],
                        in_=psT[:, s4 * 128:(s4 + 1) * 128]),
                      waits=[("sPE", td)], incs=[("sDVE", 1)])
                psT_evac[0] = C("sDVE")
            xsT_done = C("sDVE")
            for kc in range(8):
                wts = [("sPOOL", wm[0]), ("sDVE", xsT_done)]
                if kc == 0:
                    wts.append(("sACT", C("sACT")))
                E("pe", lambda e, kc=kc, slot=slot, par=par: e.matmul(
                    out=psB[:, 1024:1536],
                    lhsT=xsTb[:, par * DM + kc * 128:par * DM + (kc + 1) * 128],
                    rhs=WA[:, slot * 8192 + kc * 512:slot * 8192 + (kc + 1) * 512],
                    start=(kc == 0), stop=(kc == 7)),
                  waits=wts, incs=([("sPE", 1)] if kc == 7 else []))
            gmm = C("sPE")
            for kc in range(8):
                wts = [("sPOOL", wm[1]), ("sDVE", xsT_done)]
                if kc == 0:
                    wts.append(("sDVE", C("sDVE")))
                E("pe", lambda e, kc=kc, slot=slot, par=par: e.matmul(
                    out=psB[:, 1536:2048],
                    lhsT=xsTb[:, par * DM + kc * 128:par * DM + (kc + 1) * 128],
                    rhs=WA[:, slot * 8192 + 4096 + kc * 512:
                            slot * 8192 + 4096 + (kc + 1) * 512],
                    start=(kc == 0), stop=(kc == 7)),
                  waits=wts, incs=([("sPE", 1)] if kc == 7 else []))
            umm = C("sPE")
            xsb_war5[par] = umm
            E("act", lambda e: e.activation(out=sgb[:], in_=psB[:, 1024:1536],
                                            func=AF.Silu),
              waits=[("sPE", gmm)], incs=[("sACT", 1)])
            sgd = C("sACT")
            E("act", lambda e: e.activation(out=upb[:], in_=psB[:, 1536:2048],
                                            func=AF.Copy),
              waits=[("sPE", umm)], incs=[("sACT", 1)])
            upd = C("sACT")
            E("dve", lambda e: e.tensor_mul(out=pbf_[:], in0=sgb[:], in1=upb[:]),
              waits=[("sACT", sgd), ("sACT", upd), ("sPE", C("sPE"))],
              incs=[("sDVE", 1)])
            pd = C("sDVE")
            for tq in range(4):
                E("pe", lambda e, tq=tq: e.transpose(
                    out=psT[:, tq * 128:(tq + 1) * 128],
                    in_=pbf_[:, tq * 128:(tq + 1) * 128], identity=identb[:]),
                  waits=[("sDVE", pd), ("sDVE", psT_evac[0])], incs=[("sPE", 1)])
            td = C("sPE")
            for tq in range(4):
                E("dve", lambda e, tq=tq: e.tensor_copy(
                    out=pTb[:, tq * 128:(tq + 1) * 128],
                    in_=psT[:, tq * 128:(tq + 1) * 128]),
                  waits=[("sPE", td)], incs=[("sDVE", 1)])
            psT_evac[0] = C("sDVE")
            ptd = C("sDVE")
            for kd in range(4):
                for jj in range(2):
                    wts = [("sDVE", ptd), ("sPOOL", wm[2])]
                    if kd == 0 and jj == 0:
                        wts.append(("sACT", down_evac))
                    E("pe", lambda e, kd=kd, jj=jj, slot=slot: e.matmul(
                        out=psB[:, jj * 512:(jj + 1) * 512],
                        lhsT=pTb[:, kd * 128:(kd + 1) * 128],
                        rhs=WB[:, slot * 4096 + kd * 1024 + jj * 512:
                                slot * 4096 + kd * 1024 + (jj + 1) * 512],
                        start=(kd == 0), stop=(kd == 3)),
                      waits=wts, incs=([("sPE", 1)] if (kd == 3 and jj == 1) else []))
            dmm = C("sPE")
            arena_pe[slot] = dmm
            E("act", lambda e: e.activation(out=ffnacc[:].bitcast(BF16)[:, 0:DM],
                                            in_=psB[:, 0:1024], func=AF.Silu),
              waits=[("sPE", dmm), ("qSC", gs_stored)], incs=[("sACT", 1)])
            gsa = C("sACT")
            down_evac = gsa
            E("act", lambda e, e_i=e_i, r=r: e.dma_start(
                out=gsH[e_i * CAP + r * 128:e_i * CAP + (r + 1) * 128, :],
                in_=ffnacc[:].bitcast(BF16)[:, 0:DM]),
              waits=[("sACT", gsa)], incs=[("qSC", 16)])
            gs_stored = C("qSC")
    gs_all = gs_stored
    if PHASES == "pb" and DBG_SCRATCH:
        E("dve", lambda e: e.tensor_copy(out=sgb[:, 0:32], in_=slotI[:]),
          waits=[("sDVE", C("sDVE"))], incs=[("sDVE", 1)])
        sf = C("sDVE")
        E("sy", lambda e: e.dma_start(out=out_h[0:128, 0:32], in_=part1[:]),
          waits=[("sDVE", sf)], incs=[("qSC", 16)])
        E("sy", lambda e: e.dma_start(out=out_h[0:128, 32:288], in_=ohall[:]),
          waits=[("sDVE", sf)], incs=[("qSC", 16)])
        E("sy", lambda e: e.dma_start(out=out_h[0:128, 288:320], in_=rwall[:, 0:32]),
          waits=[("sDVE", sf)], incs=[("qSC", 16)])
        E("sy", lambda e: e.dma_start(out=out_h[0:128, 320:352], in_=sgb[:, 0:32]),
          waits=[("sDVE", sf)], incs=[("qSC", 16)])
        E("sy", lambda e: e.dma_start(out=out_h[128:129, 0:256], in_=bceF[:]),
          waits=[("sDVE", sf)], incs=[("qSC", 16)])
        for rr in range(8):
            E("sy", lambda e, rr=rr: e.dma_start(
                out=out_h[512 + rr * 128:512 + (rr + 1) * 128, 0:512].bitcast(BF16),
                in_=xsH[rr * 1024:rr * 1024 + 128, :]),
              waits=[("qPL", xs_scattered)], incs=[("qSC", 16)])
        for nm, srcap, dst0 in [("xn0", xnH[0], 2000), ("xs0", xsH[0:128, :], 2128),
                                ("gs0", gsH[0:128, :], 2256), ("xs1", xsH[1024:1152, :], 2384)]:
            E("sy", lambda e, srcap=srcap: e.dma_start(
                out=xsb[:, 0:DM], in_=srcap),
              waits=[("qSC", gs_all), ("qPL", C("qPL")), ("sDVE", C("sDVE")),
                     ("qSY", C("qSY")), ("sPE", C("sPE")), ("sACT", C("sACT")),
                     ("sPOOL", C("sPOOL"))],
              incs=[("qSY", 16)])
            E("dve", lambda e: e.tensor_copy(out=ffnacc[:], in_=xsb[:, 0:DM]),
              waits=[("qSY", C("qSY"))], incs=[("sDVE", 1)])
            E("sy", lambda e, dst0=dst0: e.dma_start(
                out=out_h[dst0:dst0 + 128, :], in_=ffnacc[:]),
              waits=[("sDVE", C("sDVE"))], incs=[("qSC", 16)])
        b.wait("sy", [("qSC", C("qSC"))])
    if DBG_SCRATCH:
        E("sy", lambda e: e.dma_start(out=xsD, in_=xsH),
          waits=[("qPL", xs_scattered)], incs=[("qSY", 16)])
        E("sy", lambda e: e.dma_start(out=gsD, in_=gsH),
          waits=[("qSC", gs_all)], incs=[("qSY", 16)])
        b.cnt["qSY"] += 0

    if PHASES == "pb":
        return finalize()
    b.barrier()

    slot_cast = [C("sPOOL"), C("sPOOL")]
    for et in range(NT):
        par = et % 2
        sl = stage[:, par * 1024:(par + 1) * 1024]
        E("sy", lambda e, et=et, sl=sl: e.dma_start(
            out=sl, in_=woutT_h[et * 128:(et + 1) * 128, :]),
          waits=[("sPOOL", slot_cast[par])], incs=[("qSY", 16)])
        E("pool", lambda e, et=et, sl=sl: e.tensor_copy(
            out=WA[:, et * 1024:(et + 1) * 1024], in_=sl),
          waits=[("qSY", C("qSY"))], incs=[("sPOOL", 1)])
        slot_cast[par] = C("sPOOL")
    wout_done = C("sPOOL")

    out_stored = 0
    mm5_done = [0, 0]
    evacd5 = [0, 0]
    for c in range(NCHUNK):
        par = c % 2
        E("sy", lambda e, c=c, par=par: e.dma_start(
            out=yzin[:, par * DI:(par + 1) * DI], in_=yzH[c]),
          waits=[("sPE", mm5_done[par])], incs=[("qSY", 16)])
        yz_ld = C("qSY")
        E("sy", lambda e, c=c, par=par: e.dma_start(
            out=xt5[:, par * DM:(par + 1) * DM], in_=x_h[c * 128:(c + 1) * 128, :]),
          waits=[("sDVE", evacd5[par])], incs=[("qSY", 16)])
        x_ld = C("qSY")
        E("pool", lambda e, c=c, par=par: e.indirect_dma_start(
            out=g5[:, par * DM:(par + 1) * DM], out_offset=None,
            in_=gsH, in_offset=bass.IndirectOffsetOnAxis(slotI[:, c:c + 1], 0)),
          waits=[("qSC", gs_all), ("sDVE", evacd5[par])], incs=[("qPL", 16)])
        g_ld = C("qPL")
        for et in range(NT):
            for jj in range(2):
                wts = [("qSY", yz_ld), ("sPOOL", wout_done)]
                if et == 0 and jj == 0:
                    wts.append(("sDVE", evacd5[par]))
                E("pe", lambda e, et=et, par=par, jj=jj: e.matmul(
                    out=psB[:, par * 1024 + jj * 512:par * 1024 + (jj + 1) * 512],
                    lhsT=yzin[:, par * DI + et * 128:par * DI + (et + 1) * 128],
                    rhs=WA[:, et * 1024 + jj * 512:et * 1024 + (jj + 1) * 512],
                    start=(et == 0), stop=(et == NT - 1)),
                  waits=wts, incs=([("sPE", 1)] if (et == NT - 1 and jj == 1) else []))
        smm = C("sPE")
        mm5_done[par] = smm
        E("dve", lambda e, par=par: e.tensor_mul(
            out=outb[:, par * DM:(par + 1) * DM],
            in0=psB[:, par * 1024:(par + 1) * 1024],
            in1=g5[:, par * DM:(par + 1) * DM]),
          waits=[("sPE", smm), ("qPL", g_ld), ("qSC", out_stored)],
          incs=[("sDVE", 1)])
        E("dve", lambda e, par=par: e.tensor_add(
            out=outb[:, par * DM:(par + 1) * DM],
            in0=outb[:, par * DM:(par + 1) * DM],
            in1=xt5[:, par * DM:(par + 1) * DM]),
          waits=[("sDVE", C("sDVE")), ("qSY", x_ld)], incs=[("sDVE", 1)])
        evacd5[par] = C("sDVE")
        E("act", lambda e, c=c, par=par: e.dma_start(
            out=out_h[c * 128:(c + 1) * 128, :], in_=outb[:, par * DM:(par + 1) * DM]),
          waits=[("sDVE", evacd5[par])], incs=[("qSC", 16)])
        out_stored = C("qSC")

    b.wait("sy", [("qSC", out_stored)])
    b.wait("act", [("qSC", out_stored)])

    return finalize()


_prog_cache = {}


def prep_inputs(L0_attn_x, L0_attn_W_in, L0_attn_conv_w, L0_attn_conv_b, L0_attn_W_x,
                L0_attn_W_dt, L0_attn_b_dt, L0_attn_A_log, L0_attn_D, L0_attn_W_out,
                L0_ffn_gate_w, L0_ffn_w_gate, L0_ffn_w_up, L0_ffn_w_down):
    x = np.ascontiguousarray(np.asarray(L0_attn_x, np.float32))
    winT = np.ascontiguousarray(np.asarray(L0_attn_W_in, np.float32).T)
    wxdtT = np.ascontiguousarray(
        np.concatenate([np.asarray(L0_attn_W_x, np.float32),
                        np.asarray(L0_attn_W_dt, np.float32)], 0).T)
    woutT = np.ascontiguousarray(np.asarray(L0_attn_W_out, np.float32).T)
    cw = np.asarray(L0_attn_conv_w, np.float32)[:, 0, :]
    convw = np.ascontiguousarray(
        cw.reshape(NT, 128, 4).transpose(1, 0, 2).reshape(128, NT * 4))
    convb = np.ascontiguousarray(
        np.asarray(L0_attn_conv_b, np.float32).reshape(NT, 128).T)
    dfm = np.ascontiguousarray(np.asarray(L0_attn_D, np.float32).reshape(NT, 128).T)
    bdt = np.asarray(L0_attn_b_dt, np.float32)[None, :]
    alog = np.asarray(L0_attn_A_log, np.float32)[None, :]
    gwT = np.ascontiguousarray(np.asarray(L0_ffn_gate_w, np.float32).T)
    wgT = np.ascontiguousarray(np.asarray(L0_ffn_w_gate, np.float32).transpose(0, 2, 1))
    wuT = np.ascontiguousarray(np.asarray(L0_ffn_w_up, np.float32).transpose(0, 2, 1))
    wdT = np.ascontiguousarray(np.asarray(L0_ffn_w_down, np.float32).transpose(0, 2, 1))

    shared = dict(winT=winT, wxdtT=wxdtT, woutT=woutT, convw=convw, convb=convb,
                  dfm=dfm, bdt=bdt, alog=alog, gwT=gwT, wgT=wgT, wuT=wuT, wdT=wdT)
    return shared, x


def kernel(**inputs):
    shared, x = prep_inputs(**inputs)
    nc = build_program()
    in_maps = [dict(shared, x=np.ascontiguousarray(x[bb])) for bb in range(8)]
    res = run_bass_kernel_spmd(nc, in_maps, core_ids=list(range(8)))
    return np.stack([r["out"] for r in res.results], axis=0)


# version 16
# speedup vs baseline: 1.4761x; 1.0695x over previous
import numpy as np
from collections import defaultdict

import concourse.bass as bass
import concourse.mybir as mybir
from concourse.bass_utils import run_bass_kernel_spmd

F32 = mybir.dt.float32
BF16 = mybir.dt.bfloat16
AF = mybir.ActivationFunctionType
ALU = mybir.AluOpType
AX = mybir.AxisListType

L = 4096
DM = 1024
DI = 2048
NS = 16
Q = 128
NCHUNK = L // Q
NT = 16
TW = 512
NTW = L // TW
NHG = L // 256
PHASES = "all"
P0_LIMIT = NCHUNK
P0_SKIP = ""
SKIP_SCATTER = False
DBG_SCRATCH = False


class Builder:
    ENGS = ("sy", "pe", "act", "dve", "pool")

    def __init__(self):
        self.q = {e: [] for e in self.ENGS}
        self.cnt = defaultdict(int)
        self.sem = None

    def emit(self, eng, fn, waits=(), incs=()):
        waits = [(s, v) for (s, v) in waits if v > 0]
        for (s, v) in waits:
            assert v <= self.cnt[s], f"wait on future count {s} {v} > {self.cnt[s]}"
        incs = [(s, n) for (s, n) in incs]
        for (s, n) in incs:
            self.cnt[s] += n

        def cl(ec, _fn=fn, _w=waits, _i=incs):
            for (s, v) in _w:
                ec.wait_ge(self.sem[s], v)
            inst = _fn(ec)
            for (s, n) in _i:
                inst.then_inc(self.sem[s], n)
        self.q[eng].append(cl)
        return {s: self.cnt[s] for s, _ in incs}

    def c(self, s):
        return self.cnt[s]

    def wait(self, eng, waits):
        def cl(ec, _w=list(waits)):
            for (s, v) in _w:
                ec.wait_ge(self.sem[s], v)
        self.q[eng].append(cl)

    def barrier(self):
        targets = {}
        for e in self.ENGS:
            self.cnt["b" + e] += 1
            targets[e] = self.cnt["b" + e]
        for e in self.ENGS:
            def cl(ec, _e=e, _t=dict(targets)):
                ec.sem_inc(self.sem["b" + _e], 1)
                for oe, v in _t.items():
                    ec.wait_ge(self.sem["b" + oe], v)
            self.q[e].append(cl)


def build_program():
    nc = bass.Bass("TRN2", target_bir_lowering=False, debug=False)
    b = Builder()
    E, C = b.emit, b.c

    x_h = nc.dram_tensor("x", [L, DM], F32, kind="ExternalInput").ap()
    winT_h = nc.dram_tensor("winT", [DM, 2 * DI], F32, kind="ExternalInput").ap()
    wxdtT_h = nc.dram_tensor("wxdtT", [DI, 48], F32, kind="ExternalInput").ap()
    woutT_h = nc.dram_tensor("woutT", [DI, DM], F32, kind="ExternalInput").ap()
    convw_h = nc.dram_tensor("convw", [128, NT * 4], F32, kind="ExternalInput").ap()
    convb_h = nc.dram_tensor("convb", [128, NT], F32, kind="ExternalInput").ap()
    dfm_h = nc.dram_tensor("dfm", [128, NT], F32, kind="ExternalInput").ap()
    bdt_h = nc.dram_tensor("bdt", [1, NS], F32, kind="ExternalInput").ap()
    alog_h = nc.dram_tensor("alog", [1, NS], F32, kind="ExternalInput").ap()
    gwT_h = nc.dram_tensor("gwT", [DM, 8], F32, kind="ExternalInput").ap()
    wgT_h = nc.dram_tensor("wgT", [8, DM, 512], F32, kind="ExternalInput").ap()
    wuT_h = nc.dram_tensor("wuT", [8, DM, 512], F32, kind="ExternalInput").ap()
    wdT_h = nc.dram_tensor("wdT", [8, 512, DM], F32, kind="ExternalInput").ap()
    out_h = nc.dram_tensor("out", [L, DM], F32, kind="ExternalOutput").ap()

    KSC = "ExternalOutput" if DBG_SCRATCH else "Internal"
    ufmH = nc.dram_tensor("ufmH", [NCHUNK, 128, NT, Q], BF16, kind=KSC).ap()
    utokH = nc.dram_tensor("utokH", [NCHUNK, 128, DI], BF16, kind=KSC).ap()
    bcH = nc.dram_tensor("bcH", [48, L], F32, kind=KSC).ap()
    yzH = nc.dram_tensor("yzH", [NCHUNK, 128, DI], BF16, kind=KSC).ap()
    gH = nc.dram_tensor("gH", [NCHUNK, 128, DM], BF16, kind=KSC).ap()
    xnH = nc.dram_tensor("xnH", [NCHUNK, 128, DM], BF16, kind="Internal").ap()
    xsH = nc.dram_tensor("xsH", [8 * 1024, DM], BF16, kind="Internal").ap()
    gsH = nc.dram_tensor("gsH", [8 * 1024, DM], BF16, kind="Internal").ap()
    xsD = nc.dram_tensor("xsD", [8 * 1024, DM], BF16, kind=KSC).ap() if DBG_SCRATCH else None
    gsD = nc.dram_tensor("gsD", [8 * 1024, DM], BF16, kind=KSC).ap() if DBG_SCRATCH else None
    bceH = nc.dram_tensor("bceH", [NCHUNK, 8], F32, kind="Internal").ap()
    slotH = nc.dram_tensor("slotH", [128, NCHUNK], mybir.dt.int32, kind=KSC).ap()
    rtD = nc.dram_tensor("rtD", [128, 640], F32, kind=KSC).ap()
    partH = nc.dram_tensor("partH", [NCHUNK, 128, DM], F32, kind="Internal").ap()
    wzHd = nc.dram_tensor("wzHd", [DM, DI], BF16, kind="Internal").ap()
    fcumH = nc.dram_tensor("fcumH", [NS, Q], F32, kind="Internal").ap()
    fcH = nc.dram_tensor("fcH", [NS, Q], BF16, kind="Internal").ap()

    def sb(name, shape, dtype):
        return nc.alloc_sbuf_tensor(name, shape, dtype)

    xnT = sb("xnT", [128, 8 * L], BF16)
    hst = sb("hst", [NS, DI], F32)
    triu = sb("triu", [128, 128], F32)
    mneg = sb("mneg", [128, 128], F32)
    identb = sb("identb", [128, 128], BF16)
    identf = sb("identf", [128, 128], F32)
    onescol = sb("onescol", [128, 1], F32)
    ones1f = sb("ones1f", [1, 128], F32)
    ones1b = sb("ones1b", [1, 128], BF16)
    convw = sb("convw_s", [128, NT * 4], F32)
    convb = sb("convb_s", [128, NT], F32)
    dfm = sb("dfm_s", [128, NT], F32)
    bdt_r = sb("bdt_r", [1, NS], F32)
    alog_r = sb("alog_r", [1, NS], F32)
    arow = sb("arow", [1, NS], F32)
    bdtB = sb("bdtB", [128, NS], F32)
    aB = sb("aB", [128, NS], F32)
    wxdtT = sb("wxdtT_s", [128, NT * 48], BF16)
    gwT = sb("gwT_s", [128, 8 * 8], BF16)
    rwall = sb("rwall", [128, NCHUNK * 8], F32)
    slotI = sb("slotI", [128, NCHUNK], mybir.dt.int32)
    tris = sb("tris", [128, 128], F32)
    ecapB = sb("ecapB", [128, 8], F32)
    wdec = sb("wdec", [128, 32], F32)
    rmax4 = sb("rmax4", [128, 4], F32)
    ohb = sb("ohb", [128, 32], F32)
    limB = sb("limB", [128, 8], F32)
    stage = sb("stage", [128, 2048], F32)
    WA = sb("WA", [128, 16 * 1024], BF16)
    WB = sb("WB", [128, 16 * 1024], BF16)

    SCR = 52 * 1024
    scr = sb("scr", [128, SCR], mybir.dt.uint8)
    scr_addr = nc.lookup_mloc(scr).addr

    def at(name, off, shape, dtype):
        assert off % 64 == 0
        nbytes = int(np.prod(shape[1:])) * mybir.dt.size(dtype)
        assert off + nbytes <= SCR, f"{name} overflows scratch: {off}+{nbytes}"
        return nc.alloc_sbuf_tensor_at(name, shape, dtype, offset=scr_addr + off)

    xt0 = at("xt0", 0, [128, 2 * DM], F32)
    xnb = at("xnb", 8192, [128, DM], BF16)
    st1 = at("st1", 10240, [128, 1], F32)
    st2 = at("st2", 10304, [128, 1], F32)
    st3 = at("st3", 10368, [128, 1], F32)
    tileb = at("tileb", 0, [128, 2 * (TW + 3)], F32)
    convacc = at("convacc", 4224, [128, 2 * TW], F32)
    tails = at("tails", 8320, [128, NT * 3], F32)
    bcev = at("bcev", 8576, [48, TW], F32)
    ufm16 = at("ufm16", 10624, [128, NT * TW], BF16)
    utokb = at("utokb", 27008, [128, 2 * DI], BF16)
    convacc2 = at("convacc2", 35200, [128, 2 * TW], F32)
    tbuf2 = at("tbuf2", 39296, [128, TW], F32)
    szf = at("szf", 0, [128, 2 * NT * 256], BF16)
    ut4 = at("ut4", 16384, [128, DI], BF16)
    uf4 = at("uf4", 20480, [128, DI], BF16)
    mb1 = at("mb1", 24576, [128, NS * Q], F32)
    mb2 = at("mb2", 32768, [128, NS * Q], BF16)
    yzb = at("yzb", 36864, [128, DI], BF16)
    flt = at("flt", 40960, [1, NS * Q], F32)
    fltc = at("fltc", 49152, [1, NS * Q], BF16)
    sm_cum = sb("sm_cum", [128, NS], F32)
    sm_mbias = sb("sm_mbias", [128, 2 * NS], F32)
    sm_dB = sb("sm_dB", [128, NS], F32)
    sm_lnb = sb("sm_lnb", [128, NS], F32)
    sm_sgn = sb("sm_sgn", [128, 2 * NS], F32)
    sm_dlt = sb("sm_dlt", [128, NS], F32)
    sm_a = sb("sm_a", [128, NS], F32)
    sm_b = sb("sm_b", [128, NS], F32)
    sm_cbf = sb("sm_cbf", [128, NS], BF16)
    sm_ctl = sb("sm_ctl", [128, NS], F32)
    cumT_e = sb("cumT_e", [NS, Q], F32)
    cb16 = sb("cb16", [NS, Q], BF16)
    ctT = sb("ctT", [NS, 2 * Q], F32)
    btok = sb("btok", [128, 2 * NS], BF16)
    dec_c = sb("dec_c", [NS, 2], F32)
    clF = sb("clF", [1, NS], F32)
    msb = sb("msb", [128, Q], F32)
    mbf = sb("mbf", [128, Q], BF16)
    bc4 = sb("bc4", [48, 2 * Q], F32)
    sgb = at("sgb", 0, [128, 512], F32)
    upb = at("upb", 2048, [128, 512], F32)
    pbf_ = at("pbf_", 4096, [128, 512], BF16)
    pTb = at("pTb", 5120, [128, 512], BF16)
    ffnacc = at("ffnacc", 6144, [128, DM], F32)
    rmaxb = at("rmaxb", 22528, [128, 1], F32)
    rwb = at("rwb", 22592, [128, 1], F32)
    oh8 = at("oh8", 22656, [128, 8], F32)
    cum8 = at("cum8", 22720, [128, 8], F32)
    z8 = at("z8", 22784, [128, 8], F32)
    yzin = at("yzin", 24576, [128, 2 * DI], BF16)
    outb = at("outb", 32768, [128, 2 * DM], F32)
    xt5 = at("xt5", 40960, [128, 2 * DM], F32)
    g5 = at("g5", 49152, [128, 2 * DM], BF16)

    psA = nc.alloc_psum_tensor("psA", [128, 1024], F32)
    psB = nc.alloc_psum_tensor("psB", [128, 2048], F32)
    psC = nc.alloc_psum_tensor("psC", [128, 512], F32)
    psT = nc.alloc_psum_tensor("psT", [128, 1024], BF16)

    for sn in (["qSY", "qSC", "qPL", "sPE", "sACT", "sDVE", "sPOOL"] +
               ["b" + e for e in Builder.ENGS]):
        pass
    b.sem = {sn: nc.alloc_semaphore(sn) for sn in
             ["qSY", "qSC", "qPL", "sPE", "sACT", "sDVE", "sPOOL"] +
             ["b" + e for e in Builder.ENGS]}

    def finalize():
        b.wait("sy", [("qPL", C("qPL")), ("qSY", C("qSY")), ("qSC", C("qSC"))])
        b.barrier()
        with nc.Block() as block:
            @block.sync
            def _(eng):
                for cl in b.q["sy"]:
                    cl(eng)

            @block.tensor
            def _(eng):
                for cl in b.q["pe"]:
                    cl(eng)

            @block.scalar
            def _(eng):
                for cl in b.q["act"]:
                    cl(eng)

            @block.vector
            def _(eng):
                for cl in b.q["dve"]:
                    cl(eng)

            @block.gpsimd
            def _(eng):
                for cl in b.q["pool"]:
                    cl(eng)
        return nc

    for ap_sb, ap_h in [(convw, convw_h), (convb, convb_h), (dfm, dfm_h),
                        (bdt_r, bdt_h), (alog_r, alog_h)]:
        E("sy", lambda e, a=ap_sb, h=ap_h: e.dma_start(out=a[:], in_=h),
          incs=[("qSY", 16)])
    small_in = C("qSY")

    E("pool", lambda e: e.memset(triu[:], 1.0), incs=[("sPOOL", 1)])
    E("pool", lambda e: e.affine_select(out=triu[:], in_=triu[:], compare_op=ALU.is_ge,
                                        fill=0.0, base=0, pattern=[[1, 128]],
                                        channel_multiplier=-1),
      waits=[("sPOOL", C("sPOOL"))], incs=[("sPOOL", 1)])
    E("pool", lambda e: e.memset(mneg[:], 0.0), incs=[("sPOOL", 1)])
    E("pool", lambda e: e.affine_select(out=mneg[:], in_=mneg[:], compare_op=ALU.is_ge,
                                        fill=-1e9, base=0, pattern=[[1, 128]],
                                        channel_multiplier=-1),
      waits=[("sPOOL", C("sPOOL"))], incs=[("sPOOL", 1)])
    E("pool", lambda e: e.memset(identb[:], 0.0), incs=[("sPOOL", 1)])
    E("pool", lambda e: e.affine_select(out=identb[:], in_=identb[:],
                                        compare_op=ALU.not_equal, fill=1.0, base=0,
                                        pattern=[[-1, 128]], channel_multiplier=1),
      waits=[("sPOOL", C("sPOOL"))], incs=[("sPOOL", 1)])
    E("pool", lambda e: e.memset(identf[:], 0.0), incs=[("sPOOL", 1)])
    E("pool", lambda e: e.affine_select(out=identf[:], in_=identf[:],
                                        compare_op=ALU.not_equal, fill=1.0, base=0,
                                        pattern=[[-1, 128]], channel_multiplier=1),
      waits=[("sPOOL", C("sPOOL"))], incs=[("sPOOL", 1)])
    E("pool", lambda e: e.memset(onescol[:], 1.0))
    E("pool", lambda e: e.memset(ones1f[:], 1.0))
    E("pool", lambda e: e.memset(ones1b[:], 1.0))
    E("pool", lambda e: e.memset(tris[:], 1.0), incs=[("sPOOL", 1)])
    E("pool", lambda e: e.affine_select(out=tris[:], in_=tris[:], compare_op=ALU.is_gt,
                                        fill=0.0, base=0, pattern=[[1, 128]],
                                        channel_multiplier=-1),
      waits=[("sPOOL", C("sPOOL"))], incs=[("sPOOL", 1)])
    E("pool", lambda e: e.iota(out=ecapB[:], pattern=[[1024, 8]], base=0,
                               channel_multiplier=0,
                               allow_small_or_imprecise_dtypes=True))
    E("pool", lambda e: e.iota(out=limB[:], pattern=[[1024, 8]], base=1023,
                               channel_multiplier=0,
                               allow_small_or_imprecise_dtypes=True))
    E("pool", lambda e: e.iota(out=wdec[:], pattern=[[0, 4], [-1, 8]], base=8,
                               channel_multiplier=0,
                               allow_small_or_imprecise_dtypes=True))
    E("pool", lambda e: e.memset(hst[:], 0.0), incs=[("sPOOL", 1)])
    consts_done = C("sPOOL")

    E("act", lambda e: e.activation(out=arow[:], in_=alog_r[:], func=AF.Exp),
      waits=[("qSY", small_in)], incs=[("sACT", 1)])
    E("dve", lambda e: e.tensor_scalar_mul(out=arow[:], in0=arow[:], scalar1=-1.0),
      waits=[("sACT", C("sACT"))], incs=[("sDVE", 1)])
    arow_done = C("sDVE")
    E("pe", lambda e: e.matmul(out=psC[:, 0:NS], lhsT=ones1f[:], rhs=bdt_r[:],
                               start=True, stop=True),
      waits=[("qSY", small_in), ("sPOOL", consts_done)], incs=[("sPE", 1)])
    E("pe", lambda e: e.matmul(out=psC[:, NS:2 * NS], lhsT=ones1f[:], rhs=arow[:],
                               start=True, stop=True),
      waits=[("sDVE", arow_done)], incs=[("sPE", 1)])
    bc_mm = C("sPE")
    E("dve", lambda e: e.tensor_copy(out=bdtB[:], in_=psC[:, 0:NS]),
      waits=[("sPE", bc_mm)], incs=[("sDVE", 1)])
    E("dve", lambda e: e.tensor_copy(out=aB[:], in_=psC[:, NS:2 * NS]),
      incs=[("sDVE", 1)])
    bcasts_done = C("sDVE")

    cast_mark = 0
    for kc in range(NT):
        sl = stage[:, (kc % 2) * 1024:(kc % 2) * 1024 + 48]
        wts = [("sPOOL", cast_mark)] if kc >= 2 else []
        E("sy", lambda e, kc=kc, sl=sl: e.dma_start(
            out=sl, in_=wxdtT_h[kc * 128:(kc + 1) * 128, :]),
          waits=wts, incs=[("qSY", 16)])
        E("pool", lambda e, kc=kc, sl=sl: e.tensor_copy(
            out=wxdtT[:, kc * 48:(kc + 1) * 48], in_=sl),
          waits=[("qSY", C("qSY"))], incs=[("sPOOL", 1)])
        cast_mark = C("sPOOL")
    for kc in range(8):
        sl = stage[:, (kc % 2) * 1024:(kc % 2) * 1024 + 8]
        E("sy", lambda e, kc=kc, sl=sl: e.dma_start(
            out=sl, in_=gwT_h[kc * 128:(kc + 1) * 128, :]),
          waits=[("sPOOL", cast_mark)], incs=[("qSY", 16)])
        E("pool", lambda e, kc=kc, sl=sl: e.tensor_copy(
            out=gwT[:, kc * 8:(kc + 1) * 8], in_=sl),
          waits=[("qSY", C("qSY"))], incs=[("sPOOL", 1)])
        cast_mark = C("sPOOL")

    if PHASES == "consts":
        return finalize()
    psT_evac = [0, 0, 0, 0]
    xn_cons = [0, 0]
    xnb_stored = 0
    for c in range(min(NCHUNK, P0_LIMIT)):
        par = c % 2
        xs = xt0[:, par * DM:(par + 1) * DM]
        wts = [("sDVE", xn_cons[par])] if c >= 2 else []
        E("sy", lambda e, c=c, xs=xs: e.dma_start(out=xs, in_=x_h[c * 128:(c + 1) * 128, :]),
          waits=wts, incs=[("qSY", 16)])
        ldc = C("qSY")
        if "l" in P0_SKIP:
            xnd = 0
        else:
         if True:
            E("dve", lambda e, xs=xs: e.tensor_reduce(out=st1[:], in_=xs, axis=AX.X,
                                                      op=ALU.add),
                  waits=[("qSY", ldc)], incs=[("sDVE", 1)])
            E("dve", lambda e: e.tensor_scalar_mul(out=st1[:], in0=st1[:], scalar1=1.0 / DM),
              waits=[("sDVE", C("sDVE"))], incs=[("sDVE", 1)])
            mdone = C("sDVE")
            E("act", lambda e, xs=xs: e.activation(out=psC[:, 0:512],
                                                   in_=xs[:, 0:512], func=AF.Square,
                                                   accum_out=st2[:]),
              waits=[("qSY", ldc)], incs=[("sACT", 1)])
            E("act", lambda e, xs=xs: e.activation(out=psC[:, 0:512], in_=xs[:, 512:1024],
                                                   func=AF.Square, accum_out=st3[:]),
              waits=[("sACT", C("sACT"))], incs=[("sACT", 1)])
            sqd = C("sACT")
            E("dve", lambda e: e.tensor_add(out=st2[:], in0=st2[:], in1=st3[:]),
              waits=[("sACT", sqd), ("sDVE", mdone)], incs=[("sDVE", 1)])
            E("dve", lambda e: e.tensor_mul(out=st3[:], in0=st1[:], in1=st1[:]),
              waits=[("sDVE", C("sDVE"))], incs=[("sDVE", 1)])
            E("dve", lambda e: e.tensor_scalar(out=st3[:], in0=st3[:], scalar1=-1.0,
                                               scalar2=1e-5, op0=ALU.mult, op1=ALU.add),
              waits=[("sDVE", C("sDVE"))], incs=[("sDVE", 1)])
            vb = C("sDVE")
            E("act", lambda e: e.activation(out=st3[:], in_=st2[:], func=AF.Ln,
                                            bias=st3[:], scale=1.0 / DM),
              waits=[("sDVE", vb)], incs=[("sACT", 1)])
            E("act", lambda e: e.activation(out=st2[:], in_=st3[:], func=AF.Exp, scale=-0.5),
              waits=[("sACT", C("sACT"))], incs=[("sACT", 1)])
            rstd = C("sACT")
            aff = C("sDVE")
            E("dve", lambda e, xs=xs: e.tensor_scalar(out=xnb[:], in0=xs, scalar1=st1[:],
                                                      scalar2=st2[:], op0=ALU.subtract,
                                                      op1=ALU.mult),
              waits=[("sDVE", aff), ("sACT", rstd)], incs=[("sDVE", 1)])
            xnd = C("sDVE")
        if "t" in P0_SKIP:
            xn_cons[par] = xnd if isinstance(xnd, int) else 0
            continue
        for g in range(2):
            for sl4 in range(4):
                kc = g * 4 + sl4
                tdst = psT[:, sl4 * 128:(sl4 + 1) * 128]
                E("pe", lambda e, kc=kc, tdst=tdst: e.transpose(
                    out=tdst, in_=xnb[:, kc * 128:(kc + 1) * 128], identity=identb[:]),
                  waits=[("sDVE", xnd), ("sDVE", psT_evac[0]),
                         ("sPOOL", consts_done)],
                  incs=[("sPE", 1)])
            td = C("sPE")
            for sl4 in range(4):
                kc = g * 4 + sl4
                tdst = psT[:, sl4 * 128:(sl4 + 1) * 128]
                E("dve", lambda e, kc=kc, c=c, tdst=tdst: e.tensor_copy(
                    out=xnT[:, kc * L + c * 128:kc * L + (c + 1) * 128], in_=tdst),
                  waits=[("sPE", td)], incs=[("sDVE", 1)])
            psT_evac[0] = C("sDVE")
        xn_cons[par] = xnd

    if PHASES == "p0":
        return finalize()
    b.barrier()

    slot_cast = [0, 0]
    idx = 0
    for kc in range(8):
        for hh in range(2):
            par = idx % 2
            sl = stage[:, par * 1024:(par + 1) * 1024]
            wts = [("sPOOL", slot_cast[par])] if slot_cast[par] else []
            E("sy", lambda e, kc=kc, hh=hh, sl=sl: e.dma_start(
                out=sl, in_=winT_h[kc * 128:(kc + 1) * 128, hh * 1024:(hh + 1) * 1024]),
              waits=wts, incs=[("qSY", 16)])
            E("pool", lambda e, kc=kc, hh=hh, sl=sl: e.tensor_copy(
                out=WA[:, kc * DI + hh * 1024:kc * DI + (hh + 1) * 1024], in_=sl),
              waits=[("qSY", C("qSY"))], incs=[("sPOOL", 1)])
            slot_cast[par] = C("sPOOL")
            idx += 1
    winxi_done = C("sPOOL")

    E("pool", lambda e: e.memset(tails[:], 0.0), incs=[("sPOOL", 1)])
    tails_zeroed = C("sPOOL")
    utok_stored = [0] * NCHUNK
    ufm_stored = [0] * NTW
    bc_stored = [0] * NTW
    psA_free = [0, 0]
    tile_free = [0, 0]
    ufm_cons_pe = 0
    ufm_cons_pl = 0
    utokb_free = [0, 0]
    for T in range(NTW):
        silu_marks = [0] * NT
        tapend_marks = [0] * NT
        for ft in range(NT):
            par = ft % 2
            tb = tileb[:, par * (TW + 3):(par + 1) * (TW + 3)]
            psl = psA[:, par * 512:(par + 1) * 512]
            for kc in range(8):
                wts = []
                if kc == 0:
                    wts = [("sPOOL", winxi_done), ("sDVE", psA_free[par]),
                           ("bsy", C("bsy"))]
                E("pe", lambda e, T=T, ft=ft, kc=kc, psl=psl: e.matmul(
                    out=psl,
                    lhsT=WA[:, kc * DI + ft * 128:kc * DI + (ft + 1) * 128],
                    rhs=xnT[:, kc * L + T * TW:kc * L + (T + 1) * TW],
                    start=(kc == 0), stop=(kc == 7)),
                  waits=wts, incs=([("sPE", 1)] if kc == 7 else []))
            mmd = C("sPE")
            E("dve", lambda e, ft=ft, tb=tb: e.tensor_copy(
                out=tb[:, 0:3], in_=tails[:, ft * 3:(ft + 1) * 3]),
              waits=[("sACT", tile_free[par]), ("sPOOL", tails_zeroed)],
              incs=[("sDVE", 1)])
            E("dve", lambda e, tb=tb, psl=psl: e.tensor_copy(out=tb[:, 3:3 + TW], in_=psl),
              waits=[("sPE", mmd)], incs=[("sDVE", 1)])
            psA_free[par] = C("sDVE")
            E("dve", lambda e, ft=ft, tb=tb: e.tensor_copy(
                out=tails[:, ft * 3:(ft + 1) * 3], in_=tb[:, TW:TW + 3]),
              waits=[("sDVE", C("sDVE"))], incs=[("sDVE", 1)])
            ca = convacc[:, par * TW:(par + 1) * TW]
            E("dve", lambda e, ft=ft, tb=tb, ca=ca: e.tensor_scalar_mul(
                out=ca, in0=tb[:, 0:TW], scalar1=convw[:, ft * 4:ft * 4 + 1]),
              waits=[("sDVE", C("sDVE")), ("sACT", tile_free[par])], incs=[("sDVE", 1)])
            for k in range(1, 4):
                E("dve", lambda e, ft=ft, k=k, tb=tb, ca=ca: e.scalar_tensor_tensor(
                    out=ca, in0=tb[:, k:k + TW],
                    scalar=convw[:, ft * 4 + k:ft * 4 + k + 1],
                    in1=ca, op0=ALU.mult, op1=ALU.add),
                  waits=[("sDVE", C("sDVE"))], incs=[("sDVE", 1)])
            tapd = C("sDVE")
            tapend_marks[ft] = tapd
            E("act", lambda e, ft=ft, ca=ca: e.activation(
                out=ufm16[:, ft * TW:(ft + 1) * TW], in_=ca, func=AF.Silu,
                bias=convb[:, ft:ft + 1]),
              waits=[("sDVE", tapd), ("sPE", ufm_cons_pe), ("qPL", ufm_cons_pl)],
              incs=[("sACT", 1)])
            silu_marks[ft] = C("sACT")
            tile_free[par] = C("sACT")
        for ft in range(NT):
            E("pe", lambda e, ft=ft: e.matmul(
                out=psC[0:48, 0:512],
                lhsT=wxdtT[:, ft * 48:(ft + 1) * 48],
                rhs=ufm16[:, ft * TW:(ft + 1) * TW],
                start=(ft == 0), stop=(ft == NT - 1)),
              waits=[("sACT", silu_marks[ft])] +
                    ([("sDVE", C("sDVE"))] if ft == 0 else []),
              incs=([("sPE", 1)] if ft == NT - 1 else []))
        bcmm = C("sPE")
        E("dve", lambda e: e.tensor_copy(out=bcev[:], in_=psC[0:48, 0:512]),
          waits=[("sPE", bcmm)], incs=[("sDVE", 1)])
        bev = C("sDVE")
        E("pool", lambda e, T=T: e.dma_start(out=bcH[:, T * TW:(T + 1) * TW], in_=bcev[:]),
          waits=[("sDVE", bev)], incs=[("qPL", 16)])
        bc_stored[T] = C("qPL")
        for ft in range(NT):
            E("pool", lambda e, T=T, ft=ft: e.dma_start(
                out=ufmH[T * 4:(T + 1) * 4, :, ft, :].rearrange("c p t -> p c t"),
                in_=ufm16[:, ft * TW:(ft + 1) * TW].rearrange("p (c t) -> p c t", c=4)),
              waits=[("sACT", silu_marks[ft])], incs=[("qPL", 16)])
        ufm_stored[T] = C("qPL")
        for tc in range(4):
            c = T * 4 + tc
            spar = tc % 2
            ub = utokb[:, spar * DI:(spar + 1) * DI]
            for g in range(4):
                for sl4 in range(4):
                    ft = g * 4 + sl4
                    tdst = psT[:, sl4 * 128:(sl4 + 1) * 128]
                    E("pe", lambda e, ft=ft, tc=tc, tdst=tdst: e.transpose(
                        out=tdst,
                        in_=ufm16[:, ft * TW + tc * 128:ft * TW + (tc + 1) * 128],
                        identity=identb[:]),
                      waits=[("sACT", silu_marks[ft]), ("sDVE", psT_evac[0])],
                      incs=[("sPE", 1)])
                td = C("sPE")
                for sl4 in range(4):
                    ft = g * 4 + sl4
                    tdst = psT[:, sl4 * 128:(sl4 + 1) * 128]
                    E("dve", lambda e, ft=ft, ub=ub, tdst=tdst: e.tensor_copy(
                        out=ub[:, ft * 128:(ft + 1) * 128], in_=tdst),
                      waits=[("sPE", td), ("qPL", utokb_free[spar])],
                      incs=[("sDVE", 1)])
                psT_evac[0] = C("sDVE")
            utd = C("sDVE")
            E("pool", lambda e, c=c, ub=ub: e.dma_start(out=utokH[c], in_=ub),
              waits=[("sDVE", utd)], incs=[("qPL", 16)])
            utok_stored[c] = C("qPL")
            utokb_free[spar] = C("qPL")
        ufm_cons_pe = C("sPE")
        ufm_cons_pl = C("qPL")

    if PHASES == "p2":
        return finalize()
    b.barrier()

    slot_cast = [C("sPOOL"), C("sPOOL")]
    idx = 0
    for kc in range(8):
        for hh in range(2):
            par = idx % 2
            sl = stage[:, par * 1024:(par + 1) * 1024]
            E("sy", lambda e, kc=kc, hh=hh, sl=sl: e.dma_start(
                out=sl, in_=winT_h[kc * 128:(kc + 1) * 128,
                                   DI + hh * 1024:DI + (hh + 1) * 1024]),
              waits=[("sPOOL", slot_cast[par])], incs=[("qSY", 16)])
            E("pool", lambda e, kc=kc, hh=hh, sl=sl: e.tensor_copy(
                out=WB[:, kc * DI + hh * 1024:kc * DI + (hh + 1) * 1024], in_=sl),
              waits=[("qSY", C("qSY"))], incs=[("sPOOL", 1)])
            slot_cast[par] = C("sPOOL")
            idx += 1
    winz_done = C("sPOOL")

    if PHASES == "pg":
        return finalize()
    b.barrier()

    def emit_zhalf(hg, szpar, after_psB_free_dve, szf_war_pool=0):
        prev_silu = [0, 0]
        for rnd in range(4):
            base = (rnd % 2) * 1024
            for fti in range(4):
                ft = rnd * 4 + fti
                for kc in range(8):
                    wts = []
                    if fti == 0 and kc == 0:
                        wts = [("sPOOL", winz_done), ("sDVE", after_psB_free_dve),
                               ("sACT", prev_silu[rnd % 2])]
                    E("pe", lambda e, hg=hg, ft=ft, fti=fti, kc=kc, base=base: e.matmul(
                        out=psB[:, base + fti * 256:base + (fti + 1) * 256],
                        lhsT=WB[:, kc * DI + ft * 128:kc * DI + (ft + 1) * 128],
                        rhs=xnT[:, kc * L + hg * 256:kc * L + (hg + 1) * 256],
                        start=(kc == 0), stop=(kc == 7)),
                      waits=wts, incs=([("sPE", 1)] if kc == 7 else []))
            zmm = C("sPE")
            for fti in range(4):
                ft = rnd * 4 + fti
                E("act", lambda e, ft=ft, fti=fti, szpar=szpar, base=base: e.activation(
                    out=szf[:, szpar * NT * 256 + ft * 256:
                            szpar * NT * 256 + (ft + 1) * 256],
                    in_=psB[:, base + fti * 256:base + (fti + 1) * 256], func=AF.Silu),
                  waits=[("sPE", zmm), ("sPOOL", szf_war_pool)], incs=[("sACT", 1)])
            prev_silu[rnd % 2] = C("sACT")
        return C("sACT")

    sz_ready = [0] * NHG
    sz_ready[0] = emit_zhalf(0, 0, 0)
    szf_cons = [0, 0]

    state = {
        "utok_cons": 0, "ufm_cons": 0, "bc_cons": [0, 0],
        "flt_cons": 0, "fltc_cons": 0, "yzb_stored": 0,
        "psB_evac_dve": 0, "psB_z_done": C("sACT"),
        "prev_h_done": 0, "prev_X1_done": 0, "psA_free": 0,
    }
    SM = {}

    def emit_stats(c):
        par = c % 2
        E("sy", lambda e, c=c, par=par: e.dma_start(
            out=bc4[:, par * Q:(par + 1) * Q], in_=bcH[:, c * Q:(c + 1) * Q]),
          waits=[("qPL", bc_stored[c // 4]), ("sPE", state["bc_cons"][par])],
          incs=[("qSY", 16)])
        bc_ld = C("qSY")
        prev_psc_free = SM[c - 1]["ctT"] if (c - 1) in SM else 0
        E("pe", lambda e, par=par: e.transpose(
            out=psC[:, 304:352], in_=bc4[:, par * Q:(par + 1) * Q],
            identity=identf[0:48, 0:48]),
          waits=[("qSY", bc_ld), ("sDVE", prev_psc_free)], incs=[("sPE", 1)])
        bct = C("sPE")
        state["bc_cons"][par] = bct
        E("dve", lambda e: e.tensor_add(out=sm_a[:], in0=psC[:, 304 + 32:304 + 48],
                                        in1=bdtB[:]),
          waits=[("sPE", bct), ("sDVE", bcasts_done)], incs=[("sDVE", 1)])
        E("dve", lambda e: e.tensor_copy(out=sm_b[:], in_=psC[:, 304:304 + 16]),
          waits=[("sDVE", C("sDVE"))], incs=[("sDVE", 1)])
        E("dve", lambda e: e.tensor_copy(out=sm_cbf[:], in_=psC[:, 304 + 16:304 + 32]),
          waits=[("sDVE", C("sDVE"))], incs=[("sDVE", 1)])
        smalls0 = C("sDVE")
        E("dve", lambda e: e.tensor_scalar_max(out=sm_dlt[:], in0=sm_a[:], scalar1=0.0),
          waits=[("sDVE", smalls0)], incs=[("sDVE", 1)])
        r0 = C("sDVE")
        E("act", lambda e: e.activation(out=sm_lnb[:], in_=sm_a[:], func=AF.Abs),
          waits=[("sDVE", smalls0)], incs=[("sACT", 1)])
        E("act", lambda e: e.activation(out=sm_lnb[:], in_=sm_lnb[:], func=AF.Exp,
                                        scale=-1.0),
          waits=[("sACT", C("sACT"))], incs=[("sACT", 1)])
        E("act", lambda e: e.activation(out=sm_lnb[:], in_=sm_lnb[:], func=AF.Ln,
                                        bias=1.0),
          waits=[("sACT", C("sACT"))], incs=[("sACT", 1)])
        spl = C("sACT")
        E("dve", lambda e: e.tensor_add(out=sm_dlt[:], in0=sm_dlt[:], in1=sm_lnb[:]),
          waits=[("sACT", spl), ("sDVE", r0)], incs=[("sDVE", 1)])
        dlt_done = C("sDVE")
        E("dve", lambda e: e.tensor_mul(out=sm_a[:], in0=sm_dlt[:], in1=aB[:]),
          waits=[("sDVE", dlt_done)], incs=[("sDVE", 1)])
        dal_done = C("sDVE")
        E("dve", lambda e: e.tensor_mul(out=sm_dB[:], in0=sm_dlt[:], in1=sm_b[:]),
          waits=[("sDVE", dlt_done)], incs=[("sDVE", 1)])
        db_done = C("sDVE")
        E("act", lambda e: e.activation(out=sm_lnb[:], in_=sm_dB[:], func=AF.Abs),
          waits=[("sDVE", db_done)], incs=[("sACT", 1)])
        E("act", lambda e: e.activation(out=sm_lnb[:], in_=sm_lnb[:], func=AF.Ln),
          waits=[("sACT", C("sACT"))], incs=[("sACT", 1)])
        lnb_done = C("sACT")
        E("act", lambda e, par=par: e.activation(
            out=sm_sgn[:, par * NS:(par + 1) * NS], in_=sm_dB[:], func=AF.Sign),
          waits=[("sDVE", db_done)], incs=[("sACT", 1)])
        sgn_done = C("sACT")
        E("pe", lambda e: e.matmul(out=psC[:, 0:16], lhsT=triu[:], rhs=sm_a[:],
                                   start=True, stop=True),
          waits=[("sDVE", dal_done)], incs=[("sPE", 1)])
        E("pe", lambda e: e.matmul(out=psC[0:16, 16:144], lhsT=sm_a[:], rhs=triu[:],
                                   start=True, stop=True),
          waits=[("sPE", C("sPE"))], incs=[("sPE", 1)])
        E("pe", lambda e: e.matmul(out=psC[0:16, 144:145], lhsT=sm_a[:], rhs=onescol[:],
                                   start=True, stop=True),
          waits=[("sPE", C("sPE"))], incs=[("sPE", 1)])
        E("pe", lambda e: e.matmul(out=psC[0:1, 145:161], lhsT=onescol[:], rhs=sm_a[:],
                                   start=True, stop=True),
          waits=[("sPE", C("sPE"))], incs=[("sPE", 1)])
        cum_mm = C("sPE")
        E("dve", lambda e: e.tensor_copy(out=sm_cum[:], in_=psC[:, 0:16]),
          waits=[("sPE", cum_mm)], incs=[("sDVE", 1)])
        E("dve", lambda e: e.tensor_copy(out=cumT_e[:], in_=psC[0:16, 16:144]),
          waits=[("sDVE", C("sDVE"))], incs=[("sDVE", 1)])
        E("dve", lambda e: e.tensor_copy(out=clF[:], in_=psC[0:1, 145:161]),
          waits=[("sDVE", C("sDVE"))], incs=[("sDVE", 1)])
        cum_ev = C("sDVE")
        E("act", lambda e, par=par: e.activation(
            out=dec_c[:, par:par + 1], in_=psC[0:16, 144:145], func=AF.Exp),
          waits=[("sPE", cum_mm)], incs=[("sACT", 1)])
        dec_done = C("sACT")
        E("pe", lambda e: e.matmul(out=psC[:, 161:177], lhsT=ones1f[:], rhs=clF[:],
                                   start=True, stop=True),
          waits=[("sDVE", cum_ev), ("sACT", dec_done)], incs=[("sPE", 1)])
        bclmm = C("sPE")
        E("dve", lambda e: e.tensor_sub(out=sm_b[:], in0=psC[:, 161:177], in1=sm_cum[:]),
          waits=[("sPE", bclmm), ("sDVE", cum_ev)], incs=[("sDVE", 1)])
        bed = C("sDVE")
        E("act", lambda e: e.activation(out=sm_b[:], in_=sm_b[:], func=AF.Exp),
          waits=[("sDVE", bed)], incs=[("sACT", 1)])
        be2 = C("sACT")
        E("dve", lambda e, par=par: e.tensor_mul(
            out=btok[:, par * NS:(par + 1) * NS], in0=sm_dB[:], in1=sm_b[:]),
          waits=[("sACT", be2), ("sDVE", db_done)], incs=[("sDVE", 1)])
        btok_done = C("sDVE")
        E("act", lambda e: e.activation(out=sm_a[:], in_=sm_cum[:], func=AF.Exp),
          waits=[("sDVE", cum_ev)], incs=[("sACT", 1)])
        ecd = C("sACT")
        E("dve", lambda e: e.tensor_mul(out=sm_ctl[:], in0=sm_cbf[:], in1=sm_a[:]),
          waits=[("sACT", ecd)], incs=[("sDVE", 1)])
        ctl = C("sDVE")
        E("pe", lambda e: e.transpose(out=psC[0:16, 177:305], in_=sm_ctl[:],
                                      identity=identf[:]),
          waits=[("sDVE", ctl), ("sDVE", bed), ("sACT", dec_done)],
          incs=[("sPE", 1)])
        ctTm = C("sPE")
        E("dve", lambda e, par=par: e.tensor_copy(
            out=ctT[:, par * Q:(par + 1) * Q], in_=psC[0:16, 177:305]),
          waits=[("sPE", ctTm)], incs=[("sDVE", 1)])
        ctT_done = C("sDVE")
        E("pe", lambda e: e.transpose(out=psT[0:16, 0:128], in_=sm_cbf[:],
                                      identity=identb[:]),
          waits=[("sDVE", smalls0), ("sDVE", psT_evac[0])], incs=[("sPE", 1)])
        cTm = C("sPE")
        E("dve", lambda e: e.tensor_copy(out=cb16[:], in_=psT[0:16, 0:128]),
          waits=[("sPE", cTm)], incs=[("sDVE", 1)])
        psT_evac[0] = C("sDVE")
        cb_done = C("sDVE")
        E("sy", lambda e: e.dma_start(out=fcumH, in_=cumT_e[:]),
          waits=[("sDVE", cum_ev), ("sPE", state["flt_cons"])], incs=[("qSY", 16)])
        E("sy", lambda e: e.dma_start(
            out=flt[:], in_=fcumH.rearrange("n t -> (n t)")[None, :]),
          waits=[("qSY", C("qSY"))], incs=[("qSY", 16)])
        flt_ld = C("qSY")
        E("sy", lambda e: e.dma_start(out=fcH, in_=cb16[:]),
          waits=[("sDVE", cb_done), ("sPE", state["fltc_cons"])], incs=[("qSY", 16)])
        E("sy", lambda e: e.dma_start(
            out=fltc[:], in_=fcH.rearrange("n t -> (n t)")[None, :]),
          waits=[("qSY", C("qSY"))], incs=[("qSY", 16)])
        fltc_ld = C("qSY")
        E("dve", lambda e, par=par: e.tensor_sub(
            out=sm_mbias[:, par * NS:(par + 1) * NS], in0=sm_lnb[:], in1=sm_cum[:]),
          waits=[("sACT", lnb_done), ("sDVE", cum_ev)], incs=[("sDVE", 1)])
        mbias_done = C("sDVE")
        SM[c] = dict(flt_ld=flt_ld, fltc_ld=fltc_ld, mbias=mbias_done,
                     sgn=sgn_done, ctT=ctT_done, btok=btok_done, dec=dec_done)

    emit_stats(0)
    for c in range(NCHUNK):
        hg = c // 2
        szpar = hg % 2
        par = c % 2
        st_m = SM[c]
        E("sy", lambda e, c=c: e.dma_start(out=ut4[:], in_=utokH[c]),
          waits=[("qPL", utok_stored[c]), ("sPE", state["utok_cons"])],
          incs=[("qSY", 16)])
        ut_ld = C("qSY")
        E("sy", lambda e, c=c: e.dma_start(
            out=uf4[:], in_=ufmH[c].rearrange("p et t -> p (et t)")),
          waits=[("qPL", ufm_stored[c // 4]), ("sDVE", state["ufm_cons"])],
          incs=[("qSY", 16)])
        uf_ld = C("qSY")

        for et in range(NT):
            wts = [("sDVE", st_m["ctT"]), ("sDVE", state["prev_h_done"])]
            if et == 0:
                wts.append(("sACT", state["psB_z_done"]))
                wts.append(("sDVE", state["psB_evac_dve"]))
            E("pe", lambda e, et=et, par=par: e.matmul(
                out=psB[:, et * 128:(et + 1) * 128],
                lhsT=hst[:, et * 128:(et + 1) * 128],
                rhs=ctT[:, par * Q:(par + 1) * Q],
                start=(et % 4 == 0), stop=False),
              waits=wts, incs=[("sPE", 1)])
        y2done = C("sPE")

        for hh in range(2):
            for jj in range(2):
                wts = [("qSY", st_m["flt_ld"])]
                if hh == 0 and jj == 0:
                    wts.append(("sDVE", state["psA_free"]))
                if hh == 1 and jj == 0:
                    wts.append(("sDVE", C("sDVE")))
                E("pe", lambda e, hh=hh, jj=jj: e.matmul(
                    out=psA[:, jj * 512:(jj + 1) * 512], lhsT=ones1f[:],
                    rhs=flt[0:1, hh * 1024 + jj * 512:hh * 1024 + (jj + 1) * 512],
                    start=True, stop=True),
                  waits=wts, incs=[("sPE", 1)])
            bctm = C("sPE")
            state["flt_cons"] = bctm
            E("dve", lambda e, hh=hh: e.tensor_add(
                out=mb1[:, hh * 1024:(hh + 1) * 1024].rearrange("p (n t) -> p n t", n=8),
                in0=psA[:, 0:1024].rearrange("p (n t) -> p n t", n=8),
                in1=mneg[:, None, :].broadcast_to([128, 8, 128])),
              waits=[("sPE", bctm)], incs=[("sDVE", 1)])
            tmpd = C("sDVE")
            for ni in range(8):
                n = hh * 8 + ni
                E("act", lambda e, n=n, par=par: e.activation(
                    out=mb2[:, n * 128:(n + 1) * 128],
                    in_=mb1[:, n * 128:(n + 1) * 128], func=AF.Exp,
                    bias=sm_mbias[:, par * NS + n:par * NS + n + 1]),
                  waits=[("sDVE", tmpd), ("sDVE", st_m["mbias"])], incs=[("sACT", 1)])
            expa = C("sACT")
            E("dve", lambda e, hh=hh, par=par: e.tensor_mul(
                out=mb1[:, hh * 1024:(hh + 1) * 1024].rearrange(
                    "p (n t) -> p n t", n=8),
                in0=mb2[:, hh * 1024:(hh + 1) * 1024].rearrange(
                    "p (n t) -> p n t", n=8),
                in1=sm_sgn[:, par * NS + hh * 8:par * NS + (hh + 1) * 8][:, :, None
                    ].broadcast_to([128, 8, 128])),
              waits=[("sACT", expa), ("sACT", st_m["sgn"])], incs=[("sDVE", 1)])
            expd = ("sDVE", C("sDVE"))
            for jj in range(2):
                E("pe", lambda e, hh=hh, jj=jj: e.matmul(
                    out=psA[:, jj * 512:(jj + 1) * 512], lhsT=ones1b[:],
                    rhs=fltc[0:1, hh * 1024 + jj * 512:hh * 1024 + (jj + 1) * 512],
                    start=True, stop=True),
                  waits=([("qSY", st_m["fltc_ld"]), ("sDVE", tmpd)]
                         if jj == 0 else []),
                  incs=[("sPE", 1)])
            bccm = C("sPE")
            state["fltc_cons"] = bccm
            E("dve", lambda e, hh=hh: e.tensor_mul(
                out=mb2[:, hh * 1024:(hh + 1) * 1024],
                in0=mb1[:, hh * 1024:(hh + 1) * 1024], in1=psA[:, 0:1024]),
              waits=[("sPE", bccm), expd], incs=[("sDVE", 1)])
        X1_done = C("sDVE")
        state["prev_X1_done"] = X1_done
        E("dve", lambda e: e.tensor_reduce(
            out=msb[:], in_=mb2[:].rearrange("p (n t) -> p t n", n=16),
            axis=AX.X, op=ALU.add),
          waits=[("sDVE", X1_done)], incs=[("sDVE", 1)])
        E("dve", lambda e: e.tensor_copy(out=mbf[:], in_=msb[:]),
          waits=[("sDVE", C("sDVE"))], incs=[("sDVE", 1)])
        mbf_done = C("sDVE")

        if c + 1 < NCHUNK:
            emit_stats(c + 1)

        for et in range(NT):
            E("pe", lambda e, et=et: e.matmul(
                out=psB[:, et * 128:(et + 1) * 128],
                lhsT=ut4[:, et * 128:(et + 1) * 128], rhs=mbf[:],
                start=False, stop=True),
              waits=[("sDVE", mbf_done), ("qSY", ut_ld), ("sPE", y2done)],
              incs=[("sPE", 1)])
        y1done = C("sPE")

        hmarks = []
        for rr in range(4):
            base = (rr % 2) * 512
            wts = [("qSY", ut_ld), ("sDVE", st_m["btok"]), ("sDVE", X1_done)]
            if rr >= 2:
                wts.append(("sDVE", hmarks[rr - 2]))
            E("pe", lambda e, rr=rr, base=base, par=par: e.matmul(
                out=psA[0:16, base:base + 512], lhsT=btok[:, par * NS:(par + 1) * NS],
                rhs=ut4[:, rr * 512:(rr + 1) * 512], start=True, stop=True),
              waits=wts, incs=[("sPE", 1)])
            dhm = C("sPE")
            E("dve", lambda e, rr=rr, base=base, par=par: e.scalar_tensor_tensor(
                out=hst[:, rr * 512:(rr + 1) * 512],
                in0=hst[:, rr * 512:(rr + 1) * 512], scalar=dec_c[:, par:par + 1],
                in1=psA[0:16, base:base + 512], op0=ALU.mult, op1=ALU.add),
              waits=[("sPE", dhm), ("sACT", st_m["dec"]), ("sPE", y2done)],
              incs=[("sDVE", 1)])
            hmarks.append(C("sDVE"))
        h_done = C("sDVE")
        state["prev_h_done"] = h_done
        state["psA_free"] = h_done
        utok_cons_mark = C("sPE")
        state["utok_cons"] = utok_cons_mark

        for et in range(NT):
            E("dve", lambda e, et=et: e.scalar_tensor_tensor(
                out=mb1[:, et * 128:(et + 1) * 128],
                in0=uf4[:, et * 128:(et + 1) * 128], scalar=dfm[:, et:et + 1],
                in1=psB[:, et * 128:(et + 1) * 128], op0=ALU.mult, op1=ALU.add),
              waits=[("sPE", y1done), ("qSY", uf_ld), expd],
              incs=[("sDVE", 1)])
            dd = C("sDVE")
            E("pool", lambda e, et=et, c=c, szpar=szpar: e.tensor_mul(
                out=yzb[:, et * 128:(et + 1) * 128],
                in0=mb1[:, et * 128:(et + 1) * 128],
                in1=szf[:, szpar * NT * 256 + et * 256 + (c % 2) * 128:
                        szpar * NT * 256 + et * 256 + (c % 2) * 128 + 128]),
              waits=[("sDVE", dd), ("sACT", sz_ready[hg]),
                     ("qSY", state["yzb_stored"])],
              incs=[("sPOOL", 1)])
        psB_evac = C("sDVE")
        state["psB_evac_dve"] = psB_evac
        yz_done = C("sPOOL")
        szf_cons[szpar] = yz_done
        state["ufm_cons"] = psB_evac
        E("sy", lambda e, c=c: e.dma_start(out=yzH[c], in_=yzb[:]),
          waits=[("sPOOL", yz_done)], incs=[("qSY", 16)])
        state["yzb_stored"] = C("qSY")

        if c % 2 == 1 and hg + 1 < NHG:
            nxt = 1 - szpar
            sz_ready[hg + 1] = emit_zhalf(hg + 1, nxt, psB_evac, szf_cons[nxt])
            state["psB_z_done"] = sz_ready[hg + 1]
        else:
            state["psB_z_done"] = C("sACT")

    if PHASES == "p4":
        return finalize()
    b.barrier()

    CAP = 1024
    xsb = at("xsb", 10240, [128, 2 * DM], BF16)
    xsTb = at("xsTb", 14336, [128, 2 * DM], BF16)
    xnb5 = at("xnb5", 18432, [128, 2 * DM], BF16)
    cnt8s = at("cnt8s", 22848, [8, NCHUNK], F32)
    z832 = at("z832", 23040, [8, NCHUNK], F32)
    bceF = at("bceF", 23296, [1, 8 * NCHUNK], F32)
    bce32 = at("bce32", 24256, [NCHUNK, 8], F32)
    part1 = at("part1", 24384, [128, NCHUNK], F32)
    ohall = at("ohall", 24512, [128, NCHUNK * 8], F32)

    psc_read = 0
    for cb in range(NCHUNK // 4):
        for ci in range(4):
            c = cb * 4 + ci
            for kc in range(8):
                wts = [("sDVE", psc_read)] if (ci == 0 and kc == 0) else []
                E("pe", lambda e, c=c, ci=ci, kc=kc: e.matmul(
                    out=psC[:, ci * 8:(ci + 1) * 8],
                    lhsT=xnT[:, kc * L + c * 128:kc * L + (c + 1) * 128],
                    rhs=gwT[:, kc * 8:(kc + 1) * 8],
                    start=(kc == 0), stop=(kc == 7)),
                  waits=wts, incs=([("sPE", 1)] if kc == 7 else []))
        lg = C("sPE")
        E("dve", lambda e: e.tensor_reduce(
            out=rmax4[:], in_=psC[:, 0:32].rearrange("p (c e) -> p c e", c=4),
            axis=AX.X, op=ALU.max),
          waits=[("sPE", lg)], incs=[("sDVE", 1)])
        rm = C("sDVE")
        E("dve", lambda e: e.tensor_tensor(
            out=ohb[:].rearrange("p (c e) -> p c e", c=4),
            in0=psC[:, 0:32].rearrange("p (c e) -> p c e", c=4),
            in1=rmax4[:, :, None].broadcast_to([128, 4, 8]), op=ALU.is_ge),
          waits=[("sDVE", rm)], incs=[("sDVE", 1)])
        E("dve", lambda e: e.tensor_mul(out=ohb[:], in0=ohb[:], in1=wdec[:]),
          waits=[("sDVE", C("sDVE"))], incs=[("sDVE", 1)])
        E("dve", lambda e: e.tensor_reduce(
            out=rmax4[:], in_=ohb[:].rearrange("p (c e) -> p c e", c=4),
            axis=AX.X, op=ALU.max),
          waits=[("sDVE", C("sDVE"))], incs=[("sDVE", 1)])
        E("dve", lambda e, cb=cb: e.tensor_tensor(
            out=ohall[:, cb * 32:(cb + 1) * 32].rearrange("p (c e) -> p c e", c=4),
            in0=ohb[:].rearrange("p (c e) -> p c e", c=4),
            in1=rmax4[:, :, None].broadcast_to([128, 4, 8]), op=ALU.is_ge),
          waits=[("sDVE", C("sDVE"))], incs=[("sDVE", 1)])
        ohf_done = C("sDVE")
        E("dve", lambda e: e.tensor_reduce(
            out=rmax4[:], in_=psC[:, 0:32].rearrange("p (c e) -> p c e", c=4),
            axis=AX.X, op=ALU.max),
          waits=[("sDVE", ohf_done)], incs=[("sDVE", 1)])
        rm2 = C("sDVE")
        E("act", lambda e: e.activation(out=rmax4[:], in_=rmax4[:], func=AF.Tanh,
                                        scale=0.5),
          waits=[("sDVE", rm2)], incs=[("sACT", 1)])
        tw = C("sACT")
        E("dve", lambda e, cb=cb: e.tensor_scalar(
            out=rwall[:, cb * 4:(cb + 1) * 4], in0=rmax4[:], scalar1=0.5,
            scalar2=0.5, op0=ALU.mult, op1=ALU.add),
          waits=[("sACT", tw)], incs=[("sDVE", 1)])
        for ci in range(4):
            c = cb * 4 + ci
            E("pe", lambda e, c=c, ci=ci: e.matmul(
                out=psC[:, 32 + ci * 8:32 + (ci + 1) * 8], lhsT=tris[:],
                rhs=ohall[:, c * 8:(c + 1) * 8], start=True, stop=True),
              waits=[("sDVE", ohf_done)], incs=[("sPE", 1)])
            E("pe", lambda e, c=c: e.matmul(
                out=psC[0:8, 64 + c:65 + c], lhsT=ohall[:, c * 8:(c + 1) * 8],
                rhs=onescol[:], start=True, stop=True),
              waits=[("sPE", C("sPE"))], incs=[("sPE", 1)])
        rkm = C("sPE")
        E("dve", lambda e: e.tensor_tensor(
            out=ohb[:].rearrange("p (c e) -> p c e", c=4),
            in0=psC[:, 32:64].rearrange("p (c e) -> p c e", c=4),
            in1=ecapB[:, None, :].broadcast_to([128, 4, 8]), op=ALU.add),
          waits=[("sPE", rkm)], incs=[("sDVE", 1)])
        E("dve", lambda e: e.tensor_tensor(
            out=ohb[:].rearrange("p (c e) -> p c e", c=4),
            in0=ohb[:].rearrange("p (c e) -> p c e", c=4),
            in1=limB[:, None, :].broadcast_to([128, 4, 8]), op=ALU.min),
          waits=[("sDVE", C("sDVE"))], incs=[("sDVE", 1)])
        E("dve", lambda e, cb=cb: e.tensor_mul(
            out=ohb[:], in0=ohb[:], in1=ohall[:, cb * 32:(cb + 1) * 32]),
          waits=[("sDVE", C("sDVE"))], incs=[("sDVE", 1)])
        E("dve", lambda e, cb=cb: e.tensor_reduce(
            out=part1[:, cb * 4:(cb + 1) * 4],
            in_=ohb[:].rearrange("p (c e) -> p c e", c=4), axis=AX.X, op=ALU.add),
          waits=[("sDVE", C("sDVE"))], incs=[("sDVE", 1)])
        psc_read = C("sDVE")

    E("dve", lambda e: e.tensor_copy(out=cnt8s[:], in_=psC[0:8, 64:64 + NCHUNK]),
      waits=[("sDVE", psc_read), ("sPE", C("sPE"))], incs=[("sDVE", 1)])
    E("dve", lambda e: e.memset(z832[:], 0.0),
      waits=[("sDVE", C("sDVE"))], incs=[("sDVE", 1)])
    E("dve", lambda e: e.tensor_tensor_scan(
        out=z832[:], data0=cnt8s[:], data1=z832[:], initial=0.0,
        op0=ALU.add, op1=ALU.add),
      waits=[("sDVE", C("sDVE"))], incs=[("sDVE", 1)])
    E("dve", lambda e: e.tensor_sub(out=cnt8s[:], in0=z832[:], in1=cnt8s[:]),
      waits=[("sDVE", C("sDVE"))], incs=[("sDVE", 1)])
    bce_done = C("sDVE")
    E("pe", lambda e: e.transpose(out=psC[0:NCHUNK, 104:112], in_=cnt8s[:],
                                  identity=identf[0:8, 0:8]),
      waits=[("sDVE", bce_done)], incs=[("sPE", 1)])
    bcet = C("sPE")
    E("dve", lambda e: e.tensor_copy(out=bce32[:], in_=psC[0:NCHUNK, 104:112]),
      waits=[("sPE", bcet)], incs=[("sDVE", 1)])
    bce32d = C("sDVE")
    E("pool", lambda e: e.dma_start(out=bceH, in_=bce32[:]),
      waits=[("sDVE", bce32d)], incs=[("qPL", 16)])
    E("pool", lambda e: e.dma_start(
        out=bceF[:], in_=bceH.rearrange("c e -> (c e)")[None, :]),
      waits=[("qPL", C("qPL"))], incs=[("qPL", 16)])
    bceF_ld = C("qPL")
    slot_marks = []
    for c in range(NCHUNK):
        E("pe", lambda e, c=c: e.matmul(
            out=psC[:, 96:104], lhsT=ones1f[:], rhs=bceF[0:1, c * 8:(c + 1) * 8],
            start=True, stop=True),
          waits=[("qPL", bceF_ld), ("sDVE", C("sDVE"))], incs=[("sPE", 1)])
        bcm = C("sPE")
        E("dve", lambda e, c=c: e.tensor_mul(
            out=oh8[:], in0=psC[:, 96:104], in1=ohall[:, c * 8:(c + 1) * 8]),
          waits=[("sPE", bcm)], incs=[("sDVE", 1)])
        E("dve", lambda e: e.tensor_reduce(out=rmaxb[:], in_=oh8[:], axis=AX.X,
                                           op=ALU.add),
          waits=[("sDVE", C("sDVE"))], incs=[("sDVE", 1)])
        E("dve", lambda e, c=c: e.tensor_add(out=rmaxb[:], in0=rmaxb[:],
                                             in1=part1[:, c:c + 1]),
          waits=[("sDVE", C("sDVE"))], incs=[("sDVE", 1)])
        E("dve", lambda e, c=c: e.tensor_copy(out=slotI[:, c:c + 1], in_=rmaxb[:]),
          waits=[("sDVE", C("sDVE"))], incs=[("sDVE", 1)])
        slot_marks.append(C("sDVE"))
    slots_done = C("sDVE")

    xs_scattered = 0
    xnb5_war = [0, 0]
    xsb_war = [0, 0]
    for c in range(NCHUNK):
        par = c % 2
        for g2 in range(2):
            for s4 in range(4):
                kc = g2 * 4 + s4
                E("pe", lambda e, kc=kc, c=c, s4=s4: e.transpose(
                    out=psT[:, s4 * 128:(s4 + 1) * 128],
                    in_=xnT[:, kc * L + c * 128:kc * L + (c + 1) * 128],
                    identity=identb[:]),
                  waits=[("sDVE", psT_evac[0]), ("sDVE", xnb5_war[par])],
                  incs=[("sPE", 1)])
            td = C("sPE")
            for s4 in range(4):
                kc = g2 * 4 + s4
                E("dve", lambda e, kc=kc, s4=s4, par=par: e.tensor_copy(
                    out=xnb5[:, par * DM + kc * 128:par * DM + (kc + 1) * 128],
                    in_=psT[:, s4 * 128:(s4 + 1) * 128]),
                  waits=[("sPE", td)], incs=[("sDVE", 1)])
            psT_evac[0] = C("sDVE")
        xn_tr = C("sDVE")
        E("dve", lambda e, c=c, par=par: e.tensor_scalar_mul(
            out=xsb[:, par * DM:(par + 1) * DM], in0=xnb5[:, par * DM:(par + 1) * DM],
            scalar1=rwall[:, c:c + 1]),
          waits=[("sDVE", xn_tr), ("qPL", xsb_war[par]), ("sDVE", slots_done)],
          incs=[("sDVE", 1)])
        xnb5_war[par] = C("sDVE")
        xsd = C("sDVE")
        if SKIP_SCATTER:
            E("pool", lambda e, c=c: e.dma_start(
                out=xsH[c * 128:(c + 1) * 128, :],
                in_=xsb[:, par * DM:(par + 1) * DM]),
              waits=[("sDVE", xsd)], incs=[("qPL", 16)])
        else:
            E("pool", lambda e, c=c, par=par: e.indirect_dma_start(
                out=xsH, out_offset=bass.IndirectOffsetOnAxis(slotI[:, c:c + 1], 0),
                in_=xsb[:, par * DM:(par + 1) * DM], in_offset=None),
              waits=[("sDVE", xsd)], incs=[("qPL", 16)])
        xsb_war[par] = C("qPL")
        xs_scattered = C("qPL")

    def make_expert_loader(e_idx, slot, war_pe):
        chunks = []
        for (wsrc, dst_base, nkc, wid, gi) in [
            (wgT_h, 0, 8, 512, 0), (wuT_h, 4096, 8, 512, 1),
            (wdT_h, 8192, 4, 1024, 2)]:
            for kc in range(nkc):
                chunks.append((wsrc, dst_base, kc, wid, gi, kc == nkc - 1))
        st = {"i": 0, "slot_cast": [C("sPOOL"), C("sPOOL")],
              "marks": [None, None, None]}

        def step(n):
            for _ in range(n):
                if st["i"] >= len(chunks):
                    return
                wsrc, dst_base, kc, wid, gi, last = chunks[st["i"]]
                par2 = st["i"] % 2
                sl = stage[:, par2 * 1024:par2 * 1024 + wid]
                E("sy", lambda e, kc=kc, sl=sl, wsrc=wsrc: e.dma_start(
                    out=sl, in_=wsrc[e_idx, kc * 128:(kc + 1) * 128, :]),
                  waits=[("sPOOL", st["slot_cast"][par2])], incs=[("qSY", 16)])
                if dst_base < 8192:
                    E("pool", lambda e, kc=kc, sl=sl, dst_base=dst_base,
                      wid=wid: e.tensor_copy(
                        out=WA[:, slot * 8192 + dst_base + kc * wid:
                                slot * 8192 + dst_base + (kc + 1) * wid], in_=sl),
                      waits=[("qSY", C("qSY")), ("sPE", war_pe)], incs=[("sPOOL", 1)])
                else:
                    E("pool", lambda e, kc=kc, sl=sl, wid=wid: e.tensor_copy(
                        out=WB[:, slot * 4096 + kc * wid:slot * 4096 + (kc + 1) * wid],
                        in_=sl),
                      waits=[("qSY", C("qSY")), ("sPE", war_pe)], incs=[("sPOOL", 1)])
                st["slot_cast"][par2] = C("sPOOL")
                if last:
                    st["marks"][gi] = C("sPOOL")
                st["i"] += 1
        return st, step

    ew = {}
    st0, step0 = make_expert_loader(0, 0, 0)
    step0(20)
    ew[0] = st0["marks"]
    arena_pe = [0, 0]
    gs_stored = 0
    xsb_war5 = [0, 0]
    down_evac = 0
    for e_i in range(8):
        slot = e_i % 2
        wm = ew[e_i]
        assert all(m is not None for m in wm)
        nxt_step = None
        if e_i + 1 < 8:
            nst, nxt_step = make_expert_loader(e_i + 1, 1 - slot,
                                               arena_pe[1 - slot])
            ew[e_i + 1] = nst["marks"]
        for r in range(8):
            if nxt_step is not None:
                nxt_step(3)
            par = r % 2
            E("sy", lambda e, e_i=e_i, r=r, par=par: e.dma_start(
                out=xsb[:, par * DM:(par + 1) * DM],
                in_=xsH[e_i * CAP + r * 128:e_i * CAP + (r + 1) * 128, :]),
              waits=[("qPL", xs_scattered), ("sPE", xsb_war5[par])],
              incs=[("qSY", 16)])
            xs_ld = C("qSY")
            for g in range(2):
                for s4 in range(4):
                    fc = g * 4 + s4
                    E("pe", lambda e, fc=fc, s4=s4, par=par: e.transpose(
                        out=psT[:, s4 * 128:(s4 + 1) * 128],
                        in_=xsb[:, par * DM + fc * 128:par * DM + (fc + 1) * 128],
                        identity=identb[:]),
                      waits=[("qSY", xs_ld), ("sDVE", psT_evac[0])],
                      incs=[("sPE", 1)])
                td = C("sPE")
                for s4 in range(4):
                    fc = g * 4 + s4
                    E("dve", lambda e, fc=fc, s4=s4, par=par: e.tensor_copy(
                        out=xsTb[:, par * DM + fc * 128:par * DM + (fc + 1) * 128],
                        in_=psT[:, s4 * 128:(s4 + 1) * 128]),
                      waits=[("sPE", td)], incs=[("sDVE", 1)])
                psT_evac[0] = C("sDVE")
            xsT_done = C("sDVE")
            for kc in range(8):
                wts = [("sPOOL", wm[0]), ("sDVE", xsT_done)]
                if kc == 0:
                    wts.append(("sACT", C("sACT")))
                E("pe", lambda e, kc=kc, slot=slot, par=par: e.matmul(
                    out=psB[:, 1024:1536],
                    lhsT=xsTb[:, par * DM + kc * 128:par * DM + (kc + 1) * 128],
                    rhs=WA[:, slot * 8192 + kc * 512:slot * 8192 + (kc + 1) * 512],
                    start=(kc == 0), stop=(kc == 7)),
                  waits=wts, incs=([("sPE", 1)] if kc == 7 else []))
            gmm = C("sPE")
            for kc in range(8):
                wts = [("sPOOL", wm[1]), ("sDVE", xsT_done)]
                if kc == 0:
                    wts.append(("sDVE", C("sDVE")))
                E("pe", lambda e, kc=kc, slot=slot, par=par: e.matmul(
                    out=psB[:, 1536:2048],
                    lhsT=xsTb[:, par * DM + kc * 128:par * DM + (kc + 1) * 128],
                    rhs=WA[:, slot * 8192 + 4096 + kc * 512:
                            slot * 8192 + 4096 + (kc + 1) * 512],
                    start=(kc == 0), stop=(kc == 7)),
                  waits=wts, incs=([("sPE", 1)] if kc == 7 else []))
            umm = C("sPE")
            xsb_war5[par] = umm
            E("act", lambda e: e.activation(out=sgb[:], in_=psB[:, 1024:1536],
                                            func=AF.Silu),
              waits=[("sPE", gmm)], incs=[("sACT", 1)])
            sgd = C("sACT")
            E("act", lambda e: e.activation(out=upb[:], in_=psB[:, 1536:2048],
                                            func=AF.Copy),
              waits=[("sPE", umm)], incs=[("sACT", 1)])
            upd = C("sACT")
            E("dve", lambda e: e.tensor_mul(out=pbf_[:], in0=sgb[:], in1=upb[:]),
              waits=[("sACT", sgd), ("sACT", upd), ("sPE", C("sPE"))],
              incs=[("sDVE", 1)])
            pd = C("sDVE")
            for tq in range(4):
                E("pe", lambda e, tq=tq: e.transpose(
                    out=psT[:, tq * 128:(tq + 1) * 128],
                    in_=pbf_[:, tq * 128:(tq + 1) * 128], identity=identb[:]),
                  waits=[("sDVE", pd), ("sDVE", psT_evac[0])], incs=[("sPE", 1)])
            td = C("sPE")
            for tq in range(4):
                E("dve", lambda e, tq=tq: e.tensor_copy(
                    out=pTb[:, tq * 128:(tq + 1) * 128],
                    in_=psT[:, tq * 128:(tq + 1) * 128]),
                  waits=[("sPE", td)], incs=[("sDVE", 1)])
            psT_evac[0] = C("sDVE")
            ptd = C("sDVE")
            for kd in range(4):
                for jj in range(2):
                    wts = [("sDVE", ptd), ("sPOOL", wm[2])]
                    if kd == 0 and jj == 0:
                        wts.append(("sACT", down_evac))
                    E("pe", lambda e, kd=kd, jj=jj, slot=slot: e.matmul(
                        out=psB[:, jj * 512:(jj + 1) * 512],
                        lhsT=pTb[:, kd * 128:(kd + 1) * 128],
                        rhs=WB[:, slot * 4096 + kd * 1024 + jj * 512:
                                slot * 4096 + kd * 1024 + (jj + 1) * 512],
                        start=(kd == 0), stop=(kd == 3)),
                      waits=wts, incs=([("sPE", 1)] if (kd == 3 and jj == 1) else []))
            dmm = C("sPE")
            arena_pe[slot] = dmm
            E("act", lambda e: e.activation(out=ffnacc[:].bitcast(BF16)[:, 0:DM],
                                            in_=psB[:, 0:1024], func=AF.Silu),
              waits=[("sPE", dmm), ("qSC", gs_stored)], incs=[("sACT", 1)])
            gsa = C("sACT")
            down_evac = gsa
            E("act", lambda e, e_i=e_i, r=r: e.dma_start(
                out=gsH[e_i * CAP + r * 128:e_i * CAP + (r + 1) * 128, :],
                in_=ffnacc[:].bitcast(BF16)[:, 0:DM]),
              waits=[("sACT", gsa)], incs=[("qSC", 16)])
            gs_stored = C("qSC")
    gs_all = gs_stored
    if PHASES == "pb" and DBG_SCRATCH:
        E("dve", lambda e: e.tensor_copy(out=sgb[:, 0:32], in_=slotI[:]),
          waits=[("sDVE", C("sDVE"))], incs=[("sDVE", 1)])
        sf = C("sDVE")
        E("sy", lambda e: e.dma_start(out=out_h[0:128, 0:32], in_=part1[:]),
          waits=[("sDVE", sf)], incs=[("qSC", 16)])
        E("sy", lambda e: e.dma_start(out=out_h[0:128, 32:288], in_=ohall[:]),
          waits=[("sDVE", sf)], incs=[("qSC", 16)])
        E("sy", lambda e: e.dma_start(out=out_h[0:128, 288:320], in_=rwall[:, 0:32]),
          waits=[("sDVE", sf)], incs=[("qSC", 16)])
        E("sy", lambda e: e.dma_start(out=out_h[0:128, 320:352], in_=sgb[:, 0:32]),
          waits=[("sDVE", sf)], incs=[("qSC", 16)])
        E("sy", lambda e: e.dma_start(out=out_h[128:129, 0:256], in_=bceF[:]),
          waits=[("sDVE", sf)], incs=[("qSC", 16)])
        for rr in range(8):
            E("sy", lambda e, rr=rr: e.dma_start(
                out=out_h[512 + rr * 128:512 + (rr + 1) * 128, 0:512].bitcast(BF16),
                in_=xsH[rr * 1024:rr * 1024 + 128, :]),
              waits=[("qPL", xs_scattered)], incs=[("qSC", 16)])
        for nm, srcap, dst0 in [("xn0", xnH[0], 2000), ("xs0", xsH[0:128, :], 2128),
                                ("gs0", gsH[0:128, :], 2256), ("xs1", xsH[1024:1152, :], 2384)]:
            E("sy", lambda e, srcap=srcap: e.dma_start(
                out=xsb[:, 0:DM], in_=srcap),
              waits=[("qSC", gs_all), ("qPL", C("qPL")), ("sDVE", C("sDVE")),
                     ("qSY", C("qSY")), ("sPE", C("sPE")), ("sACT", C("sACT")),
                     ("sPOOL", C("sPOOL"))],
              incs=[("qSY", 16)])
            E("dve", lambda e: e.tensor_copy(out=ffnacc[:], in_=xsb[:, 0:DM]),
              waits=[("qSY", C("qSY"))], incs=[("sDVE", 1)])
            E("sy", lambda e, dst0=dst0: e.dma_start(
                out=out_h[dst0:dst0 + 128, :], in_=ffnacc[:]),
              waits=[("sDVE", C("sDVE"))], incs=[("qSC", 16)])
        b.wait("sy", [("qSC", C("qSC"))])
    if DBG_SCRATCH:
        E("sy", lambda e: e.dma_start(out=xsD, in_=xsH),
          waits=[("qPL", xs_scattered)], incs=[("qSY", 16)])
        E("sy", lambda e: e.dma_start(out=gsD, in_=gsH),
          waits=[("qSC", gs_all)], incs=[("qSY", 16)])
        b.cnt["qSY"] += 0

    if PHASES == "pb":
        return finalize()
    b.barrier()

    slot_cast = [C("sPOOL"), C("sPOOL")]
    for et in range(NT):
        par = et % 2
        sl = stage[:, par * 1024:(par + 1) * 1024]
        E("sy", lambda e, et=et, sl=sl: e.dma_start(
            out=sl, in_=woutT_h[et * 128:(et + 1) * 128, :]),
          waits=[("sPOOL", slot_cast[par])], incs=[("qSY", 16)])
        E("pool", lambda e, et=et, sl=sl: e.tensor_copy(
            out=WA[:, et * 1024:(et + 1) * 1024], in_=sl),
          waits=[("qSY", C("qSY"))], incs=[("sPOOL", 1)])
        slot_cast[par] = C("sPOOL")
    wout_done = C("sPOOL")

    out_stored = 0
    mm5_done = [0, 0]
    evacd5 = [0, 0]
    for c in range(NCHUNK):
        par = c % 2
        E("sy", lambda e, c=c, par=par: e.dma_start(
            out=yzin[:, par * DI:(par + 1) * DI], in_=yzH[c]),
          waits=[("sPE", mm5_done[par])], incs=[("qSY", 16)])
        yz_ld = C("qSY")
        E("sy", lambda e, c=c, par=par: e.dma_start(
            out=xt5[:, par * DM:(par + 1) * DM], in_=x_h[c * 128:(c + 1) * 128, :]),
          waits=[("sDVE", evacd5[par])], incs=[("qSY", 16)])
        x_ld = C("qSY")
        E("pool", lambda e, c=c, par=par: e.indirect_dma_start(
            out=g5[:, par * DM:(par + 1) * DM], out_offset=None,
            in_=gsH, in_offset=bass.IndirectOffsetOnAxis(slotI[:, c:c + 1], 0)),
          waits=[("qSC", gs_all), ("sDVE", evacd5[par])], incs=[("qPL", 16)])
        g_ld = C("qPL")
        for et in range(NT):
            for jj in range(2):
                wts = [("qSY", yz_ld), ("sPOOL", wout_done)]
                if et == 0 and jj == 0:
                    wts.append(("sDVE", evacd5[par]))
                E("pe", lambda e, et=et, par=par, jj=jj: e.matmul(
                    out=psB[:, par * 1024 + jj * 512:par * 1024 + (jj + 1) * 512],
                    lhsT=yzin[:, par * DI + et * 128:par * DI + (et + 1) * 128],
                    rhs=WA[:, et * 1024 + jj * 512:et * 1024 + (jj + 1) * 512],
                    start=(et == 0), stop=(et == NT - 1)),
                  waits=wts, incs=([("sPE", 1)] if (et == NT - 1 and jj == 1) else []))
        smm = C("sPE")
        mm5_done[par] = smm
        E("dve", lambda e, par=par: e.tensor_mul(
            out=outb[:, par * DM:(par + 1) * DM],
            in0=psB[:, par * 1024:(par + 1) * 1024],
            in1=g5[:, par * DM:(par + 1) * DM]),
          waits=[("sPE", smm), ("qPL", g_ld), ("qSC", out_stored)],
          incs=[("sDVE", 1)])
        E("dve", lambda e, par=par: e.tensor_add(
            out=outb[:, par * DM:(par + 1) * DM],
            in0=outb[:, par * DM:(par + 1) * DM],
            in1=xt5[:, par * DM:(par + 1) * DM]),
          waits=[("sDVE", C("sDVE")), ("qSY", x_ld)], incs=[("sDVE", 1)])
        evacd5[par] = C("sDVE")
        E("act", lambda e, c=c, par=par: e.dma_start(
            out=out_h[c * 128:(c + 1) * 128, :], in_=outb[:, par * DM:(par + 1) * DM]),
          waits=[("sDVE", evacd5[par])], incs=[("qSC", 16)])
        out_stored = C("qSC")

    b.wait("sy", [("qSC", out_stored)])
    b.wait("act", [("qSC", out_stored)])

    return finalize()


_prog_cache = {}


def prep_inputs(L0_attn_x, L0_attn_W_in, L0_attn_conv_w, L0_attn_conv_b, L0_attn_W_x,
                L0_attn_W_dt, L0_attn_b_dt, L0_attn_A_log, L0_attn_D, L0_attn_W_out,
                L0_ffn_gate_w, L0_ffn_w_gate, L0_ffn_w_up, L0_ffn_w_down):
    x = np.ascontiguousarray(np.asarray(L0_attn_x, np.float32))
    winT = np.ascontiguousarray(np.asarray(L0_attn_W_in, np.float32).T)
    wxdtT = np.ascontiguousarray(
        np.concatenate([np.asarray(L0_attn_W_x, np.float32),
                        np.asarray(L0_attn_W_dt, np.float32)], 0).T)
    woutT = np.ascontiguousarray(np.asarray(L0_attn_W_out, np.float32).T)
    cw = np.asarray(L0_attn_conv_w, np.float32)[:, 0, :]
    convw = np.ascontiguousarray(
        cw.reshape(NT, 128, 4).transpose(1, 0, 2).reshape(128, NT * 4))
    convb = np.ascontiguousarray(
        np.asarray(L0_attn_conv_b, np.float32).reshape(NT, 128).T)
    dfm = np.ascontiguousarray(np.asarray(L0_attn_D, np.float32).reshape(NT, 128).T)
    bdt = np.asarray(L0_attn_b_dt, np.float32)[None, :]
    alog = np.asarray(L0_attn_A_log, np.float32)[None, :]
    gwT = np.ascontiguousarray(np.asarray(L0_ffn_gate_w, np.float32).T)
    wgT = np.ascontiguousarray(np.asarray(L0_ffn_w_gate, np.float32).transpose(0, 2, 1))
    wuT = np.ascontiguousarray(np.asarray(L0_ffn_w_up, np.float32).transpose(0, 2, 1))
    wdT = np.ascontiguousarray(np.asarray(L0_ffn_w_down, np.float32).transpose(0, 2, 1))

    shared = dict(winT=winT, wxdtT=wxdtT, woutT=woutT, convw=convw, convb=convb,
                  dfm=dfm, bdt=bdt, alog=alog, gwT=gwT, wgT=wgT, wuT=wuT, wdT=wdT)
    return shared, x


def kernel(**inputs):
    shared, x = prep_inputs(**inputs)
    nc = build_program()
    in_maps = [dict(shared, x=np.ascontiguousarray(x[bb])) for bb in range(8)]
    res = run_bass_kernel_spmd(nc, in_maps, core_ids=list(range(8)))
    return np.stack([r["out"] for r in res.results], axis=0)
